# version 05
# baseline (speedup 1.0000x reference)
\

import numpy as np
import ml_dtypes
from contextlib import ExitStack

import concourse.bass as bass
import concourse.mybir as mybir
from concourse.bass_utils import run_bass_kernel_spmd

F32 = mybir.dt.float32
BF16 = mybir.dt.bfloat16
AF = mybir.ActivationFunctionType
ALU = mybir.AluOpType
AX = mybir.AxisListType.X
BF16NP = ml_dtypes.bfloat16

NCORES = 8
S_TOT = 16384
D = 2048
NH, HD, NM = 16, 128, 4
TPC = S_TOT // NCORES
NCH = TPC // 128
NFT = D // 128
EPS = 1e-6
LN_EPS = 1e-5
E_MOE, I_MOE, I_SH = 8, 1408, 5632
HALF = TPC // 2
REL_SCALE = 1.0 / float(S_TOT)

ENGINES = ("sync", "tensor", "vector", "scalar", "gpsimd")


class Sem:
    __slots__ = ("h", "v", "name")

    def __init__(self, h, name):
        self.h = h
        self.v = 0
        self.name = name


class Planner:
    def __init__(self, nc, ctx):
        self.nc = nc
        self.ctx = ctx
        self.q = {e: [] for e in ENGINES}
        self.ax = {e: self.sem("ax_" + e) for e in ENGINES}

    def sem(self, name):
        return Sem(self.ctx.enter_context(self.nc.semaphore(name)), name)

    def wait(self, eng, sem, val=None):
        v = sem.v if val is None else val
        if v is None or v <= 0:
            return
        self.q[eng].append(lambda e, h=sem.h, v=v: e.wait_ge(h, v))

    def waitk(self, eng, tok):
        if tok is None:
            return
        self.wait(eng, tok[0], tok[1])

    def op(self, eng, fn, inc=None, by=1):
        if inc is True:
            inc = self.ax[eng]
            ret_tok = True
        else:
            ret_tok = False
        if inc is not None:
            inc.v += by

            def g(e, fn=fn, h=inc.h, by=by):
                fn(e).then_inc(h, by)
        else:
            def g(e, fn=fn):
                fn(e)
        self.q[eng].append(g)
        if inc is None:
            return None
        return (inc, inc.v) if ret_tok else inc.v

    def opw(self, eng, fn):
        tok = self.op(eng, fn, inc=True)
        self.waitk(eng, tok)
        return tok

    def mark(self, eng):
        if eng == "vector":
            return self.op(eng, lambda e: e.engine_nop(), inc=True)
        return self.op(eng, lambda e: e.nop(), inc=True)

    def dma(self, eng, out, in_, sem, **kw):
        return self.op(eng, lambda e, o=out, i=in_, kw=dict(kw): e.dma_start(out=o, in_=i, **kw), inc=sem, by=16)

    def barrier(self, bar, scalar_scratch=None):
        self.op("vector", lambda e: e.engine_nop(), inc=bar)
        self.op("gpsimd", lambda e: e.engine_nop(), inc=bar)
        if scalar_scratch is not None:
            self.op("scalar", lambda e, o=scalar_scratch: e.copy(o, o), inc=bar)
        else:
            self.op("scalar", lambda e, h=bar.h: e.sem_inc(h, 1))
            bar.v += 1
        self.op("tensor", lambda e, h=bar.h: e.sem_inc(h, 1))
        self.op("sync", lambda e, h=bar.h: e.sem_inc(h, 1))
        bar.v += 2
        tgt = bar.v
        for eng in ENGINES:
            self.wait(eng, bar, tgt)


class Bank:

    def __init__(self, P, stack, name, shape, dtype):
        self.t = stack.enter_context(P.nc.psum_tensor(name, shape, dtype))
        self.mm = P.sem(name + "_mm")
        self.ev = {}
        self.name = name
        self.P = P

    def _ev(self, eng):
        if eng not in self.ev:
            self.ev[eng] = self.P.sem(self.name + "_ev_" + eng)
        return self.ev[eng]

    def await_free(self, eng="tensor"):
        for s in self.ev.values():
            self.P.wait(eng, s)

    def await_data(self, eng):
        self.P.wait(eng, self.mm)

    def rel(self, eng):
        return self._ev(eng)


def _mm(P, out, lhsT, rhs, start, stop, inc=None):
    return P.op("tensor", lambda e, o=out, l=lhsT, r=rhs, s=start, t=stop:
                e.matmul(o, l, r, start=s, stop=t), inc=inc)


def _tr(P, out, in_, ident, inc=None):
    return P.op("tensor", lambda e, o=out, i=in_, d=ident: e.transpose(o, i, d), inc=inc)


def build(dbg=False):
    nc = bass.Bass("TRN2", target_bir_lowering=False)

    def din(name, shape, dt=F32):
        return nc.dram_tensor(name, shape, dt, kind="ExternalInput").ap()

    def dout(name, shape, dt=F32):
        return nc.dram_tensor(name, shape, dt, kind="ExternalOutput").ap()

    def dint(name, shape, dt=F32):
        k = "ExternalOutput" if dbg else "Internal"
        return nc.dram_tensor(name, shape, dt, kind=k).ap()

    x_d = din("x", [TPC, D])
    wq_d = din("wq_t", [D, D])
    wk_d = din("wk_t", [D, D])
    wv_d = din("wv_t", [D, D])
    wo_d = din("wo_t", [D, D])
    egw1_d = din("egw1_t", [D, 512])
    egb1_d = din("egb1_c", [128, 4])
    egw2_d = din("egw2_c", [128, 4])
    egb2_d = din("egb2_c", [1, 1])
    memgate_d = din("memgate", [1, NH])
    fmem_d = din("fmem", [NM, NH, HD, HD])
    fnorm_d = din("fnorm_c", [128, NH * 5])
    gsg_d = din("gsg_t", [D, 9])
    wg_d = din("wg", [E_MOE, D, I_MOE])
    wu_d = din("wu", [E_MOE, D, I_MOE])
    wd_d = din("wd", [E_MOE, I_MOE, D])
    swg_d = din("swg", [D, I_SH])
    swu_d = din("swu", [D, I_SH])
    swd_d = din("swd", [I_SH, D])
    idb_d = din("ident_bf", [128, 128], BF16)
    idf_d = din("ident_f32", [128, 128])
    mask_d = din("maskT", [128, 128])
    onesrow_d = din("ones_row", [1, 128])
    onescol_d = din("ones_col", [128, 1])
    sel_d = din("sel9", [9, 9 * 128])
    pmask_d = din("pmask", [128, NCORES])

    out_d = dout("out", [TPC, D])

    sq_sp = dint("sq_sp", [NCH, 128, NH * 128], BF16)
    sk_sp = dint("sk_sp", [NCH, 128, NH * 128], BF16)
    v_sp = dint("v_sp", [NCH, 128, NH * 129], BF16)
    oT_sp = dint("oT_sp", [NCH, 128, NH * 128], BF16)
    x1f_sp = dint("x1f_sp", [NCH, 128, D])
    attn_dbg = dout("attn_dbg", [NCH, 128, D]) if dbg else None
    if dbg:
        dbg_n = dout("dbg_n", [128, NH * 5])
        dbg_cm = dout("dbg_cm", [128, NH * 4])
        dbg_crf = dout("dbg_crf", [128, D])
        dbg_af = dout("dbg_af", [128, NH * 640], BF16)
        dbg_S = dout("dbg_S", [128, NH * 129])
        dbg_misc = dout("dbg_misc", [128, 128])
        dbg_sqrl = dout("dbg_sqrl", [128, NH * 128], BF16)
        dbg_cmb = dout("dbg_cmb", [16, 128, 9])
        dbg_lgt = dout("dbg_lgt", [16, 128, 9])
        dbg_x1T = dout("dbg_x1T", [2, 128, NFT * HALF], BF16)
        dbg_x1T2 = dout("dbg_x1T2", [2, 128, NFT * HALF], BF16)
        dbg_lgt2 = dout("dbg_lgt2", [16, 128, 9])
        dbg_lgt3 = dout("dbg_lgt3", [2, 128, 9])
        dbg_acc = dout("dbg_acc", [128, D])
        dbg_fm = dout("dbg_fm", [128, NH * 640], BF16)

    cc_in = nc.dram_tensor("cc_in", [128, NH * 130], F32).ap()
    cc_out = nc.dram_tensor("cc_out", [NCORES * 128, NH * 130], F32).ap()

    root = ExitStack()
    with root as ctx:
        P = Planner(nc, ctx)
        sb = lambda n, s, d=F32: ctx.enter_context(nc.sbuf_tensor(n, s, d))

        idb = sb("idb", [128, 128], BF16)
        idf = sb("idf", [128, 128])
        mask_sb = sb("mask_sb", [128, 128])
        onesrow = sb("onesrow", [1, 128])
        onescol = sb("onescol", [128, 1])
        pmask_sb = sb("pmask_sb", [128, NCORES])
        memgate_sb = sb("memgate_sb", [1, NH])
        egb1_sb = sb("egb1_sb", [128, 4])
        egb2_sb = sb("egb2_sb", [128, 1])
        fnorm_sb = sb("fnorm_sb", [128, NH, 5])
        nrhs_bf = sb("nrhs_bf", [128, NH, 5], BF16)
        S_f32 = sb("S_f32", [128, NH, 129])
        S_bf = sb("S_bf", [128, NH, 129], BF16)
        sq_own = sb("sq_own", [128, NH])
        sqsumg = sb("sqsumg", [128, NH])
        sqsum4 = sb("sqsum4", [128, NH, 4])
        prod_sb = sb("prod_sb", [128, NH, 4])
        gate_cols = sb("gate_cols", [128, NH])
        omgate_cols = sb("omgate_cols", [128, NH])
        coeffbase = sb("coeffbase", [128, NH, 5])
        zalive_col = sb("zalive_col", [128, 1])
        rowscr = sb("rowscr", [1, 512])
        g_col = sb("g_col", [128, 1])
        omg_col = sb("omg_col", [128, 1])
        c_loc = sb("c_loc", [128, 1])
        den_t = sb("den_tmp", [128, 1])
        crcoefs = sb("crcoefs", [128, NH])
        c2cols = sb("c2cols", [128, NH])
        n_all = sb("n_all", [128, NH, 5])
        r_all = sb("r_all", [128, NH, 5])
        coeffm = sb("coeffm", [128, NH, 4])
        ln_m = sb("ln_m", [128, 1])
        ln_v = sb("ln_v", [128, 1])
        ln_r = sb("ln_r", [128, 1])

        ldS = P.sem("ldS")
        splS = P.sem("splS")
        rldS = P.sem("rldS")
        stS = P.sem("stS")
        ldG = P.sem("ldG")
        splG = P.sem("splG")
        ccs = P.sem("ccs")
        bar = P.sem("bar")

        for dst, src in [(idb[:, :], idb_d), (idf[:, :], idf_d), (mask_sb[:, :], mask_d),
                         (onesrow[:, :], onesrow_d), (onescol[:, :], onescol_d),
                         (pmask_sb[:, :], pmask_d),
                         (memgate_sb[:, :], memgate_d), (egb1_sb[:, :], egb1_d),
                         (fnorm_sb[:, :, :], fnorm_d), (rowscr[0:1, 0:1], egb2_d)]:
            P.dma("sync", dst, src, ldS)
        ld0_done = ldS.v

        with ExitStack() as ph1:
            psb = lambda n, s, d=F32: ph1.enter_context(nc.sbuf_tensor(n, s, d))
            xT = psb("xT", [128, NFT, TPC], BF16)
            A = psb("A", [128, NFT, TPC], BF16)
            wsb = psb("wsb", [128, NFT, 1024], BF16)
            xstage = [psb(f"xst{i}", [128, D], BF16) for i in range(2)]
            v_stage = [psb(f"vst{i}", [128, 8, 129], BF16) for i in range(2)]
            skc_st = psb("skc_st", [128, 8, 128], BF16)
            elu_t1 = [psb(f"elu1_{i}", [128, 512]) for i in range(4)]

            B = [Bank(P, ph1, f"ps{i}", [128, 512], F32) for i in range(7)]
            Btr = Bank(P, ph1, "pstr", [128, 1024], BF16)

            x_tgt = [None] * NCH
            x_tgt[0] = P.dma("gpsimd", xstage[0][:, :], x_d[0:128, :], ldG)
            x_tgt[1] = P.dma("gpsimd", xstage[1][:, :], x_d[128:256, :], ldG)
            for t in range(NCH):
                P.wait("tensor", ldG, x_tgt[t])
                for r in range(2):
                    Btr.await_free()
                    for k in range(8):
                        f = r * 8 + k
                        _tr(P, Btr.t[:, k * 128:(k + 1) * 128], xstage[t % 2][:, f * 128:(f + 1) * 128],
                            idb[:, :], inc=(Btr.mm if k == 7 else None))
                    Btr.await_data("scalar")
                    P.op("scalar", lambda e, d=xT[:, r * 8:(r + 1) * 8, t * 128:(t + 1) * 128],
                         s=Btr.t[:, :].rearrange("p (k j) -> p k j", j=128): e.copy(d, s),
                         inc=Btr.rel("scalar"))
                tok = (Btr.mm, Btr.mm.v)
                if t + 2 < NCH:
                    P.waitk("gpsimd", tok)
                    x_tgt[t + 2] = P.dma("gpsimd", xstage[t % 2][:, :],
                                         x_d[(t + 2) * 128:(t + 3) * 128, :], ldG)

            def load_w_half(dram, hw):
                tg = None
                for f in range(NFT):
                    tg = P.dma("gpsimd", wsb[:, f, :],
                               dram[f * 128:(f + 1) * 128, hw * 1024:(hw + 1) * 1024], ldG)
                return tg

            def proj_featmajor(dst, dram, prev_rel):
                P.wait("tensor", Btr.rel("scalar"))
                for hw in range(2):
                    if hw == 0 and prev_rel is not None:
                        P.waitk("gpsimd", prev_rel)
                    w_tgt = load_w_half(dram, hw)
                    for il in range(8):
                        i = hw * 8 + il
                        bs = B[0:4] if (il % 2 == 0) else [B[4], B[5], B[6], B[0]]
                        for b in bs:
                            b.await_free()
                        if il == 0:
                            P.wait("tensor", ldG, w_tgt)
                        for f in range(NFT):
                            for tc in range(4):
                                _mm(P, bs[tc].t[:, :], wsb[:, f, il * 128:(il + 1) * 128],
                                    xT[:, f, tc * 512:(tc + 1) * 512], start=(f == 0),
                                    stop=(f == NFT - 1), inc=(bs[tc].mm if (f == NFT - 1) else None))
                        t1 = None
                        for tc in range(4):
                            b = bs[tc]
                            b.await_data("vector")
                            t1 = P.op("vector", lambda e, o=elu_t1[tc][:, :], s=b.t[:, :]:
                                      e.tensor_single_scalar(o, s, 0.0, ALU.min), inc=True)
                        P.waitk("scalar", t1)
                        t2 = None
                        for tc in range(4):
                            t2 = P.op("scalar", lambda e, o=elu_t1[tc][:, :], s=elu_t1[tc][:, :]:
                                      e.activation(o, s, AF.Exp), inc=True)
                        P.waitk("vector", t2)
                        for tc in range(4):
                            b = bs[tc]
                            dslice = dst[:, i, tc * 512:(tc + 1) * 512]
                            P.op("vector", lambda e, o=dslice, s=b.t[:, :], x2=elu_t1[tc][:, :]:
                                 e.scalar_tensor_tensor(o, s, 0.0, x2, ALU.max, ALU.add),
                                 inc=b.rel("vector"))
                    if hw == 0:
                        lb = B[0]
                        half_rel = (lb.mm, lb.mm.v)
                        P.waitk("gpsimd", half_rel)
                lb = B[0]
                return (lb.mm, lb.mm.v)

            xT_done = P.ax["scalar"].v
            wk_rel = proj_featmajor(A, wk_d, None)
            sk_wr = P.mark("vector")
            P.waitk("tensor", sk_wr)

            P.waitk("sync", sk_wr)
            sk_sp_tgt = [None] * NCH
            for c in range(NCH):
                sk_sp_tgt[c] = P.dma("sync", sk_sp[c], A[:, :, c * 128:(c + 1) * 128], splS)

            for i in range(2):
                P.op("vector", lambda e, o=v_stage[i][:, :, 128:129]: e.memset(o, 1.0))
            vinit = P.op("vector", lambda e, o=S_f32[:, :, :]: e.memset(o, 0.0), inc=True)
            P.waitk("tensor", vinit)
            P.waitk("sync", vinit)

            prev_rel = wk_rel
            last_vspill = {0: None, 1: None}
            for hw in range(2):
                P.waitk("gpsimd", prev_rel)
                wv_tgt = load_w_half(wv_d, hw)
                for t in range(NCH):
                    par = t % 2
                    bs = B[0:2] if (t % 2 == 0) else B[2:4]
                    for b in bs:
                        b.await_free()
                    if t == 0:
                        P.wait("tensor", ldG, wv_tgt)
                    for f in range(NFT):
                        for oc in range(2):
                            _mm(P, bs[oc].t[:, :], xT[:, f, t * 128:(t + 1) * 128],
                                wsb[:, f, oc * 512:(oc + 1) * 512], start=(f == 0),
                                stop=(f == NFT - 1), inc=(bs[oc].mm if f == NFT - 1 else None))
                    if last_vspill[par] is not None:
                        P.wait("vector", splS, last_vspill[par])
                    for oc in range(2):
                        b = bs[oc]
                        b.await_data("vector")
                        P.op("vector", lambda e, o=v_stage[par][:, 4 * oc:4 * oc + 4, 0:128],
                             s=b.t[:, :].rearrange("p (h e) -> p h e", h=4): e.tensor_copy(o, s),
                             inc=b.rel("vector"))
                    vdata = P.mark("vector")
                    Btr.await_free()
                    for k in range(8):
                        h = hw * 8 + k
                        _tr(P, Btr.t[:, k * 128:(k + 1) * 128], A[:, h, t * 128:(t + 1) * 128],
                            idb[:, :], inc=(Btr.mm if k == 7 else None))
                    Btr.await_data("scalar")
                    P.op("scalar", lambda e, d=skc_st[:, :, :],
                         s=Btr.t[:, :].rearrange("p (k j) -> p k j", j=128): e.copy(d, s),
                         inc=Btr.rel("scalar"))
                    sk_tr = Btr.rel("scalar").v
                    for k in range(8):
                        h = hw * 8 + k
                        bd = B[4 + (k % 2)]
                        bd.await_free()
                        if k == 0:
                            P.wait("tensor", Btr.rel("scalar"), sk_tr)
                            P.waitk("tensor", vdata)
                        _mm(P, bd.t[:, 0:129], skc_st[:, k, :], v_stage[par][:, k, :],
                            start=True, stop=True, inc=bd.mm)
                        bd.await_data("vector")
                        P.op("vector", lambda e, o=S_f32[:, h, :], s=bd.t[:, 0:129]:
                             e.scalar_tensor_tensor(o, s, 1.0, o, ALU.mult, ALU.add),
                             inc=bd.rel("vector"))
                    P.waitk("sync", vdata)
                    last_vspill[par] = P.dma("sync", v_sp[t][:, hw * 8 * 129:(hw * 8 + 8) * 129],
                                             v_stage[par][:, :, :], splS)
                prev_rel = (B[5].mm, B[5].mm.v)

            P.wait("vector", splS, sk_sp_tgt[NCH - 1])
            ph1_mm_tok = proj_featmajor(A, wq_d, prev_rel)
            sq_sp_tgt = [None] * NCH
            sq_wr = P.mark("vector")
            P.waitk("sync", sq_wr)
            for c in range(NCH):
                sq_sp_tgt[c] = P.dma("sync", sq_sp[c], A[:, :, c * 128:(c + 1) * 128], splS)
            P.waitk("vector", sq_wr)
            sqs_tok = None
            for h in range(NH):
                sqs_tok = P.op("vector", lambda e, o=sq_own[:, h:h + 1], s=A[:, h, :]:
                               e.reduce_sum(o, s, axis=AX), inc=True)

            P.waitk("gpsimd", sqs_tok)
            cc1 = P.dma("gpsimd", cc_in[:, 0:NH * 129], S_f32[:, :, :], ldG)
            cc1 = P.dma("gpsimd", cc_in[:, NH * 129:NH * 130], sq_own[:, :], ldG)
            P.wait("gpsimd", ldG, cc1)
            P.op("gpsimd", lambda e: e.collective_compute(
                "AllGather", ALU.bypass, replica_groups=[list(range(NCORES))],
                ins=[cc_in.opt()], outs=[cc_out.opt()]), inc=ccs, by=1)
            cc_done = ccs.v

        P.barrier(bar, scalar_scratch=rowscr[0:1, 255:256])
        P.wait("gpsimd", splS, splS.v)
        P.wait("sync", splS, splS.v)

        with ExitStack() as ph2:
            psb = lambda n, s, d=F32: ph2.enter_context(nc.sbuf_tensor(n, s, d))
            fmem_sb = psb("fmem_sb", [128, NH, 640], BF16)
            msum_f = psb("msum_f", [128, NH, 128])
            egw1_sb = psb("egw1_sb", [128, NFT, 512], BF16)
            egw2_sb = psb("egw2_sb", [128, 4], BF16)
            af_sb = psb("af_sb", [128, NH, 640], BF16)
            shard_st = [psb(f"shard{i}", [128, NH * 130]) for i in range(2)]
            sq_rl = [psb(f"sq_rl{i}", [128, NH, 128], BF16) for i in range(2)]
            sk_rl = [psb(f"sk_rl{i}", [128, NH, 128], BF16) for i in range(2)]
            v_rl = [psb(f"v_rl{i}", [128, NH, 129], BF16) for i in range(2)]
            skc2 = psb("skc2", [128, NH, 128], BF16)
            msT = psb("msT", [128, NH, 128], BF16)
            den16 = psb("den16", [128, NH])
            c_loc16 = psb("c_loc16", [128, NH])
            num_sb = psb("num_sb", [128, NH, 128])
            crf = psb("crf", [128, D])
            crb = psb("crb", [128, D], BF16)
            crT = psb("crT", [128, NFT, 128], BF16)
            h1_bf = psb("h1_bf", [128, 4, 128], BF16)
            oa = psb("oa", [128, D])
            oa_bf = psb("oa_bf", [128, D], BF16)
            oT = psb("oT", [128, NFT, 128], BF16)

            B2 = [Bank(P, ph2, f"p2_{i}", [128, 512], F32) for i in range(5)]
            Btr2 = Bank(P, ph2, "p2tr", [128, 1024], BF16)
            Baf, Bb, Bsc, Bnum, Bdel = B2
            P.waitk("tensor", ph1_mm_tok)

            egw1_tgt = None
            for f in range(NFT):
                egw1_tgt = P.dma("gpsimd", egw1_sb[:, f, :], egw1_d[f * 128:(f + 1) * 128, :], ldG)
            egw2_tgt = P.dma("gpsimd", egw2_sb[:, :], egw2_d, ldG)
            fm_tgt = None
            for m in range(NM):
                fm_tgt = P.dma("gpsimd", fmem_sb[:, :, m * 128:(m + 1) * 128],
                               fmem_d[m].rearrange("h d e -> d h e"), ldG)
            ms_tgt = None
            for m in range(NM):
                if m > 0:
                    P.wait("gpsimd", ldG, ms_tgt)
                kw = {} if m == 0 else {"accum_op": ALU.add}
                ms_tgt = P.dma("gpsimd", msum_f[:, :, :], fmem_d[m].rearrange("h d e -> d h e"), ldG, **kw)
            P.wait("vector", ldG, ms_tgt)
            msum_tok = P.op("vector", lambda e, o=fmem_sb[:, :, 512:640], s=msum_f[:, :, :]:
                            e.tensor_copy(o, s), inc=True)

            P.wait("vector", ldS, ld0_done)
            P.opw("vector", lambda e, o=fnorm_sb[:, :, 4], a=fnorm_sb[:, :, 0], b=fnorm_sb[:, :, 1]:
                  e.tensor_tensor(o, a, b, ALU.add))
            P.opw("vector", lambda e, o=r_all[:, :, 0], a=fnorm_sb[:, :, 2], b=fnorm_sb[:, :, 3]:
                  e.tensor_tensor(o, a, b, ALU.add))
            P.opw("vector", lambda e, o=fnorm_sb[:, :, 4], a=fnorm_sb[:, :, 4], b=r_all[:, :, 0]:
                  e.tensor_tensor(o, a, b, ALU.add))
            nrhs_tok = P.op("vector", lambda e, o=nrhs_bf[:, :, :], s=fnorm_sb[:, :, :]:
                            e.tensor_copy(o, s), inc=True)
            dsum_tok = None
            for m in range(NM):
                dsum_tok = P.op("vector", lambda e, o=sqsum4[:, 0, m:m + 1], s=fnorm_sb[:, :, m]:
                                e.reduce_sum(o, s, axis=AX), inc=True)
            P.waitk("vector", dsum_tok)
            Bsc.await_free()
            P.waitk("tensor", dsum_tok)
            _mm(P, Bsc.t[0:1, 0:4], onescol[:, :], sqsum4[:, 0, :], True, True, inc=Bsc.mm)
            Bsc.await_data("vector")
            P.opw("vector", lambda e, o=rowscr[0:1, 16:20], s=Bsc.t[0:1, 0:4]: e.tensor_copy(o, s))
            P.op("vector", lambda e: e.engine_nop(), inc=Bsc.rel("vector"))
            P.opw("vector", lambda e, o=rowscr[0:1, 20:24], s=rowscr[0:1, 16:20]:
                  e.tensor_single_scalar(o, s, EPS, ALU.is_ge))
            P.opw("vector", lambda e, o=rowscr[0:1, 24:25], s=rowscr[0:1, 16:20]:
                  e.reduce_sum(o, s, axis=AX))
            za_tok = P.opw("vector", lambda e, o=rowscr[0:1, 25:26], s=rowscr[0:1, 24:25]:
                           e.tensor_single_scalar(o, s, EPS, ALU.is_ge))
            P.wait("scalar", ldS, ld0_done)
            gr_tok = P.op("scalar", lambda e, o=rowscr[0:1, 32:48], s=memgate_sb[0:1, :]:
                          e.activation(o, s, AF.Sigmoid), inc=True)
            Bnum.await_free()
            P.waitk("tensor", gr_tok)
            P.waitk("tensor", za_tok)
            P.wait("tensor", ldS, ld0_done)
            _mm(P, Bnum.t[:, 0:16], onesrow[0:1, :], rowscr[0:1, 32:48], True, True)
            _mm(P, Bnum.t[:, 16:17], onesrow[0:1, :], rowscr[0:1, 25:26], True, True)
            _mm(P, Bnum.t[:, 17:18], onesrow[0:1, :], rowscr[0:1, 0:1], True, True, inc=Bnum.mm)
            Bnum.await_data("vector")
            P.op("vector", lambda e, o=gate_cols[:, :], s=Bnum.t[:, 0:16]: e.tensor_copy(o, s))
            P.op("vector", lambda e, o=zalive_col[:, :], s=Bnum.t[:, 16:17]: e.tensor_copy(o, s))
            P.opw("vector", lambda e, o=egb2_sb[:, :], s=Bnum.t[:, 17:18]: e.tensor_copy(o, s))
            P.op("vector", lambda e: e.engine_nop(), inc=Bnum.rel("vector"))
            P.op("vector", lambda e, o=omgate_cols[:, :], s=gate_cols[:, :]:
                 e.tensor_scalar(o, s, -1.0, 1.0, ALU.mult, ALU.add))

            P.wait("gpsimd", ccs, cc_done)
            sh_tgt = [None] * NCORES
            sh_rel = [None] * NCORES
            sh_tgt[0] = P.dma("gpsimd", shard_st[0][:, :], cc_out[0:128, :], ldG)
            sh_tgt[1] = P.dma("gpsimd", shard_st[1][:, :], cc_out[128:256, :], ldG)
            for r in range(NCORES):
                P.wait("vector", ldG, sh_tgt[r])
                ss = shard_st[r % 2]
                P.op("vector", lambda e, o=S_f32[:, :, :],
                     s=ss[:, 0:NH * 129].rearrange("p (h c) -> p h c", h=NH),
                     m=pmask_sb[:, r:r + 1], op1=(ALU.bypass if r == 0 else ALU.add):
                     e.scalar_tensor_tensor(o, s, m, o, ALU.mult, op1))
                if r == 0:
                    sh_rel[r] = P.opw("vector", lambda e, o=sqsumg[:, :], s=ss[:, NH * 129:NH * 130]:
                                      e.tensor_copy(o, s))
                else:
                    sh_rel[r] = P.opw("vector", lambda e, o=sqsumg[:, :], s=ss[:, NH * 129:NH * 130],
                                      b=sqsumg[:, :]: e.tensor_tensor(o, b, s, ALU.add))
                if r + 2 < NCORES:
                    P.waitk("gpsimd", sh_rel[r])
                    sh_tgt[r + 2] = P.dma("gpsimd", shard_st[r % 2][:, :],
                                          cc_out[(r + 2) * 128:(r + 3) * 128, :], ldG)
            sbf_init = P.op("vector", lambda e, o=S_bf[:, :, :], s=S_f32[:, :, :]: e.tensor_copy(o, s),
                            inc=True)

            P.waitk("vector", sh_rel[NCORES - 1])
            for m in range(NM):
                P.op("vector", lambda e, o=sqsum4[:, :, m], s=sqsumg[:, :]: e.tensor_copy(o, s))
            P.opw("vector", lambda e: e.engine_nop())
            P.opw("vector", lambda e, o=prod_sb[:, :, :], a=fnorm_sb[:, :, 0:4], b=sqsum4[:, :, :]:
                  e.tensor_tensor(o, a, b, ALU.mult))
            prod_tok = P.opw("vector", lambda e, o=prod_sb[:, :, :], s=prod_sb[:, :, :]:
                             e.tensor_single_scalar(o, s, REL_SCALE, ALU.mult))
            Bsc.await_free()
            P.waitk("tensor", prod_tok)
            _mm(P, Bsc.t[0:1, 0:64], onescol[:, :], prod_sb[:, :, :], True, True, inc=Bsc.mm)
            Bsc.await_data("vector")
            P.opw("vector", lambda e, o=rowscr[0:1, 64:128], s=Bsc.t[0:1, 0:64]: e.tensor_copy(o, s))
            P.op("vector", lambda e: e.engine_nop(), inc=Bsc.rel("vector"))
            rel = rowscr[0:1, 64:128].rearrange("p (h m) -> p h m", m=4)
            mxe = rowscr[0:1, 160:224].rearrange("p (h m) -> p h m", m=4)
            P.opw("vector", lambda e, o=rowscr[0:1, 128:144], a=rel[:, :, 0], b=rel[:, :, 1]:
                  e.tensor_tensor(o, a, b, ALU.max))
            P.opw("vector", lambda e, o=rowscr[0:1, 144:160], a=rel[:, :, 2], b=rel[:, :, 3]:
                  e.tensor_tensor(o, a, b, ALU.max))
            P.opw("vector", lambda e, o=rowscr[0:1, 128:144], a=rowscr[0:1, 128:144], b=rowscr[0:1, 144:160]:
                  e.tensor_tensor(o, a, b, ALU.max))
            for m in range(NM):
                P.op("vector", lambda e, o=mxe[:, :, m], s=rowscr[0:1, 128:144]: e.tensor_copy(o, s))
            P.opw("vector", lambda e: e.engine_nop())
            sm_tok = P.opw("vector", lambda e, o=rowscr[0:1, 64:128], a=rowscr[0:1, 64:128],
                           b=rowscr[0:1, 160:224]: e.tensor_tensor(o, a, b, ALU.subtract))
            P.waitk("scalar", sm_tok)
            ex_tok = P.op("scalar", lambda e, o=rowscr[0:1, 64:128], s=rowscr[0:1, 64:128]:
                          e.activation(o, s, AF.Exp), inc=True)
            P.waitk("vector", ex_tok)
            P.opw("vector", lambda e, o=rowscr[0:1, 128:144], a=rel[:, :, 0], b=rel[:, :, 1]:
                  e.tensor_tensor(o, a, b, ALU.add))
            P.opw("vector", lambda e, o=rowscr[0:1, 144:160], a=rel[:, :, 2], b=rel[:, :, 3]:
                  e.tensor_tensor(o, a, b, ALU.add))
            P.opw("vector", lambda e, o=rowscr[0:1, 128:144], a=rowscr[0:1, 128:144], b=rowscr[0:1, 144:160]:
                  e.tensor_tensor(o, a, b, ALU.add))
            P.opw("vector", lambda e, o=rowscr[0:1, 144:160], s=rowscr[0:1, 128:144]: e.reciprocal(o, s))
            for m in range(NM):
                P.op("vector", lambda e, o=mxe[:, :, m], s=rowscr[0:1, 144:160]: e.tensor_copy(o, s))
            P.opw("vector", lambda e: e.engine_nop())
            P.opw("vector", lambda e, o=rowscr[0:1, 64:128], a=rowscr[0:1, 64:128], b=rowscr[0:1, 160:224]:
                  e.tensor_tensor(o, a, b, ALU.mult))
            for m in range(NM):
                P.opw("vector", lambda e, o=rel[:, :, m], s=rel[:, :, m], a=rowscr[0:1, 20 + m:21 + m]:
                      e.tensor_scalar(o, s, a, None, ALU.mult))
            wrow_tok = P.mark("vector")
            Bsc.await_free()
            P.waitk("tensor", wrow_tok)
            _mm(P, Bsc.t[:, 0:64], onesrow[0:1, :], rowscr[0:1, 64:128], True, True, inc=Bsc.mm)
            Bsc.await_data("vector")
            P.opw("vector", lambda e, o=coeffm[:, :, :], s=Bsc.t[:, 0:64].rearrange("p (h m) -> p h m", m=4):
                  e.tensor_copy(o, s))
            P.op("vector", lambda e: e.engine_nop(), inc=Bsc.rel("vector"))
            for h in range(NH):
                P.op("vector", lambda e, o=coeffbase[:, h, 0:4], s=coeffm[:, h, :], a=gate_cols[:, h:h + 1]:
                     e.tensor_scalar(o, s, a, None, ALU.mult))
            cbase_tok = P.opw("vector", lambda e, o=coeffbase[:, :, 4], s=gate_cols[:, :], a=zalive_col[:, :]:
                              e.tensor_scalar(o, s, a, None, ALU.mult))

            sqr_tgt = [None] * NCH
            skr_tgt = [None] * NCH
            vr_tgt = [None] * NCH
            rl_rel = [None] * NCH
            pool_rel = [None] * NCH
            dbg_oa_tgt = [None] * NCH
            dbg_gate = [None]
            oab_rel = [None] * NCH
            oT_sp_tgt = [None] * NCH
            sbf_tok = [[None] * NH for _ in range(NCH)]

            def plan_reload(c):
                p = c % 2
                if c >= 2:
                    P.waitk("sync", rl_rel[c - 2])
                sqr_tgt[c] = P.dma("sync", sq_rl[p][:, :, :], sq_sp[c], rldS)
                skr_tgt[c] = P.dma("sync", sk_rl[p][:, :, :], sk_sp[c], rldS)
                vr_tgt[c] = P.dma("sync", v_rl[p][:, :, :], v_sp[c], rldS)

            plan_reload(0)
            plan_reload(1)

            for c in range(NCH):
                p = c % 2
                P.wait("tensor", rldS, sqr_tgt[c])
                if c == 0:
                    P.wait("tensor", ldG, fm_tgt)
                    P.waitk("tensor", msum_tok)
                    P.waitk("tensor", nrhs_tok)
                if c > 0:
                    P.waitk("scalar", pool_rel[c - 1])
                if dbg and c == 1 and dbg_gate[0]:
                    P.wait("scalar", splS, dbg_gate[0])
                    P.wait("vector", splS, dbg_gate[0])
                    P.wait("sync", splS, dbg_gate[0])
                for h in range(NH):
                    Baf.await_free()
                    Bb.await_free()
                    lh = sq_rl[p][:, h, :]
                    _mm(P, Baf.t[:, :], lh, fmem_sb[:, h, 0:512], True, True, inc=Baf.mm)
                    _mm(P, Bb.t[:, 0:128], lh, fmem_sb[:, h, 512:640], True, True)
                    _mm(P, Bb.t[:, 128:133], lh, nrhs_bf[:, h, :], True, True, inc=Bb.mm)
                    Baf.await_data("scalar")
                    P.op("scalar", lambda e, o=af_sb[:, h, 0:512], s=Baf.t[:, :]: e.copy(o, s),
                         inc=Baf.rel("scalar"))
                    Bb.await_data("scalar")
                    P.op("scalar", lambda e, o=af_sb[:, h, 512:640], s=Bb.t[:, 0:128]: e.copy(o, s),
                         inc=Bb.rel("scalar"))
                    Bb.await_data("vector")
                    tok_n = P.op("vector", lambda e, o=n_all[:, h, :], s=Bb.t[:, 128:133]:
                                 e.tensor_copy(o, s), inc=Bb.rel("vector"))
                    if h == NH - 1:
                        P.wait("vector", Bb.rel("vector"), tok_n)
                af_sc_tok = (Bb.rel("scalar"), Bb.rel("scalar").v)
                P.opw("vector", lambda e, o=n_all[:, :, :], s=n_all[:, :, :]:
                      e.tensor_single_scalar(o, s, EPS, ALU.max))
                P.opw("vector", lambda e, o=r_all[:, :, :], s=n_all[:, :, :]: e.reciprocal(o, s))
                if c == 0:
                    P.waitk("vector", cbase_tok)
                crc_tok = P.opw("vector", lambda e, o=crcoefs[:, :], s=r_all[:, :, 4], a=zalive_col[:, :]:
                                e.tensor_scalar(o, s, a, None, ALU.mult))
                P.waitk("vector", af_sc_tok)
                cr_tok = None
                for h in range(NH):
                    cr_tok = P.op("vector", lambda e, o=crf[:, h * 128:(h + 1) * 128],
                                  s=af_sb[:, h, 512:640], a=crcoefs[:, h:h + 1]:
                                  e.scalar_tensor_tensor(o, s, a, o, ALU.mult, ALU.bypass), inc=True)
                P.waitk("scalar", cr_tok)
                crb_tok = P.op("scalar", lambda e, o=crb[:, :], s=crf[:, :]: e.copy(o, s), inc=True)
                P.wait("tensor", rldS, skr_tgt[c])
                P.wait("tensor", rldS, vr_tgt[c])
                for r in range(2):
                    Btr2.await_free()
                    for k in range(8):
                        h = r * 8 + k
                        _tr(P, Btr2.t[:, k * 128:(k + 1) * 128], sk_rl[p][:, h, :], idb[:, :],
                            inc=(Btr2.mm if k == 7 else None))
                    Btr2.await_data("scalar")
                    P.op("scalar", lambda e, d=skc2[:, r * 8:(r + 1) * 8, :],
                         s=Btr2.t[:, :].rearrange("p (k j) -> p k j", j=128): e.copy(d, s),
                         inc=Btr2.rel("scalar"))
                skc2_tgt = Btr2.rel("scalar").v
                P.wait("tensor", Btr2.rel("scalar"), skc2_tgt)
                for h in range(NH):
                    bs_ = Bsc if h % 2 == 0 else Baf
                    bs_.await_free()
                    _mm(P, bs_.t[:, 0:128], sk_rl[p][:, h, :], sq_rl[p][:, h, :], True, True, inc=bs_.mm)
                    bs_.await_data("vector")
                    P.op("vector", lambda e, o=msT[:, h, :], a=bs_.t[:, 0:128], b=mask_sb[:, :]:
                         e.tensor_tensor(o, a, b, ALU.mult), inc=bs_.rel("vector"))
                ms_all = P.mark("vector")
                P.waitk("tensor", ms_all)
                for h in range(NH):
                    bn_ = Bnum if h % 2 == 0 else Bb
                    bd_ = Bdel if h % 2 == 0 else Bsc
                    bn_.await_free()
                    if c > 0:
                        P.waitk("tensor", sbf_tok[c - 1][h])
                    elif h == 0:
                        P.waitk("tensor", sbf_init)
                    _mm(P, bn_.t[:, 0:129], msT[:, h, :], v_rl[p][:, h, :], True, False)
                    _mm(P, bn_.t[:, 0:129], sq_rl[p][:, h, :], S_bf[:, h, :], False, True, inc=bn_.mm)
                    bn_.await_data("scalar")
                    P.op("scalar", lambda e, o=den16[:, h:h + 1], s=bn_.t[:, 128:129]:
                         e.copy(o, s))
                    num_tok = P.op("scalar", lambda e, o=num_sb[:, h, :], s=bn_.t[:, 0:128]:
                                   e.copy(o, s), inc=bn_.rel("scalar"))
                    bd_.await_free()
                    _mm(P, bd_.t[:, 0:129], skc2[:, h, :], v_rl[p][:, h, :], True, True, inc=bd_.mm)
                    bd_.await_data("vector")
                    su_tok = P.op("vector", lambda e, o=S_f32[:, h, :], s=bd_.t[:, 0:129]:
                                  e.scalar_tensor_tensor(o, s, 1.0, o, ALU.mult, ALU.add),
                                  inc=bd_.rel("vector"))
                    P.wait("scalar", bd_.rel("vector"), su_tok)
                    sbf_tok[c][h] = P.op("scalar", lambda e, o=S_bf[:, h, :], s=S_f32[:, h, :]:
                                         e.copy(o, s), inc=True)
                P.wait("vector", bn_.rel("scalar"), num_tok)
                P.opw("vector", lambda e: e.engine_nop())
                P.opw("vector", lambda e, o=den16[:, :], s=den16[:, :]:
                      e.tensor_single_scalar(o, s, EPS, ALU.max))
                P.opw("vector", lambda e, o=c_loc16[:, :], s=den16[:, :]: e.reciprocal(o, s))
                P.opw("vector", lambda e, o=c_loc16[:, :], a=c_loc16[:, :], b=omgate_cols[:, :]:
                      e.tensor_tensor(o, a, b, ALU.mult))
                P.waitk("tensor", crb_tok)
                for r in range(2):
                    Btr2.await_free()
                    for k in range(8):
                        f = r * 8 + k
                        _tr(P, Btr2.t[:, k * 128:(k + 1) * 128], crb[:, f * 128:(f + 1) * 128], idb[:, :],
                            inc=(Btr2.mm if k == 7 else None))
                    Btr2.await_data("scalar")
                    P.op("scalar", lambda e, d=crT[:, r * 8:(r + 1) * 8, :],
                         s=Btr2.t[:, :].rearrange("p (k j) -> p k j", j=128): e.copy(d, s),
                         inc=Btr2.rel("scalar"))
                crT_tgt = Btr2.rel("scalar").v
                Bsc.await_free()
                P.wait("tensor", Btr2.rel("scalar"), crT_tgt)
                if c == 0:
                    P.wait("tensor", ldG, egw1_tgt)
                    P.wait("tensor", ldG, egw2_tgt)
                for ut in range(4):
                    for f in range(NFT):
                        _mm(P, Bsc.t[:, ut * 128:(ut + 1) * 128], egw1_sb[:, f, ut * 128:(ut + 1) * 128],
                            crT[:, f, :], start=(f == 0), stop=(f == NFT - 1),
                            inc=(Bsc.mm if (ut == 3 and f == NFT - 1) else None))
                Bsc.await_data("scalar")
                for ut in range(4):
                    P.op("scalar", lambda e, o=h1_bf[:, ut, :], s=Bsc.t[:, ut * 128:(ut + 1) * 128],
                         b=egb1_sb[:, ut:ut + 1]: e.activation(o, s, AF.Relu, bias=b),
                         inc=(Bsc.rel("scalar") if ut == 3 else None))
                Bnum.await_free()
                P.wait("tensor", Bsc.rel("scalar"))
                for ut in range(4):
                    _mm(P, Bnum.t[:, 0:1], h1_bf[:, ut, :], egw2_sb[:, ut:ut + 1],
                        start=(ut == 0), stop=(ut == 3), inc=(Bnum.mm if ut == 3 else None))
                Bnum.await_data("scalar")
                gc_tok = P.op("scalar", lambda e, o=g_col[:, :], s=Bnum.t[:, 0:1], b=egb2_sb[:, :]:
                              e.activation(o, s, AF.Sigmoid, bias=b), inc=Bnum.rel("scalar"))
                P.wait("vector", Bnum.rel("scalar"), gc_tok)
                omg_tok = P.opw("vector", lambda e, o=omg_col[:, :], s=g_col[:, :]:
                                e.tensor_scalar(o, s, -1.0, 1.0, ALU.mult, ALU.add))
                P.opw("vector", lambda e, o=coeffm[:, :, :], a=r_all[:, :, 0:4], b=coeffbase[:, :, 0:4]:
                      e.tensor_tensor(o, a, b, ALU.mult))
                P.opw("vector", lambda e, o=coeffm[:, :, :], s=coeffm[:, :, :], a=g_col[:, :]:
                      e.tensor_scalar(o, s, a, None, ALU.mult))
                cm_tok = P.opw("vector", lambda e, o=c2cols[:, :], s=gate_cols[:, :], a=omg_col[:, :]:
                               e.tensor_scalar(o, s, a, None, ALU.mult))

                if c > 0:
                    P.waitk("vector", oab_rel[c - 1])
                    if dbg and dbg_oa_tgt[c - 1]:
                        P.wait("vector", splS, dbg_oa_tgt[c - 1])
                for h in range(NH):
                    P.op("vector", lambda e, o=oa[:, h * 128:(h + 1) * 128], s=num_sb[:, h, :],
                         a=c_loc16[:, h:h + 1]: e.scalar_tensor_tensor(o, s, a, o, ALU.mult, ALU.bypass))
                for m in range(NM):
                    for h in range(NH):
                        P.op("vector", lambda e, o=oa[:, h * 128:(h + 1) * 128],
                             s=af_sb[:, h, m * 128:(m + 1) * 128], a=coeffm[:, h, m:m + 1]:
                             e.scalar_tensor_tensor(o, s, a, o, ALU.mult, ALU.add))
                for h in range(NH):
                    P.op("vector", lambda e, o=oa[:, h * 128:(h + 1) * 128], s=crf[:, h * 128:(h + 1) * 128],
                         a=c2cols[:, h:h + 1]: e.scalar_tensor_tensor(o, s, a, o, ALU.mult, ALU.add),
                         inc=(True if h == NH - 1 else None))
                pool_rel[c] = (P.ax["vector"], P.ax["vector"].v)
                rl_rel[c] = (Bdel.mm, Bdel.mm.v)

                P.waitk("scalar", pool_rel[c])
                oab_tok = P.op("scalar", lambda e, o=oa_bf[:, :], s=oa[:, :]: e.copy(o, s), inc=True)
                oab_rel[c] = oab_tok
                P.waitk("tensor", oab_tok)
                if c >= 1:
                    P.wait("scalar", splS, oT_sp_tgt[c - 1])
                for r in range(2):
                    Btr2.await_free()
                    for k in range(8):
                        f = r * 8 + k
                        _tr(P, Btr2.t[:, k * 128:(k + 1) * 128], oa_bf[:, f * 128:(f + 1) * 128], idb[:, :],
                            inc=(Btr2.mm if k == 7 else None))
                    Btr2.await_data("scalar")
                    P.op("scalar", lambda e, d=oT[:, r * 8:(r + 1) * 8, :],
                         s=Btr2.t[:, :].rearrange("p (k j) -> p k j", j=128): e.copy(d, s),
                         inc=Btr2.rel("scalar"))
                oT_done = Btr2.rel("scalar").v
                P.wait("sync", Btr2.rel("scalar"), oT_done)
                oT_sp_tgt[c] = P.dma("sync", oT_sp[c], oT[:, :, :], splS)
                if dbg:
                    P.waitk("sync", pool_rel[c])
                    dbg_oa_tgt[c] = P.dma("sync", attn_dbg[c], oa[:, :], splS)
                    if c == 0:
                        P.waitk("sync", oab_rel[0])
                        P.dma("sync", dbg_n, n_all[:, :, :], splS)
                        P.dma("sync", dbg_cm, coeffm[:, :, :], splS)
                        P.dma("sync", dbg_crf, crf[:, :], splS)
                        P.dma("sync", dbg_af, af_sb[:, :, :], splS)
                        P.dma("sync", dbg_S, S_f32[:, :, :], splS)
                        P.dma("sync", dbg_misc[:, 0:16], gate_cols[:, :], splS, allow_slow_non_contiguous=True)
                        P.dma("sync", dbg_misc[:, 16:32], omgate_cols[:, :], splS, allow_slow_non_contiguous=True)
                        P.dma("sync", dbg_misc[:, 32:48], sqsumg[:, :], splS, allow_slow_non_contiguous=True)
                        P.dma("sync", dbg_misc[:, 48:64], crcoefs[:, :], splS, allow_slow_non_contiguous=True)
                        P.dma("sync", dbg_misc[:, 64:80], c2cols[:, :], splS, allow_slow_non_contiguous=True)
                        P.dma("sync", dbg_misc[:, 80:81], zalive_col[:, :], splS,
                              allow_slow_non_contiguous=True)
                        P.dma("sync", dbg_misc[:, 81:82], g_col[:, :], splS,
                              allow_slow_non_contiguous=True)
                        P.dma("sync", dbg_misc[:, 82:83], egb2_sb[:, :], splS,
                              allow_slow_non_contiguous=True)
                        P.dma("sync", dbg_misc[:, 83:88], coeffbase[:, 0, :], splS, allow_slow_non_contiguous=True)
                        P.dma("sync", dbg_misc[:, 88:93], coeffbase[:, 1, :], splS, allow_slow_non_contiguous=True)
                        P.dma("sync", dbg_sqrl, sq_rl[0][:, :, :], splS)
                        P.dma("sync", dbg_fm, fmem_sb[:, :, :], splS)
                        dbg_gate[0] = splS.v
                        P.dma("sync", dbg_misc[0:1, 96:128], rowscr[0:1, 0:32], splS, allow_slow_non_contiguous=True)
                        P.dma("sync", dbg_misc[1:2, 96:128], rowscr[0:1, 32:64], splS, allow_slow_non_contiguous=True)
                if c + 2 < NCH:
                    plan_reload(c + 2)

        P.barrier(bar, scalar_scratch=rowscr[0:1, 255:256])
        P.wait("gpsimd", splS, splS.v)
        P.wait("sync", splS, splS.v)

        with ExitStack() as ph2b:
            psb = lambda n, s, d=F32: ph2b.enter_context(nc.sbuf_tensor(n, s, d))
            wo_sb = psb("wo_sb", [128, NFT, D], BF16)
            oT_rl = [psb(f"oT_rl{i}", [128, NFT, 128], BF16) for i in range(2)]
            x_res = [psb(f"x_res{i}", [128, D]) for i in range(2)]
            ln_in = psb("ln_in", [128, D])
            ln_sc = psb("ln_sc", [128, D])
            x1f = psb("x1f", [128, D])
            Bw = [Bank(P, ph2b, f"pw{i}", [128, 512], F32) for i in range(8)]
            P.wait("tensor", Btr2.mm)
            for _b in B2 + [Btr2]:
                for _s in _b.ev.values():
                    P.wait("tensor", _s)

            wo_tgt = None
            for f in range(NFT):
                wo_tgt = P.dma("gpsimd", wo_sb[:, f, :], wo_d[f * 128:(f + 1) * 128, :], ldG)

            oTr_tgt = [None] * NCH
            xr_tgt = [None] * NCH
            oTr_rel = [None] * NCH
            xres_rel = [None] * NCH
            x1fS_tgt = [None] * NCH

            def plan_reload2(c):
                p = c % 2
                if c >= 2:
                    P.waitk("sync", oTr_rel[c - 2])
                oTr_tgt[c] = P.dma("sync", oT_rl[p][:, :, :], oT_sp[c], rldS)
                if c >= 2:
                    P.wait("sync", xres_rel[c - 2][0], xres_rel[c - 2][1])
                xr_tgt[c] = P.dma("sync", x_res[p][:, :], x_d[c * 128:(c + 1) * 128, :], rldS)

            plan_reload2(0)
            plan_reload2(1)
            for c in range(NCH):
                p = c % 2
                bwg = Bw[0:4] if c % 2 == 0 else Bw[4:8]
                for b in bwg:
                    b.await_free()
                P.wait("tensor", rldS, oTr_tgt[c])
                if c == 0:
                    P.wait("tensor", ldG, wo_tgt)
                for f in range(NFT):
                    for oc in range(4):
                        _mm(P, bwg[oc].t[:, :], oT_rl[p][:, f, :], wo_sb[:, f, oc * 512:(oc + 1) * 512],
                            start=(f == 0), stop=(f == NFT - 1), inc=(bwg[oc].mm if f == NFT - 1 else None))
                oTr_rel[c] = (bwg[3].mm, bwg[3].mm.v)
                P.wait("vector", rldS, xr_tgt[c])
                for oc in range(4):
                    bwg[oc].await_data("vector")
                    P.op("vector", lambda e, o=ln_in[:, oc * 512:(oc + 1) * 512], a=bwg[oc].t[:, :],
                         b=x_res[p][:, oc * 512:(oc + 1) * 512]: e.tensor_tensor(o, a, b, ALU.add),
                         inc=bwg[oc].rel("vector"))
                xres_rel[c] = (bwg[3].rel("vector"), bwg[3].rel("vector").v)
                P.opw("vector", lambda e: e.engine_nop())
                P.opw("vector", lambda e, o=ln_m[:, :], s=ln_in[:, :]: e.reduce_sum(o, s, axis=AX))
                P.opw("vector", lambda e, o=ln_m[:, :], s=ln_m[:, :]:
                      e.tensor_single_scalar(o, s, 1.0 / D, ALU.mult))
                xm_tok = P.opw("vector", lambda e, o=ln_sc[:, :], s=ln_in[:, :], a=ln_m[:, :]:
                               e.tensor_scalar(o, s, a, None, ALU.subtract))
                P.waitk("scalar", xm_tok)
                sq_tok = P.op("scalar", lambda e, o=ln_in[:, :], s=ln_sc[:, :], acc=ln_v[:, :]:
                              e.activation(o, s, AF.Square, accum_out=acc), inc=True)
                P.waitk("vector", sq_tok)
                P.opw("vector", lambda e, o=ln_v[:, :], s=ln_v[:, :]:
                      e.tensor_single_scalar(o, s, 1.0 / D, ALU.mult))
                P.opw("vector", lambda e, o=ln_v[:, :], s=ln_v[:, :]:
                      e.tensor_single_scalar(o, s, LN_EPS, ALU.add))
                rv_tok = P.opw("vector", lambda e, o=ln_r[:, :], s=ln_v[:, :]: e.reciprocal(o, s))
                P.waitk("scalar", rv_tok)
                rs_tok = P.op("scalar", lambda e, o=ln_r[:, :], s=ln_r[:, :]: e.activation(o, s, AF.Sqrt),
                              inc=True)
                P.waitk("vector", rs_tok)
                if c >= 1:
                    P.wait("vector", splS, x1fS_tgt[c - 1])
                x1_tok = P.op("vector", lambda e, o=x1f[:, :], s=ln_sc[:, :], a=ln_r[:, :]:
                              e.tensor_scalar(o, s, a, None, ALU.mult), inc=True)
                P.waitk("sync", x1_tok)
                x1fS_tgt[c] = P.dma("sync", x1f_sp[c], x1f[:, :], splS)
                if c + 2 < NCH:
                    plan_reload2(c + 2)

        P.barrier(bar, scalar_scratch=rowscr[0:1, 255:256])
        P.wait("gpsimd", splS, splS.v)
        P.wait("gpsimd", splG, splG.v)
        P.wait("sync", splS, splS.v)
        P.wait("sync", splG, splG.v)

        with ExitStack() as ph3:
            psb = lambda n, s, d=F32: ph3.enter_context(nc.sbuf_tensor(n, s, d))
            x1T = psb("x1T", [128, NFT, HALF], BF16)
            acc = psb("acc", [128, 8, D])
            cb = psb("cb", [128, 9, HALF], BF16)
            gs = psb("gs", [128, 4, HALF], BF16)
            u_tmp = psb("u_tmp", [128, 2, HALF], BF16)
            wbuf = [psb(f"wbuf{i}", [128, 8192], BF16) for i in range(2)]
            gsg2 = psb("gsg2", [128, 9])
            comb = psb("comb", [128, 9])
            comb_rT = psb("comb_rT", [9, 8, 128])
            l2_in = psb("l2sc", [128, D])
            sel_sb = psb("sel_sb", [9, 9, 128])
            gsg_sb = psb("gsg_sb", [128, NFT, 9], BF16)
            gsg2b = psb("gsg2b", [128, 9])
            sel_tgt = P.dma("sync", sel_sb[:, :, :], sel_d, rldS)
            gsg_tgt = None
            for f in range(NFT):
                gsg_tgt = P.dma("gpsimd", gsg_sb[:, f, :], gsg_d[f * 128:(f + 1) * 128, :], ldG)

            with ExitStack() as ph3p:
                Bp = [Bank(P, ph3p, f"p3_{i}", [128, 512], F32) for i in range(8)]
                P.wait("tensor", Bw[7].mm)
                for _b in Bw:
                    for _s in _b.ev.values():
                        P.wait("tensor", _s)


                xb_rel = [None] * 8
                acc_tgt = [None] * 8
                dbg_cmb_tgt = [None] * 16

                def emit_half_prep(half):
                    if half == 1:
                        P.wait("sync", stS, stS.v)
                    for tt in range(8):
                        acc_tgt[tt] = P.dma("sync", acc[:, tt, :], x1f_sp[half * 8 + tt], rldS)
                    for tt in range(8):
                        P.wait("tensor", rldS, acc_tgt[tt])
                        for r in range(4):
                            Bt = Bp[2 + (r % 2)]
                            Bt.await_free()
                            for k in range(4):
                                f = r * 4 + k
                                _tr(P, Bt.t[:, k * 128:(k + 1) * 128],
                                    acc[:, tt, f * 128:(f + 1) * 128], idf[:, :],
                                    inc=(Bt.mm if k == 3 else None))
                            Bt.await_data("scalar")
                            P.op("scalar", lambda e, d=x1T[:, r * 4:(r + 1) * 4, tt * 128:(tt + 1) * 128],
                                 s=Bt.t[:, :].rearrange("p (k j) -> p k j", j=128): e.copy(d, s),
                                 inc=Bt.rel("scalar"))
                    x1T_tgt_a = Bp[2].rel("scalar").v
                    x1T_tgt_b = Bp[3].rel("scalar").v
                    P.wait("tensor", Bp[2].rel("scalar"), x1T_tgt_a)
                    P.wait("tensor", Bp[3].rel("scalar"), x1T_tgt_b)
                    if dbg:
                        P.wait("sync", Bp[2].rel("scalar"), x1T_tgt_a)
                        P.wait("sync", Bp[3].rel("scalar"), x1T_tgt_b)
                        x1T_dump = P.dma("sync", dbg_x1T[half], x1T[:, :, :], rldS)
                        P.wait("tensor", rldS, x1T_dump)

                    for tt in range(8):
                        Bg = Bp[tt % 2]
                        Bg.await_free()
                        if tt == 0:
                            P.wait("tensor", ldG, gsg_tgt)
                        for f in range(NFT):
                            _mm(P, Bg.t[:, 0:9], x1T[:, f, tt * 128:(tt + 1) * 128], gsg_sb[:, f, :],
                                start=(f == 0), stop=(f == NFT - 1), inc=(Bg.mm if f == NFT - 1 else None))
                        Bg.await_data("vector")
                        lg_tok = P.opw("vector", lambda e, o=gsg2[:, :], s=Bg.t[:, 0:9]: e.tensor_copy(o, s))
                        if dbg:
                            Bg.await_data("scalar")
                            lg2_tok = P.op("scalar", lambda e, o=gsg2b[:, :], s=Bg.t[:, 0:9]:
                                           e.copy(o, s), inc=True)
                            P.waitk("sync", lg2_tok)
                            lgt2_t = P.dma("sync", dbg_lgt2[half * 8 + tt], gsg2b[:, :], rldS)
                            P.wait("vector", rldS, lgt2_t)
                            P.wait("scalar", rldS, lgt2_t)
                        P.op("vector", lambda e: e.engine_nop(), inc=Bg.rel("vector"))
                        if dbg:
                            P.waitk("sync", lg_tok)
                            lgt_t = P.dma("sync", dbg_lgt[half * 8 + tt], gsg2[:, :], rldS)
                            P.wait("vector", rldS, lgt_t)
                        P.opw("vector", lambda e, o=ln_m[:, :], s=gsg2[:, 0:8]: e.reduce_max(o, s, axis=AX))
                        sm1 = P.opw("vector", lambda e, o=comb[:, 0:8], s=gsg2[:, 0:8], a=ln_m[:, :]:
                                    e.tensor_scalar(o, s, a, None, ALU.subtract))
                        P.waitk("scalar", sm1)
                        sm2 = P.op("scalar", lambda e, o=comb[:, 0:8], s=comb[:, 0:8]:
                                   e.activation(o, s, AF.Exp), inc=True)
                        sg1 = P.op("scalar", lambda e, o=comb[:, 8:9], s=gsg2[:, 8:9]:
                                   e.activation(o, s, AF.Sigmoid), inc=True)
                        P.waitk("vector", sg1)
                        P.opw("vector", lambda e, o=ln_m[:, :], s=comb[:, 0:8]: e.reduce_sum(o, s, axis=AX))
                        P.opw("vector", lambda e, o=ln_v[:, :], s=ln_m[:, :]: e.reciprocal(o, s))
                        P.opw("vector", lambda e, o=comb[:, 0:8], s=comb[:, 0:8], a=ln_v[:, :]:
                              e.tensor_scalar(o, s, a, None, ALU.mult))
                        P.opw("vector", lambda e, o=ln_m[:, :], s=comb[:, 0:8]: e.reduce_max(o, s, axis=AX))
                        P.opw("vector", lambda e, o=gsg2[:, 0:8], s=comb[:, 0:8], a=ln_m[:, :]:
                              e.tensor_scalar(o, s, a, None, ALU.subtract))
                        P.opw("vector", lambda e, o=gsg2[:, 0:8], s=gsg2[:, 0:8]:
                              e.tensor_single_scalar(o, s, 0.0, ALU.is_ge))
                        P.opw("vector", lambda e, o=gsg2[:, 0:8], s=gsg2[:, 0:8], b=comb[:, 0:8]:
                              e.scalar_tensor_tensor(o, s, -1e30, b, ALU.mult, ALU.add))
                        P.opw("vector", lambda e, o=ln_v[:, :], s=gsg2[:, 0:8]: e.reduce_max(o, s, axis=AX))
                        P.opw("vector", lambda e, o=gsg2[:, 0:8], s=comb[:, 0:8], a=ln_v[:, :]:
                              e.tensor_scalar(o, s, a, None, ALU.subtract))
                        P.opw("vector", lambda e, o=gsg2[:, 0:8], s=gsg2[:, 0:8]:
                              e.tensor_single_scalar(o, s, 0.0, ALU.is_ge))
                        P.opw("vector", lambda e, o=ln_m[:, :], a=ln_m[:, :], b=ln_v[:, :]:
                              e.tensor_tensor(o, a, b, ALU.add))
                        P.opw("vector", lambda e, o=ln_v[:, :], s=ln_m[:, :]: e.reciprocal(o, s))
                        P.opw("vector", lambda e, o=comb[:, 0:8], a=comb[:, 0:8], b=gsg2[:, 0:8]:
                              e.tensor_tensor(o, a, b, ALU.mult))
                        cmb = P.opw("vector", lambda e, o=comb[:, 0:8], s=comb[:, 0:8], a=ln_v[:, :]:
                                    e.tensor_scalar(o, s, a, None, ALU.mult))
                        if dbg:
                            P.waitk("sync", cmb)
                            dbg_cmb_tgt[half * 8 + tt] = P.dma("sync", dbg_cmb[half * 8 + tt],
                                                               comb[:, :], rldS)
                        Bt = Bp[2]
                        Bt.await_free()
                        P.waitk("tensor", cmb)
                        if dbg:
                            P.wait("tensor", rldS, dbg_cmb_tgt[half * 8 + tt])
                        _tr(P, Bt.t[0:9, 0:128], comb[:, 0:9], idf[:, :], inc=Bt.mm)
                        Bt.await_data("vector")
                        P.op("vector", lambda e, o=comb_rT[:, tt, :], s=Bt.t[0:9, 0:128]:
                             e.tensor_copy(o, s), inc=Bt.rel("vector"))
                    if dbg:
                        gl_done = P.mark("vector")
                        P.waitk("sync", gl_done)
                        x1T_d2 = P.dma("sync", dbg_x1T2[half], x1T[:, :, :], rldS)
                        P.wait("tensor", rldS, x1T_d2)
                    crT_done = Bp[2].rel("vector").v
                    for e9 in range(9):
                        for tc in range(2):
                            Bt = Bp[3]
                            Bt.await_free()
                            if e9 == 0 and tc == 0:
                                P.wait("tensor", Bp[2].rel("vector"), crT_done)
                                P.wait("tensor", rldS, sel_tgt)
                            _mm(P, Bt.t[:, :], sel_sb[:, e9, :], comb_rT[:, tc * 4:(tc + 1) * 4, :],
                                True, True, inc=Bt.mm)
                            Bt.await_data("vector")
                            P.op("vector", lambda e, o=cb[:, e9, tc * 512:(tc + 1) * 512], s=Bt.t[:, :]:
                                 e.tensor_copy(o, s), inc=Bt.rel("vector"))
                    P.opw("vector", lambda e: e.engine_nop())
                    if dbg:
                        Bx = Bp[1]
                        Bx.await_free()
                        for f in range(NFT):
                            _mm(P, Bx.t[:, 0:9], x1T[:, f, 0:128], gsg_sb[:, f, :],
                                start=(f == 0), stop=(f == NFT - 1), inc=(Bx.mm if f == NFT - 1 else None))
                        Bx.await_data("vector")
                        l3 = P.opw("vector", lambda e, o=gsg2[:, :], s=Bx.t[:, 0:9]: e.tensor_copy(o, s))
                        P.op("vector", lambda e: e.engine_nop(), inc=Bx.rel("vector"))
                        P.waitk("sync", l3)
                        l3t = P.dma("sync", dbg_lgt3[half], gsg2[:, :], rldS)
                        P.wait("vector", rldS, l3t)

                def emit_half_experts(half):
                    subs = []
                    for e9 in range(E_MOE):
                        subs += [(e9, 0, 512), (e9, 512, 512), (e9, 1024, 384)]
                    subs += [(8, g * 512, 512) for g in range(11)]
                    wload_tgt = {}

                    def load_gu(which, sub_idx, slot):
                        e9, off, sz = subs[sub_idx]
                        if e9 < E_MOE:
                            src = (wg_d if which == "g" else wu_d)
                            aps = [src[e9, f * 128:(f + 1) * 128, off:off + sz] for f in range(NFT)]
                        else:
                            src = (swg_d if which == "g" else swu_d)
                            aps = [src[f * 128:(f + 1) * 128, off:off + sz] for f in range(NFT)]
                        tg = None
                        for f in range(NFT):
                            tg = P.dma("gpsimd", wbuf[slot][:, f * 512:f * 512 + sz], aps[f], ldG)
                        wload_tgt[(which, sub_idx)] = tg

                    def load_d(sub_idx, slot):
                        e9, off, sz = subs[sub_idx]
                        nit = sz // 128
                        tg = None
                        for it in range(nit):
                            if e9 < E_MOE:
                                ap = wd_d[e9, off + it * 128: off + (it + 1) * 128, :]
                            else:
                                ap = swd_d[off + it * 128: off + (it + 1) * 128, :]
                            tg = P.dma("gpsimd", wbuf[slot][:, it * D:(it + 1) * D], ap, ldG)
                        wload_tgt[("d", sub_idx)] = tg

                    load_gu("g", 0, 0)
                    load_gu("u", 0, 1)
                    hm_prev = [None, None]

                    for si, (e9, off, sz) in enumerate(subs):
                        nit = sz // 128
                        sg_slot = (3 * si) % 2
                        su_slot = (3 * si + 1) % 2
                        sd_slot = (3 * si + 2) % 2

                        P.wait("tensor", ldG, wload_tgt[("g", si)])
                        for it in range(nit):
                            bp = [Bp[0], Bp[1]] if it % 2 == 0 else [Bp[2], Bp[3]]
                            for b in bp:
                                b.await_free()
                            for f in range(NFT):
                                lh = wbuf[sg_slot][:, f * 512 + it * 128: f * 512 + (it + 1) * 128]
                                _mm(P, bp[0].t[:, :], lh, x1T[:, f, 0:512], start=(f == 0),
                                    stop=(f == NFT - 1), inc=(bp[0].mm if f == NFT - 1 else None))
                                _mm(P, bp[1].t[:, :], lh, x1T[:, f, 512:1024], start=(f == 0),
                                    stop=(f == NFT - 1), inc=(bp[1].mm if f == NFT - 1 else None))
                            for tc in range(2):
                                bp[tc].await_data("scalar")
                                P.op("scalar", lambda e, o=gs[:, it, tc * 512:(tc + 1) * 512],
                                     s=bp[tc].t[:, :]: e.activation(o, s, AF.Silu), inc=bp[tc].rel("scalar"))
                        lb = Bp[1] if nit % 2 == 1 else Bp[3]
                        g_done = (lb.mm, lb.mm.v)
                        P.waitk("gpsimd", g_done)
                        load_d(si, sg_slot)

                        P.wait("tensor", ldG, wload_tgt[("u", si)])
                        for it in range(nit):
                            bp = [Bp[0], Bp[1]] if it % 2 == 0 else [Bp[2], Bp[3]]
                            for b in bp:
                                b.await_free()
                            for f in range(NFT):
                                lh = wbuf[su_slot][:, f * 512 + it * 128: f * 512 + (it + 1) * 128]
                                _mm(P, bp[0].t[:, :], lh, x1T[:, f, 0:512], start=(f == 0),
                                    stop=(f == NFT - 1), inc=(bp[0].mm if f == NFT - 1 else None))
                                _mm(P, bp[1].t[:, :], lh, x1T[:, f, 512:1024], start=(f == 0),
                                    stop=(f == NFT - 1), inc=(bp[1].mm if f == NFT - 1 else None))
                            for tc in range(2):
                                bp[tc].await_data("vector")
                                if hm_prev[it % 2] is not None:
                                    P.waitk("vector", hm_prev[it % 2])
                                ut = P.op("vector", lambda e, o=u_tmp[:, it % 2, tc * 512:(tc + 1) * 512],
                                          a=bp[tc].t[:, :], b=cb[:, e9, tc * 512:(tc + 1) * 512]:
                                          e.tensor_tensor(o, a, b, ALU.mult), inc=bp[tc].rel("vector"))
                                P.wait("gpsimd", bp[tc].rel("vector"), ut)
                                hm_prev[it % 2] = P.op("gpsimd", lambda e, o=gs[:, it, tc * 512:(tc + 1) * 512],
                                     a=gs[:, it, tc * 512:(tc + 1) * 512],
                                     b=u_tmp[:, it % 2, tc * 512:(tc + 1) * 512]:
                                     e.tensor_tensor(o, a, b, ALU.mult), inc=True)
                        lb = Bp[1] if nit % 2 == 1 else Bp[3]
                        u_done = (lb.mm, lb.mm.v)
                        P.waitk("gpsimd", u_done)
                        if si + 1 < len(subs):
                            load_gu("g", si + 1, su_slot)
                        h_done = P.mark("gpsimd")

                        P.wait("tensor", ldG, wload_tgt[("d", si)])
                        P.waitk("tensor", h_done)
                        for tt in range(8):
                            bq = [Bp[4], Bp[5], Bp[6], Bp[7]] if tt % 2 == 0 else [Bp[0], Bp[1], Bp[2], Bp[3]]
                            for b in bq:
                                b.await_free()
                            for it in range(nit):
                                lh = gs[:, it, tt * 128:(tt + 1) * 128]
                                for oc in range(4):
                                    _mm(P, bq[oc].t[:, :], lh,
                                        wbuf[sd_slot][:, it * D + oc * 512: it * D + (oc + 1) * 512],
                                        start=(it == 0), stop=(it == nit - 1),
                                        inc=(bq[oc].mm if it == nit - 1 else None))
                            if si == 0:
                                P.wait("vector", rldS, acc_tgt[tt])
                            for oc in range(4):
                                bq[oc].await_data("vector")
                                P.op("vector", lambda e, o=acc[:, tt, oc * 512:(oc + 1) * 512],
                                     a=acc[:, tt, oc * 512:(oc + 1) * 512], b=bq[oc].t[:, :]:
                                     e.tensor_tensor(o, a, b, ALU.add), inc=bq[oc].rel("vector"))
                        d_done = (bq[3].mm, bq[3].mm.v)
                        P.waitk("gpsimd", d_done)
                        if si + 1 < len(subs):
                            load_gu("u", si + 1, sd_slot)

                def emit_half_ln(half):
                    if dbg and half == 0:
                        acc_done = P.mark("vector")
                        P.waitk("sync", acc_done)
                        tgt = P.dma("sync", dbg_acc, acc[:, 0, :], rldS)
                        P.wait("vector", rldS, tgt)
                    for tt in range(8):
                        P.opw("vector", lambda e: e.engine_nop())
                        P.opw("vector", lambda e, o=ln_m[:, :], s=acc[:, tt, :]: e.reduce_sum(o, s, axis=AX))
                        P.opw("vector", lambda e, o=ln_m[:, :], s=ln_m[:, :]:
                              e.tensor_single_scalar(o, s, 1.0 / D, ALU.mult))
                        xm = P.opw("vector", lambda e, o=l2_in[:, :], s=acc[:, tt, :], a=ln_m[:, :]:
                                   e.tensor_scalar(o, s, a, None, ALU.subtract))
                        P.waitk("scalar", xm)
                        sq2 = P.op("scalar", lambda e, o=acc[:, tt, :], s=l2_in[:, :], a2=ln_v[:, :]:
                                   e.activation(o, s, AF.Square, accum_out=a2), inc=True)
                        P.waitk("vector", sq2)
                        P.opw("vector", lambda e, o=ln_v[:, :], s=ln_v[:, :]:
                              e.tensor_single_scalar(o, s, 1.0 / D, ALU.mult))
                        P.opw("vector", lambda e, o=ln_v[:, :], s=ln_v[:, :]:
                              e.tensor_single_scalar(o, s, LN_EPS, ALU.add))
                        rv = P.opw("vector", lambda e, o=ln_r[:, :], s=ln_v[:, :]: e.reciprocal(o, s))
                        P.waitk("scalar", rv)
                        rs = P.op("scalar", lambda e, o=ln_r[:, :], s=ln_r[:, :]:
                                  e.activation(o, s, AF.Sqrt), inc=True)
                        P.waitk("vector", rs)
                        ob = P.op("vector", lambda e, o=acc[:, tt, :], s=l2_in[:, :], a=ln_r[:, :]:
                                  e.tensor_scalar(o, s, a, None, ALU.mult), inc=True)
                        P.waitk("sync", ob)
                        P.dma("sync", out_d[(half * 8 + tt) * 128:(half * 8 + tt + 1) * 128, :],
                              acc[:, tt, :], stS)

                for half in range(2):
                    emit_half_prep(half)
                    emit_half_experts(half)
                    emit_half_ln(half)
                    if half == 0:
                        P.barrier(bar, scalar_scratch=rowscr[0:1, 255:256])

        P.wait("sync", stS, stS.v)

        with nc.Block() as block:
            @block.sync
            def _(e):
                for fn in P.q["sync"]:
                    fn(e)

            @block.tensor
            def _(e):
                for fn in P.q["tensor"]:
                    fn(e)

            @block.vector
            def _(e):
                for fn in P.q["vector"]:
                    fn(e)

            @block.scalar
            def _(e):
                for fn in P.q["scalar"]:
                    fn(e)

            @block.gpsimd
            def _(e):
                for fn in P.q["gpsimd"]:
                    fn(e)

    return nc



def make_in_maps(inputs):
    f32 = np.float32
    asf = lambda a: np.ascontiguousarray(np.asarray(a), dtype=f32)
    x_full = asf(inputs["L0_attn_hidden_states"]).reshape(S_TOT, D)
    wq_t = np.ascontiguousarray(asf(inputs["L0_attn_w_q"]).T)
    wk_t = np.ascontiguousarray(asf(inputs["L0_attn_w_k"]).T)
    wv_t = np.ascontiguousarray(asf(inputs["L0_attn_w_v"]).T)
    wo_t = np.ascontiguousarray(asf(inputs["L0_attn_w_o"]).T)
    egw1_t = np.ascontiguousarray(asf(inputs["L0_attn_eg_w1"]).T)
    egb1_c = np.ascontiguousarray(asf(inputs["L0_attn_eg_b1"]).reshape(4, 128).T)
    egw2_c = np.ascontiguousarray(asf(inputs["L0_attn_eg_w2"]).reshape(4, 128).T)
    egb2_c = asf(inputs["L0_attn_eg_b2"]).reshape(1, 1)
    memgate = asf(inputs["L0_attn_memory_gate"]).reshape(1, NH)
    fmem = asf(inputs["L0_attn_fine_memories"])
    fn = asf(inputs["L0_attn_fine_memory_norms"])
    fnorm_c = np.zeros((128, NH, 5), f32)
    for m in range(NM):
        fnorm_c[:, :, m] = fn[m].T
    fnorm_c = np.ascontiguousarray(fnorm_c.reshape(128, NH * 5))
    gsg_t = np.ascontiguousarray(
        np.concatenate([asf(inputs["L0_ffn_gate_w"]).T, asf(inputs["L0_ffn_shared_gate_w"])], axis=1))
    wg = asf(inputs["L0_ffn_w_gate"])
    wu = asf(inputs["L0_ffn_w_up"])
    wd = asf(inputs["L0_ffn_w_down"])
    swg = asf(inputs["L0_ffn_sw_gate"])
    swu = asf(inputs["L0_ffn_sw_up"])
    swd = asf(inputs["L0_ffn_sw_down"])

    ident_bf = np.eye(128, dtype=BF16NP)
    ident_f32 = np.eye(128, dtype=f32)
    maskT = np.triu(np.ones((128, 128), f32))
    ones_row = np.ones((1, 128), f32)
    ones_col = np.ones((128, 1), f32)
    sel9 = np.zeros((9, 9 * 128), f32)
    for e in range(9):
        sel9[e, e * 128:(e + 1) * 128] = 1.0

    in_maps = []
    for cid in range(NCORES):
        pm = np.zeros((128, NCORES), f32)
        pm[:, :cid] = 1.0
        in_maps.append({
            "x": np.ascontiguousarray(x_full[cid * TPC:(cid + 1) * TPC, :]),
            "wq_t": wq_t, "wk_t": wk_t, "wv_t": wv_t, "wo_t": wo_t,
            "egw1_t": egw1_t, "egb1_c": egb1_c, "egw2_c": egw2_c, "egb2_c": egb2_c,
            "memgate": memgate, "fmem": fmem, "fnorm_c": fnorm_c, "gsg_t": gsg_t,
            "wg": wg, "wu": wu, "wd": wd, "swg": swg, "swu": swu, "swd": swd,
            "ident_bf": ident_bf, "ident_f32": ident_f32, "maskT": maskT,
            "ones_row": ones_row, "ones_col": ones_col, "sel9": sel9, "pmask": pm,
        })
    return in_maps


def kernel(**inputs):
    nc = build(dbg=False)
    in_maps = make_in_maps(inputs)
    res = run_bass_kernel_spmd(nc, in_maps, core_ids=list(range(NCORES)))
    out = np.concatenate([res.results[c]["out"] for c in range(NCORES)], axis=0)
    return out.reshape(1, S_TOT, D).astype(np.float32)


# version 06
# speedup vs baseline: 1.0074x; 1.0074x over previous
\

import numpy as np
import ml_dtypes
from contextlib import ExitStack

import concourse.bass as bass
import concourse.mybir as mybir
from concourse.bass_utils import run_bass_kernel_spmd

F32 = mybir.dt.float32
BF16 = mybir.dt.bfloat16
AF = mybir.ActivationFunctionType
ALU = mybir.AluOpType
AX = mybir.AxisListType.X
BF16NP = ml_dtypes.bfloat16

NCORES = 8
S_TOT = 16384
D = 2048
NH, HD, NM = 16, 128, 4
TPC = S_TOT // NCORES
NCH = TPC // 128
NFT = D // 128
EPS = 1e-6
LN_EPS = 1e-5
E_MOE, I_MOE, I_SH = 8, 1408, 5632
HALF = TPC // 2
REL_SCALE = 1.0 / float(S_TOT)

ENGINES = ("sync", "tensor", "vector", "scalar", "gpsimd")


class Sem:
    __slots__ = ("h", "v", "name")

    def __init__(self, h, name):
        self.h = h
        self.v = 0
        self.name = name


class Planner:
    def __init__(self, nc, ctx):
        self.nc = nc
        self.ctx = ctx
        self.q = {e: [] for e in ENGINES}
        self.ax = {e: self.sem("ax_" + e) for e in ENGINES}

    def sem(self, name):
        return Sem(self.ctx.enter_context(self.nc.semaphore(name)), name)

    def wait(self, eng, sem, val=None):
        v = sem.v if val is None else val
        if v is None or v <= 0:
            return
        self.q[eng].append(lambda e, h=sem.h, v=v: e.wait_ge(h, v))

    def waitk(self, eng, tok):
        if tok is None:
            return
        self.wait(eng, tok[0], tok[1])

    def op(self, eng, fn, inc=None, by=1):
        if inc is True:
            inc = self.ax[eng]
            ret_tok = True
        else:
            ret_tok = False
        if inc is not None:
            inc.v += by

            def g(e, fn=fn, h=inc.h, by=by):
                fn(e).then_inc(h, by)
        else:
            def g(e, fn=fn):
                fn(e)
        self.q[eng].append(g)
        if inc is None:
            return None
        return (inc, inc.v) if ret_tok else inc.v

    def opw(self, eng, fn):
        tok = self.op(eng, fn, inc=True)
        self.waitk(eng, tok)
        return tok

    def mark(self, eng):
        if eng == "vector":
            return self.op(eng, lambda e: e.engine_nop(), inc=True)
        return self.op(eng, lambda e: e.nop(), inc=True)

    def dma(self, eng, out, in_, sem, **kw):
        return self.op(eng, lambda e, o=out, i=in_, kw=dict(kw): e.dma_start(out=o, in_=i, **kw), inc=sem, by=16)

    def barrier(self, bar, scalar_scratch=None):
        self.op("vector", lambda e: e.engine_nop(), inc=bar)
        self.op("gpsimd", lambda e: e.engine_nop(), inc=bar)
        if scalar_scratch is not None:
            self.op("scalar", lambda e, o=scalar_scratch: e.copy(o, o), inc=bar)
        else:
            self.op("scalar", lambda e, h=bar.h: e.sem_inc(h, 1))
            bar.v += 1
        self.op("tensor", lambda e, h=bar.h: e.sem_inc(h, 1))
        self.op("sync", lambda e, h=bar.h: e.sem_inc(h, 1))
        bar.v += 2
        tgt = bar.v
        for eng in ENGINES:
            self.wait(eng, bar, tgt)


class Bank:

    def __init__(self, P, stack, name, shape, dtype):
        self.t = stack.enter_context(P.nc.psum_tensor(name, shape, dtype))
        self.mm = P.sem(name + "_mm")
        self.ev = {}
        self.name = name
        self.P = P

    def _ev(self, eng):
        if eng not in self.ev:
            self.ev[eng] = self.P.sem(self.name + "_ev_" + eng)
        return self.ev[eng]

    def await_free(self, eng="tensor"):
        for s in self.ev.values():
            self.P.wait(eng, s)

    def await_data(self, eng):
        self.P.wait(eng, self.mm)

    def rel(self, eng):
        return self._ev(eng)


def _mm(P, out, lhsT, rhs, start, stop, inc=None):
    return P.op("tensor", lambda e, o=out, l=lhsT, r=rhs, s=start, t=stop:
                e.matmul(o, l, r, start=s, stop=t), inc=inc)


def _tr(P, out, in_, ident, inc=None):
    return P.op("tensor", lambda e, o=out, i=in_, d=ident: e.transpose(o, i, d), inc=inc)


def build(dbg=False):
    nc = bass.Bass("TRN2", target_bir_lowering=False)

    def din(name, shape, dt=F32):
        return nc.dram_tensor(name, shape, dt, kind="ExternalInput").ap()

    def dout(name, shape, dt=F32):
        return nc.dram_tensor(name, shape, dt, kind="ExternalOutput").ap()

    def dint(name, shape, dt=F32):
        k = "ExternalOutput" if dbg else "Internal"
        return nc.dram_tensor(name, shape, dt, kind=k).ap()

    x_d = din("x", [TPC, D])
    wq_d = din("wq_t", [D, D])
    wk_d = din("wk_t", [D, D])
    wv_d = din("wv_t", [D, D])
    wo_d = din("wo_t", [D, D])
    egw1_d = din("egw1_t", [D, 512])
    egb1_d = din("egb1_c", [128, 4])
    egw2_d = din("egw2_c", [128, 4])
    egb2_d = din("egb2_c", [1, 1])
    memgate_d = din("memgate", [1, NH])
    fmem_d = din("fmem", [NM, NH, HD, HD])
    fnorm_d = din("fnorm_c", [128, NH * 5])
    gsg_d = din("gsg_t", [D, 9])
    wg_d = din("wg", [E_MOE, D, I_MOE])
    wu_d = din("wu", [E_MOE, D, I_MOE])
    wd_d = din("wd", [E_MOE, I_MOE, D])
    swg_d = din("swg", [D, I_SH])
    swu_d = din("swu", [D, I_SH])
    swd_d = din("swd", [I_SH, D])
    idb_d = din("ident_bf", [128, 128], BF16)
    idf_d = din("ident_f32", [128, 128])
    mask_d = din("maskT", [128, 128])
    onesrow_d = din("ones_row", [1, 128])
    onescol_d = din("ones_col", [128, 1])
    sel_d = din("sel9", [9, 9 * 128])
    pmask_d = din("pmask", [128, NCORES])

    out_d = dout("out", [TPC, D])

    sq_sp = dint("sq_sp", [NCH, 128, NH * 128], BF16)
    sk_sp = dint("sk_sp", [NCH, 128, NH * 128], BF16)
    v_sp = dint("v_sp", [NCH, 128, NH * 129], BF16)
    oT_sp = dint("oT_sp", [NCH, 128, NH * 128], BF16)
    x1f_sp = dint("x1f_sp", [NCH, 128, D])
    attn_dbg = dout("attn_dbg", [NCH, 128, D]) if dbg else None
    if dbg:
        dbg_n = dout("dbg_n", [128, NH * 5])
        dbg_cm = dout("dbg_cm", [128, NH * 4])
        dbg_crf = dout("dbg_crf", [128, D])
        dbg_af = dout("dbg_af", [128, NH * 640], BF16)
        dbg_S = dout("dbg_S", [128, NH * 129])
        dbg_misc = dout("dbg_misc", [128, 128])
        dbg_sqrl = dout("dbg_sqrl", [128, NH * 128], BF16)
        dbg_cmb = dout("dbg_cmb", [16, 128, 9])
        dbg_lgt = dout("dbg_lgt", [16, 128, 9])
        dbg_x1T = dout("dbg_x1T", [2, 128, NFT * HALF], BF16)
        dbg_x1T2 = dout("dbg_x1T2", [2, 128, NFT * HALF], BF16)
        dbg_lgt2 = dout("dbg_lgt2", [16, 128, 9])
        dbg_lgt3 = dout("dbg_lgt3", [2, 128, 9])
        dbg_acc = dout("dbg_acc", [128, D])
        dbg_fm = dout("dbg_fm", [128, NH * 640], BF16)

    cc_in = nc.dram_tensor("cc_in", [128, NH * 130], F32).ap()
    cc_out = nc.dram_tensor("cc_out", [NCORES * 128, NH * 130], F32).ap()

    root = ExitStack()
    with root as ctx:
        P = Planner(nc, ctx)
        sb = lambda n, s, d=F32: ctx.enter_context(nc.sbuf_tensor(n, s, d))

        idb = sb("idb", [128, 128], BF16)
        idf = sb("idf", [128, 128])
        mask_sb = sb("mask_sb", [128, 128])
        onesrow = sb("onesrow", [1, 128])
        onescol = sb("onescol", [128, 1])
        pmask_sb = sb("pmask_sb", [128, NCORES])
        memgate_sb = sb("memgate_sb", [1, NH])
        egb1_sb = sb("egb1_sb", [128, 4])
        egb2_sb = sb("egb2_sb", [128, 1])
        fnorm_sb = sb("fnorm_sb", [128, NH, 5])
        nrhs_bf = sb("nrhs_bf", [128, NH, 5], BF16)
        S_f32 = sb("S_f32", [128, NH, 129])
        S_bf = sb("S_bf", [128, NH, 129], BF16)
        sq_own = sb("sq_own", [128, NH])
        sqsumg = sb("sqsumg", [128, NH])
        sqsum4 = sb("sqsum4", [128, NH, 4])
        prod_sb = sb("prod_sb", [128, NH, 4])
        gate_cols = sb("gate_cols", [128, NH])
        omgate_cols = sb("omgate_cols", [128, NH])
        coeffbase = sb("coeffbase", [128, NH, 5])
        zalive_col = sb("zalive_col", [128, 1])
        rowscr = sb("rowscr", [1, 512])
        g_col = sb("g_col", [128, 1])
        omg_col = sb("omg_col", [128, 1])
        c_loc = sb("c_loc", [128, 1])
        den_t = sb("den_tmp", [128, 1])
        crcoefs = sb("crcoefs", [128, NH])
        c2cols = sb("c2cols", [128, NH])
        n_all = [sb(f"n_all{i}", [128, NH, 5]) for i in range(2)]
        r_all = sb("r_all", [128, NH, 5])
        coeffm = sb("coeffm", [128, NH, 4])
        ln_m = sb("ln_m", [128, 1])
        ln_v = sb("ln_v", [128, 1])
        ln_r = sb("ln_r", [128, 1])

        ldS = P.sem("ldS")
        splS = P.sem("splS")
        rldS = P.sem("rldS")
        stS = P.sem("stS")
        ldG = P.sem("ldG")
        splG = P.sem("splG")
        ccs = P.sem("ccs")
        bar = P.sem("bar")

        for dst, src in [(idb[:, :], idb_d), (idf[:, :], idf_d), (mask_sb[:, :], mask_d),
                         (onesrow[:, :], onesrow_d), (onescol[:, :], onescol_d),
                         (pmask_sb[:, :], pmask_d),
                         (memgate_sb[:, :], memgate_d), (egb1_sb[:, :], egb1_d),
                         (fnorm_sb[:, :, :], fnorm_d), (rowscr[0:1, 0:1], egb2_d)]:
            P.dma("sync", dst, src, ldS)
        ld0_done = ldS.v

        with ExitStack() as ph1:
            psb = lambda n, s, d=F32: ph1.enter_context(nc.sbuf_tensor(n, s, d))
            xT = psb("xT", [128, NFT, TPC], BF16)
            A = psb("A", [128, NFT, TPC], BF16)
            wsb = psb("wsb", [128, NFT, 1024], BF16)
            xstage = [psb(f"xst{i}", [128, D], BF16) for i in range(2)]
            v_stage = [psb(f"vst{i}", [128, 8, 129], BF16) for i in range(2)]
            skc_st = psb("skc_st", [128, 8, 128], BF16)
            elu_t1 = [psb(f"elu1_{i}", [128, 512]) for i in range(4)]

            B = [Bank(P, ph1, f"ps{i}", [128, 512], F32) for i in range(7)]
            Btr = Bank(P, ph1, "pstr", [128, 1024], BF16)

            x_tgt = [None] * NCH
            x_tgt[0] = P.dma("gpsimd", xstage[0][:, :], x_d[0:128, :], ldG)
            x_tgt[1] = P.dma("gpsimd", xstage[1][:, :], x_d[128:256, :], ldG)
            for t in range(NCH):
                P.wait("tensor", ldG, x_tgt[t])
                for r in range(2):
                    Btr.await_free()
                    for k in range(8):
                        f = r * 8 + k
                        _tr(P, Btr.t[:, k * 128:(k + 1) * 128], xstage[t % 2][:, f * 128:(f + 1) * 128],
                            idb[:, :], inc=(Btr.mm if k == 7 else None))
                    Btr.await_data("scalar")
                    P.op("scalar", lambda e, d=xT[:, r * 8:(r + 1) * 8, t * 128:(t + 1) * 128],
                         s=Btr.t[:, :].rearrange("p (k j) -> p k j", j=128): e.copy(d, s),
                         inc=Btr.rel("scalar"))
                tok = (Btr.mm, Btr.mm.v)
                if t + 2 < NCH:
                    P.waitk("gpsimd", tok)
                    x_tgt[t + 2] = P.dma("gpsimd", xstage[t % 2][:, :],
                                         x_d[(t + 2) * 128:(t + 3) * 128, :], ldG)

            def load_w_half(dram, hw):
                tg = None
                for f in range(NFT):
                    tg = P.dma("gpsimd", wsb[:, f, :],
                               dram[f * 128:(f + 1) * 128, hw * 1024:(hw + 1) * 1024], ldG)
                return tg

            def proj_featmajor(dst, dram, prev_rel):
                P.wait("tensor", Btr.rel("scalar"))
                for hw in range(2):
                    if hw == 0 and prev_rel is not None:
                        P.waitk("gpsimd", prev_rel)
                    w_tgt = load_w_half(dram, hw)
                    for il in range(8):
                        i = hw * 8 + il
                        bs = B[0:4] if (il % 2 == 0) else [B[4], B[5], B[6], B[0]]
                        for b in bs:
                            b.await_free()
                        if il == 0:
                            P.wait("tensor", ldG, w_tgt)
                        for f in range(NFT):
                            for tc in range(4):
                                _mm(P, bs[tc].t[:, :], wsb[:, f, il * 128:(il + 1) * 128],
                                    xT[:, f, tc * 512:(tc + 1) * 512], start=(f == 0),
                                    stop=(f == NFT - 1), inc=(bs[tc].mm if (f == NFT - 1) else None))
                        t1 = None
                        for tc in range(4):
                            b = bs[tc]
                            b.await_data("vector")
                            t1 = P.op("vector", lambda e, o=elu_t1[tc][:, :], s=b.t[:, :]:
                                      e.tensor_single_scalar(o, s, 0.0, ALU.min), inc=True)
                        P.waitk("scalar", t1)
                        t2 = None
                        for tc in range(4):
                            t2 = P.op("scalar", lambda e, o=elu_t1[tc][:, :], s=elu_t1[tc][:, :]:
                                      e.activation(o, s, AF.Exp), inc=True)
                        P.waitk("vector", t2)
                        for tc in range(4):
                            b = bs[tc]
                            dslice = dst[:, i, tc * 512:(tc + 1) * 512]
                            P.op("vector", lambda e, o=dslice, s=b.t[:, :], x2=elu_t1[tc][:, :]:
                                 e.scalar_tensor_tensor(o, s, 0.0, x2, ALU.max, ALU.add),
                                 inc=b.rel("vector"))
                    if hw == 0:
                        lb = B[0]
                        half_rel = (lb.mm, lb.mm.v)
                        P.waitk("gpsimd", half_rel)
                lb = B[0]
                return (lb.mm, lb.mm.v)

            xT_done = P.ax["scalar"].v
            wk_rel = proj_featmajor(A, wk_d, None)
            sk_wr = P.mark("vector")
            P.waitk("tensor", sk_wr)

            P.waitk("sync", sk_wr)
            sk_sp_tgt = [None] * NCH
            for c in range(NCH):
                sk_sp_tgt[c] = P.dma("sync", sk_sp[c], A[:, :, c * 128:(c + 1) * 128], splS)

            for i in range(2):
                P.op("vector", lambda e, o=v_stage[i][:, :, 128:129]: e.memset(o, 1.0))
            vinit = P.op("vector", lambda e, o=S_f32[:, :, :]: e.memset(o, 0.0), inc=True)
            P.waitk("tensor", vinit)
            P.waitk("sync", vinit)

            prev_rel = wk_rel
            last_vspill = {0: None, 1: None}
            for hw in range(2):
                P.waitk("gpsimd", prev_rel)
                wv_tgt = load_w_half(wv_d, hw)
                for t in range(NCH):
                    par = t % 2
                    bs = B[0:2] if (t % 2 == 0) else B[2:4]
                    for b in bs:
                        b.await_free()
                    if t == 0:
                        P.wait("tensor", ldG, wv_tgt)
                    for f in range(NFT):
                        for oc in range(2):
                            _mm(P, bs[oc].t[:, :], xT[:, f, t * 128:(t + 1) * 128],
                                wsb[:, f, oc * 512:(oc + 1) * 512], start=(f == 0),
                                stop=(f == NFT - 1), inc=(bs[oc].mm if f == NFT - 1 else None))
                    if last_vspill[par] is not None:
                        P.wait("vector", splS, last_vspill[par])
                    for oc in range(2):
                        b = bs[oc]
                        b.await_data("vector")
                        P.op("vector", lambda e, o=v_stage[par][:, 4 * oc:4 * oc + 4, 0:128],
                             s=b.t[:, :].rearrange("p (h e) -> p h e", h=4): e.tensor_copy(o, s),
                             inc=b.rel("vector"))
                    vdata = P.mark("vector")
                    Btr.await_free()
                    for k in range(8):
                        h = hw * 8 + k
                        _tr(P, Btr.t[:, k * 128:(k + 1) * 128], A[:, h, t * 128:(t + 1) * 128],
                            idb[:, :], inc=(Btr.mm if k == 7 else None))
                    Btr.await_data("scalar")
                    P.op("scalar", lambda e, d=skc_st[:, :, :],
                         s=Btr.t[:, :].rearrange("p (k j) -> p k j", j=128): e.copy(d, s),
                         inc=Btr.rel("scalar"))
                    sk_tr = Btr.rel("scalar").v
                    for k in range(8):
                        h = hw * 8 + k
                        bd = B[4 + (k % 2)]
                        bd.await_free()
                        if k == 0:
                            P.wait("tensor", Btr.rel("scalar"), sk_tr)
                            P.waitk("tensor", vdata)
                        _mm(P, bd.t[:, 0:129], skc_st[:, k, :], v_stage[par][:, k, :],
                            start=True, stop=True, inc=bd.mm)
                        bd.await_data("vector")
                        P.op("vector", lambda e, o=S_f32[:, h, :], s=bd.t[:, 0:129]:
                             e.scalar_tensor_tensor(o, s, 1.0, o, ALU.mult, ALU.add),
                             inc=bd.rel("vector"))
                    P.waitk("sync", vdata)
                    last_vspill[par] = P.dma("sync", v_sp[t][:, hw * 8 * 129:(hw * 8 + 8) * 129],
                                             v_stage[par][:, :, :], splS)
                prev_rel = (B[5].mm, B[5].mm.v)

            P.wait("vector", splS, sk_sp_tgt[NCH - 1])
            ph1_mm_tok = proj_featmajor(A, wq_d, prev_rel)
            sq_sp_tgt = [None] * NCH
            sq_wr = P.mark("vector")
            P.waitk("sync", sq_wr)
            for c in range(NCH):
                sq_sp_tgt[c] = P.dma("sync", sq_sp[c], A[:, :, c * 128:(c + 1) * 128], splS)
            P.waitk("vector", sq_wr)
            sqs_tok = None
            for h in range(NH):
                sqs_tok = P.op("vector", lambda e, o=sq_own[:, h:h + 1], s=A[:, h, :]:
                               e.reduce_sum(o, s, axis=AX), inc=True)

            P.waitk("gpsimd", sqs_tok)
            cc1 = P.dma("gpsimd", cc_in[:, 0:NH * 129], S_f32[:, :, :], ldG)
            cc1 = P.dma("gpsimd", cc_in[:, NH * 129:NH * 130], sq_own[:, :], ldG)
            P.wait("gpsimd", ldG, cc1)
            P.op("gpsimd", lambda e: e.collective_compute(
                "AllGather", ALU.bypass, replica_groups=[list(range(NCORES))],
                ins=[cc_in.opt()], outs=[cc_out.opt()]), inc=ccs, by=1)
            cc_done = ccs.v

        P.barrier(bar, scalar_scratch=rowscr[0:1, 255:256])
        P.wait("gpsimd", splS, splS.v)
        P.wait("sync", splS, splS.v)

        with ExitStack() as ph2:
            psb = lambda n, s, d=F32: ph2.enter_context(nc.sbuf_tensor(n, s, d))
            fmem_sb = psb("fmem_sb", [128, NH, 640], BF16)
            msum_f = psb("msum_f", [128, NH, 128])
            egw1_sb = psb("egw1_sb", [128, NFT, 512], BF16)
            egw2_sb = psb("egw2_sb", [128, 4], BF16)
            af_sb = [psb(f"af_sb{i}", [128, NH, 640], BF16) for i in range(2)]
            shard_st = [psb(f"shard{i}", [128, NH * 130]) for i in range(2)]
            sq_rl = [psb(f"sq_rl{i}", [128, NH, 128], BF16) for i in range(2)]
            sk_rl = [psb(f"sk_rl{i}", [128, NH, 128], BF16) for i in range(2)]
            v_rl = [psb(f"v_rl{i}", [128, NH, 129], BF16) for i in range(2)]
            skc2 = psb("skc2", [128, NH, 128], BF16)
            msT = psb("msT", [128, NH, 128], BF16)
            den16 = psb("den16", [128, NH])
            c_loc16 = psb("c_loc16", [128, NH])
            num_sb = psb("num_sb", [128, NH, 128])
            crf = [psb(f"crf{i}", [128, D]) for i in range(2)]
            crb = psb("crb", [128, D], BF16)
            crT = psb("crT", [128, NFT, 128], BF16)
            h1_bf = psb("h1_bf", [128, 4, 128], BF16)
            oa = psb("oa", [128, D])
            oa_bf = psb("oa_bf", [128, D], BF16)
            oT = psb("oT", [128, NFT, 128], BF16)

            B2 = [Bank(P, ph2, f"p2_{i}", [128, 512], F32) for i in range(5)]
            Btr2 = Bank(P, ph2, "p2tr", [128, 1024], BF16)
            Baf, Bb, Bsc, Bnum, Bdel = B2
            P.waitk("tensor", ph1_mm_tok)

            egw1_tgt = None
            for f in range(NFT):
                egw1_tgt = P.dma("gpsimd", egw1_sb[:, f, :], egw1_d[f * 128:(f + 1) * 128, :], ldG)
            egw2_tgt = P.dma("gpsimd", egw2_sb[:, :], egw2_d, ldG)
            fm_tgt = None
            for m in range(NM):
                fm_tgt = P.dma("gpsimd", fmem_sb[:, :, m * 128:(m + 1) * 128],
                               fmem_d[m].rearrange("h d e -> d h e"), ldG)
            ms_tgt = None
            for m in range(NM):
                if m > 0:
                    P.wait("gpsimd", ldG, ms_tgt)
                kw = {} if m == 0 else {"accum_op": ALU.add}
                ms_tgt = P.dma("gpsimd", msum_f[:, :, :], fmem_d[m].rearrange("h d e -> d h e"), ldG, **kw)
            P.wait("vector", ldG, ms_tgt)
            msum_tok = P.op("vector", lambda e, o=fmem_sb[:, :, 512:640], s=msum_f[:, :, :]:
                            e.tensor_copy(o, s), inc=True)

            P.wait("vector", ldS, ld0_done)
            P.opw("vector", lambda e, o=fnorm_sb[:, :, 4], a=fnorm_sb[:, :, 0], b=fnorm_sb[:, :, 1]:
                  e.tensor_tensor(o, a, b, ALU.add))
            P.opw("vector", lambda e, o=r_all[:, :, 0], a=fnorm_sb[:, :, 2], b=fnorm_sb[:, :, 3]:
                  e.tensor_tensor(o, a, b, ALU.add))
            P.opw("vector", lambda e, o=fnorm_sb[:, :, 4], a=fnorm_sb[:, :, 4], b=r_all[:, :, 0]:
                  e.tensor_tensor(o, a, b, ALU.add))
            nrhs_tok = P.op("vector", lambda e, o=nrhs_bf[:, :, :], s=fnorm_sb[:, :, :]:
                            e.tensor_copy(o, s), inc=True)
            dsum_tok = None
            for m in range(NM):
                dsum_tok = P.op("vector", lambda e, o=sqsum4[:, 0, m:m + 1], s=fnorm_sb[:, :, m]:
                                e.reduce_sum(o, s, axis=AX), inc=True)
            P.waitk("vector", dsum_tok)
            Bsc.await_free()
            P.waitk("tensor", dsum_tok)
            _mm(P, Bsc.t[0:1, 0:4], onescol[:, :], sqsum4[:, 0, :], True, True, inc=Bsc.mm)
            Bsc.await_data("vector")
            P.opw("vector", lambda e, o=rowscr[0:1, 16:20], s=Bsc.t[0:1, 0:4]: e.tensor_copy(o, s))
            P.op("vector", lambda e: e.engine_nop(), inc=Bsc.rel("vector"))
            P.opw("vector", lambda e, o=rowscr[0:1, 20:24], s=rowscr[0:1, 16:20]:
                  e.tensor_single_scalar(o, s, EPS, ALU.is_ge))
            P.opw("vector", lambda e, o=rowscr[0:1, 24:25], s=rowscr[0:1, 16:20]:
                  e.reduce_sum(o, s, axis=AX))
            za_tok = P.opw("vector", lambda e, o=rowscr[0:1, 25:26], s=rowscr[0:1, 24:25]:
                           e.tensor_single_scalar(o, s, EPS, ALU.is_ge))
            P.wait("scalar", ldS, ld0_done)
            gr_tok = P.op("scalar", lambda e, o=rowscr[0:1, 32:48], s=memgate_sb[0:1, :]:
                          e.activation(o, s, AF.Sigmoid), inc=True)
            Bnum.await_free()
            P.waitk("tensor", gr_tok)
            P.waitk("tensor", za_tok)
            P.wait("tensor", ldS, ld0_done)
            _mm(P, Bnum.t[:, 0:16], onesrow[0:1, :], rowscr[0:1, 32:48], True, True)
            _mm(P, Bnum.t[:, 16:17], onesrow[0:1, :], rowscr[0:1, 25:26], True, True)
            _mm(P, Bnum.t[:, 17:18], onesrow[0:1, :], rowscr[0:1, 0:1], True, True, inc=Bnum.mm)
            Bnum.await_data("vector")
            P.op("vector", lambda e, o=gate_cols[:, :], s=Bnum.t[:, 0:16]: e.tensor_copy(o, s))
            P.op("vector", lambda e, o=zalive_col[:, :], s=Bnum.t[:, 16:17]: e.tensor_copy(o, s))
            P.opw("vector", lambda e, o=egb2_sb[:, :], s=Bnum.t[:, 17:18]: e.tensor_copy(o, s))
            P.op("vector", lambda e: e.engine_nop(), inc=Bnum.rel("vector"))
            P.op("vector", lambda e, o=omgate_cols[:, :], s=gate_cols[:, :]:
                 e.tensor_scalar(o, s, -1.0, 1.0, ALU.mult, ALU.add))

            P.wait("gpsimd", ccs, cc_done)
            sh_tgt = [None] * NCORES
            sh_rel = [None] * NCORES
            sh_tgt[0] = P.dma("gpsimd", shard_st[0][:, :], cc_out[0:128, :], ldG)
            sh_tgt[1] = P.dma("gpsimd", shard_st[1][:, :], cc_out[128:256, :], ldG)
            for r in range(NCORES):
                P.wait("vector", ldG, sh_tgt[r])
                ss = shard_st[r % 2]
                P.op("vector", lambda e, o=S_f32[:, :, :],
                     s=ss[:, 0:NH * 129].rearrange("p (h c) -> p h c", h=NH),
                     m=pmask_sb[:, r:r + 1], op1=(ALU.bypass if r == 0 else ALU.add):
                     e.scalar_tensor_tensor(o, s, m, o, ALU.mult, op1))
                if r == 0:
                    sh_rel[r] = P.opw("vector", lambda e, o=sqsumg[:, :], s=ss[:, NH * 129:NH * 130]:
                                      e.tensor_copy(o, s))
                else:
                    sh_rel[r] = P.opw("vector", lambda e, o=sqsumg[:, :], s=ss[:, NH * 129:NH * 130],
                                      b=sqsumg[:, :]: e.tensor_tensor(o, b, s, ALU.add))
                if r + 2 < NCORES:
                    P.waitk("gpsimd", sh_rel[r])
                    sh_tgt[r + 2] = P.dma("gpsimd", shard_st[r % 2][:, :],
                                          cc_out[(r + 2) * 128:(r + 3) * 128, :], ldG)
            sbf_init = P.op("vector", lambda e, o=S_bf[:, :, :], s=S_f32[:, :, :]: e.tensor_copy(o, s),
                            inc=True)

            P.waitk("vector", sh_rel[NCORES - 1])
            for m in range(NM):
                P.op("vector", lambda e, o=sqsum4[:, :, m], s=sqsumg[:, :]: e.tensor_copy(o, s))
            P.opw("vector", lambda e: e.engine_nop())
            P.opw("vector", lambda e, o=prod_sb[:, :, :], a=fnorm_sb[:, :, 0:4], b=sqsum4[:, :, :]:
                  e.tensor_tensor(o, a, b, ALU.mult))
            prod_tok = P.opw("vector", lambda e, o=prod_sb[:, :, :], s=prod_sb[:, :, :]:
                             e.tensor_single_scalar(o, s, REL_SCALE, ALU.mult))
            Bsc.await_free()
            P.waitk("tensor", prod_tok)
            _mm(P, Bsc.t[0:1, 0:64], onescol[:, :], prod_sb[:, :, :], True, True, inc=Bsc.mm)
            Bsc.await_data("vector")
            P.opw("vector", lambda e, o=rowscr[0:1, 64:128], s=Bsc.t[0:1, 0:64]: e.tensor_copy(o, s))
            P.op("vector", lambda e: e.engine_nop(), inc=Bsc.rel("vector"))
            rel = rowscr[0:1, 64:128].rearrange("p (h m) -> p h m", m=4)
            mxe = rowscr[0:1, 160:224].rearrange("p (h m) -> p h m", m=4)
            P.opw("vector", lambda e, o=rowscr[0:1, 128:144], a=rel[:, :, 0], b=rel[:, :, 1]:
                  e.tensor_tensor(o, a, b, ALU.max))
            P.opw("vector", lambda e, o=rowscr[0:1, 144:160], a=rel[:, :, 2], b=rel[:, :, 3]:
                  e.tensor_tensor(o, a, b, ALU.max))
            P.opw("vector", lambda e, o=rowscr[0:1, 128:144], a=rowscr[0:1, 128:144], b=rowscr[0:1, 144:160]:
                  e.tensor_tensor(o, a, b, ALU.max))
            for m in range(NM):
                P.op("vector", lambda e, o=mxe[:, :, m], s=rowscr[0:1, 128:144]: e.tensor_copy(o, s))
            P.opw("vector", lambda e: e.engine_nop())
            sm_tok = P.opw("vector", lambda e, o=rowscr[0:1, 64:128], a=rowscr[0:1, 64:128],
                           b=rowscr[0:1, 160:224]: e.tensor_tensor(o, a, b, ALU.subtract))
            P.waitk("scalar", sm_tok)
            ex_tok = P.op("scalar", lambda e, o=rowscr[0:1, 64:128], s=rowscr[0:1, 64:128]:
                          e.activation(o, s, AF.Exp), inc=True)
            P.waitk("vector", ex_tok)
            P.opw("vector", lambda e, o=rowscr[0:1, 128:144], a=rel[:, :, 0], b=rel[:, :, 1]:
                  e.tensor_tensor(o, a, b, ALU.add))
            P.opw("vector", lambda e, o=rowscr[0:1, 144:160], a=rel[:, :, 2], b=rel[:, :, 3]:
                  e.tensor_tensor(o, a, b, ALU.add))
            P.opw("vector", lambda e, o=rowscr[0:1, 128:144], a=rowscr[0:1, 128:144], b=rowscr[0:1, 144:160]:
                  e.tensor_tensor(o, a, b, ALU.add))
            P.opw("vector", lambda e, o=rowscr[0:1, 144:160], s=rowscr[0:1, 128:144]: e.reciprocal(o, s))
            for m in range(NM):
                P.op("vector", lambda e, o=mxe[:, :, m], s=rowscr[0:1, 144:160]: e.tensor_copy(o, s))
            P.opw("vector", lambda e: e.engine_nop())
            P.opw("vector", lambda e, o=rowscr[0:1, 64:128], a=rowscr[0:1, 64:128], b=rowscr[0:1, 160:224]:
                  e.tensor_tensor(o, a, b, ALU.mult))
            for m in range(NM):
                P.opw("vector", lambda e, o=rel[:, :, m], s=rel[:, :, m], a=rowscr[0:1, 20 + m:21 + m]:
                      e.tensor_scalar(o, s, a, None, ALU.mult))
            wrow_tok = P.mark("vector")
            Bsc.await_free()
            P.waitk("tensor", wrow_tok)
            _mm(P, Bsc.t[:, 0:64], onesrow[0:1, :], rowscr[0:1, 64:128], True, True, inc=Bsc.mm)
            Bsc.await_data("vector")
            P.opw("vector", lambda e, o=coeffm[:, :, :], s=Bsc.t[:, 0:64].rearrange("p (h m) -> p h m", m=4):
                  e.tensor_copy(o, s))
            P.op("vector", lambda e: e.engine_nop(), inc=Bsc.rel("vector"))
            for h in range(NH):
                P.op("vector", lambda e, o=coeffbase[:, h, 0:4], s=coeffm[:, h, :], a=gate_cols[:, h:h + 1]:
                     e.tensor_scalar(o, s, a, None, ALU.mult))
            cbase_tok = P.opw("vector", lambda e, o=coeffbase[:, :, 4], s=gate_cols[:, :], a=zalive_col[:, :]:
                              e.tensor_scalar(o, s, a, None, ALU.mult))

            sqr_tgt = [None] * NCH
            skr_tgt = [None] * NCH
            vr_tgt = [None] * NCH
            rl_rel = [None] * NCH
            pool_rel = [None] * NCH
            dbg_oa_tgt = [None] * NCH
            af_sc_tok = [None] * NCH
            dbg_gate = [None]
            oab_rel = [None] * NCH
            oT_sp_tgt = [None] * NCH
            sbf_tok = [[None] * NH for _ in range(NCH)]

            def plan_reload(c):
                p = c % 2
                if c >= 2:
                    P.waitk("sync", rl_rel[c - 2])
                sqr_tgt[c] = P.dma("sync", sq_rl[p][:, :, :], sq_sp[c], rldS)
                skr_tgt[c] = P.dma("sync", sk_rl[p][:, :, :], sk_sp[c], rldS)
                vr_tgt[c] = P.dma("sync", v_rl[p][:, :, :], v_sp[c], rldS)

            plan_reload(0)
            plan_reload(1)

            def emit_stage1(c):
                p = c % 2
                P.wait("tensor", rldS, sqr_tgt[c])
                if c == 0:
                    P.wait("tensor", ldG, fm_tgt)
                    P.waitk("tensor", msum_tok)
                    P.waitk("tensor", nrhs_tok)
                if c >= 2:
                    P.waitk("scalar", pool_rel[c - 2])
                    P.waitk("vector", pool_rel[c - 2])
                for h in range(NH):
                    Baf.await_free()
                    Bb.await_free()
                    lh = sq_rl[p][:, h, :]
                    _mm(P, Baf.t[:, :], lh, fmem_sb[:, h, 0:512], True, True, inc=Baf.mm)
                    _mm(P, Bb.t[:, 0:128], lh, fmem_sb[:, h, 512:640], True, True)
                    _mm(P, Bb.t[:, 128:133], lh, nrhs_bf[:, h, :], True, True, inc=Bb.mm)
                    Baf.await_data("scalar")
                    P.op("scalar", lambda e, o=af_sb[p][:, h, 0:512], s=Baf.t[:, :]: e.copy(o, s),
                         inc=Baf.rel("scalar"))
                    Bb.await_data("scalar")
                    P.op("scalar", lambda e, o=af_sb[p][:, h, 512:640], s=Bb.t[:, 0:128]: e.copy(o, s),
                         inc=Bb.rel("scalar"))
                    Bb.await_data("vector")
                    tok_n = P.op("vector", lambda e, o=n_all[p][:, h, :], s=Bb.t[:, 128:133]:
                                 e.tensor_copy(o, s), inc=Bb.rel("vector"))
                    if h == NH - 1:
                        P.wait("vector", Bb.rel("vector"), tok_n)
                af_sc_tok[c] = (Bb.rel("scalar"), Bb.rel("scalar").v)

            emit_stage1(0)
            for c in range(NCH):
                p = c % 2
                P.opw("vector", lambda e, o=n_all[p][:, :, :], s=n_all[p][:, :, :]:
                      e.tensor_single_scalar(o, s, EPS, ALU.max))
                P.opw("vector", lambda e, o=r_all[:, :, :], s=n_all[p][:, :, :]: e.reciprocal(o, s))
                if c == 0:
                    P.waitk("vector", cbase_tok)
                crc_tok = P.opw("vector", lambda e, o=crcoefs[:, :], s=r_all[:, :, 4], a=zalive_col[:, :]:
                                e.tensor_scalar(o, s, a, None, ALU.mult))
                P.waitk("vector", af_sc_tok[c])
                cr_tok = None
                for h in range(NH):
                    cr_tok = P.op("vector", lambda e, o=crf[p][:, h * 128:(h + 1) * 128],
                                  s=af_sb[p][:, h, 512:640], a=crcoefs[:, h:h + 1]:
                                  e.scalar_tensor_tensor(o, s, a, o, ALU.mult, ALU.bypass), inc=True)
                P.waitk("scalar", cr_tok)
                crb_tok = P.op("scalar", lambda e, o=crb[:, :], s=crf[p][:, :]: e.copy(o, s), inc=True)
                P.wait("tensor", rldS, skr_tgt[c])
                P.wait("tensor", rldS, vr_tgt[c])
                if c > 0:
                    P.waitk("scalar", pool_rel[c - 1])
                for r in range(2):
                    Btr2.await_free()
                    for k in range(8):
                        h = r * 8 + k
                        _tr(P, Btr2.t[:, k * 128:(k + 1) * 128], sk_rl[p][:, h, :], idb[:, :],
                            inc=(Btr2.mm if k == 7 else None))
                    Btr2.await_data("scalar")
                    P.op("scalar", lambda e, d=skc2[:, r * 8:(r + 1) * 8, :],
                         s=Btr2.t[:, :].rearrange("p (k j) -> p k j", j=128): e.copy(d, s),
                         inc=Btr2.rel("scalar"))
                skc2_tgt = Btr2.rel("scalar").v
                P.wait("tensor", Btr2.rel("scalar"), skc2_tgt)
                for h in range(NH):
                    bs_ = Bsc if h % 2 == 0 else Baf
                    bs_.await_free()
                    _mm(P, bs_.t[:, 0:128], sk_rl[p][:, h, :], sq_rl[p][:, h, :], True, True, inc=bs_.mm)
                    bs_.await_data("vector")
                    P.op("vector", lambda e, o=msT[:, h, :], a=bs_.t[:, 0:128], b=mask_sb[:, :]:
                         e.tensor_tensor(o, a, b, ALU.mult), inc=bs_.rel("vector"))
                ms_all = P.mark("vector")
                P.waitk("tensor", ms_all)
                num_tok = None
                for h in range(NH):
                    bn_ = Bnum if h % 2 == 0 else Bb
                    bd_ = Bdel if h % 2 == 0 else Bsc
                    bn_.await_free()
                    if c > 0:
                        P.waitk("tensor", sbf_tok[c - 1][h])
                    elif h == 0:
                        P.waitk("tensor", sbf_init)
                    _mm(P, bn_.t[:, 0:129], msT[:, h, :], v_rl[p][:, h, :], True, False)
                    _mm(P, bn_.t[:, 0:129], sq_rl[p][:, h, :], S_bf[:, h, :], False, True, inc=bn_.mm)
                    bn_.await_data("scalar")
                    P.op("scalar", lambda e, o=den16[:, h:h + 1], s=bn_.t[:, 128:129]:
                         e.copy(o, s))
                    num_tok = P.op("scalar", lambda e, o=num_sb[:, h, :], s=bn_.t[:, 0:128]:
                                   e.copy(o, s), inc=bn_.rel("scalar"))
                    nt_sem = bn_.rel("scalar")
                    bd_.await_free()
                    _mm(P, bd_.t[:, 0:129], skc2[:, h, :], v_rl[p][:, h, :], True, True, inc=bd_.mm)
                    bd_.await_data("vector")
                    su_tok = P.op("vector", lambda e, o=S_f32[:, h, :], s=bd_.t[:, 0:129]:
                                  e.scalar_tensor_tensor(o, s, 1.0, o, ALU.mult, ALU.add),
                                  inc=bd_.rel("vector"))
                    P.wait("scalar", bd_.rel("vector"), su_tok)
                    sbf_tok[c][h] = P.op("scalar", lambda e, o=S_bf[:, h, :], s=S_f32[:, h, :]:
                                         e.copy(o, s), inc=True)
                rl_rel[c] = (Bdel.mm, Bdel.mm.v)
                P.wait("vector", nt_sem, num_tok)
                P.opw("vector", lambda e: e.engine_nop())
                P.opw("vector", lambda e, o=den16[:, :], s=den16[:, :]:
                      e.tensor_single_scalar(o, s, EPS, ALU.max))
                P.opw("vector", lambda e, o=c_loc16[:, :], s=den16[:, :]: e.reciprocal(o, s))
                P.opw("vector", lambda e, o=c_loc16[:, :], a=c_loc16[:, :], b=omgate_cols[:, :]:
                      e.tensor_tensor(o, a, b, ALU.mult))

                if c + 1 < NCH:
                    emit_stage1(c + 1)

                P.waitk("tensor", crb_tok)
                for r in range(2):
                    Btr2.await_free()
                    for k in range(8):
                        f = r * 8 + k
                        _tr(P, Btr2.t[:, k * 128:(k + 1) * 128], crb[:, f * 128:(f + 1) * 128], idb[:, :],
                            inc=(Btr2.mm if k == 7 else None))
                    Btr2.await_data("scalar")
                    P.op("scalar", lambda e, d=crT[:, r * 8:(r + 1) * 8, :],
                         s=Btr2.t[:, :].rearrange("p (k j) -> p k j", j=128): e.copy(d, s),
                         inc=Btr2.rel("scalar"))
                crT_tgt = Btr2.rel("scalar").v
                Bsc.await_free()
                P.wait("tensor", Btr2.rel("scalar"), crT_tgt)
                if c == 0:
                    P.wait("tensor", ldG, egw1_tgt)
                    P.wait("tensor", ldG, egw2_tgt)
                for ut in range(4):
                    for f in range(NFT):
                        _mm(P, Bsc.t[:, ut * 128:(ut + 1) * 128], egw1_sb[:, f, ut * 128:(ut + 1) * 128],
                            crT[:, f, :], start=(f == 0), stop=(f == NFT - 1),
                            inc=(Bsc.mm if (ut == 3 and f == NFT - 1) else None))
                Bsc.await_data("scalar")
                for ut in range(4):
                    P.op("scalar", lambda e, o=h1_bf[:, ut, :], s=Bsc.t[:, ut * 128:(ut + 1) * 128],
                         b=egb1_sb[:, ut:ut + 1]: e.activation(o, s, AF.Relu, bias=b),
                         inc=(Bsc.rel("scalar") if ut == 3 else None))
                Bnum.await_free()
                P.wait("tensor", Bsc.rel("scalar"))
                for ut in range(4):
                    _mm(P, Bnum.t[:, 0:1], h1_bf[:, ut, :], egw2_sb[:, ut:ut + 1],
                        start=(ut == 0), stop=(ut == 3), inc=(Bnum.mm if ut == 3 else None))
                Bnum.await_data("scalar")
                gc_tok = P.op("scalar", lambda e, o=g_col[:, :], s=Bnum.t[:, 0:1], b=egb2_sb[:, :]:
                              e.activation(o, s, AF.Sigmoid, bias=b), inc=Bnum.rel("scalar"))
                P.wait("vector", Bnum.rel("scalar"), gc_tok)
                omg_tok = P.opw("vector", lambda e, o=omg_col[:, :], s=g_col[:, :]:
                                e.tensor_scalar(o, s, -1.0, 1.0, ALU.mult, ALU.add))
                P.opw("vector", lambda e, o=coeffm[:, :, :], a=r_all[:, :, 0:4], b=coeffbase[:, :, 0:4]:
                      e.tensor_tensor(o, a, b, ALU.mult))
                P.opw("vector", lambda e, o=coeffm[:, :, :], s=coeffm[:, :, :], a=g_col[:, :]:
                      e.tensor_scalar(o, s, a, None, ALU.mult))
                cm_tok = P.opw("vector", lambda e, o=c2cols[:, :], s=gate_cols[:, :], a=omg_col[:, :]:
                               e.tensor_scalar(o, s, a, None, ALU.mult))

                if c > 0:
                    P.waitk("vector", oab_rel[c - 1])
                    if dbg and dbg_oa_tgt[c - 1]:
                        P.wait("vector", splS, dbg_oa_tgt[c - 1])
                for h in range(NH):
                    P.op("vector", lambda e, o=oa[:, h * 128:(h + 1) * 128], s=num_sb[:, h, :],
                         a=c_loc16[:, h:h + 1]: e.scalar_tensor_tensor(o, s, a, o, ALU.mult, ALU.bypass))
                for m in range(NM):
                    for h in range(NH):
                        P.op("vector", lambda e, o=oa[:, h * 128:(h + 1) * 128],
                             s=af_sb[p][:, h, m * 128:(m + 1) * 128], a=coeffm[:, h, m:m + 1]:
                             e.scalar_tensor_tensor(o, s, a, o, ALU.mult, ALU.add))
                for h in range(NH):
                    P.op("vector", lambda e, o=oa[:, h * 128:(h + 1) * 128],
                         s=crf[p][:, h * 128:(h + 1) * 128],
                         a=c2cols[:, h:h + 1]: e.scalar_tensor_tensor(o, s, a, o, ALU.mult, ALU.add),
                         inc=(True if h == NH - 1 else None))
                pool_rel[c] = (P.ax["vector"], P.ax["vector"].v)

                P.waitk("scalar", pool_rel[c])
                oab_tok = P.op("scalar", lambda e, o=oa_bf[:, :], s=oa[:, :]: e.copy(o, s), inc=True)
                oab_rel[c] = oab_tok
                P.waitk("tensor", oab_tok)
                if c >= 1:
                    P.wait("scalar", splS, oT_sp_tgt[c - 1])
                for r in range(2):
                    Btr2.await_free()
                    for k in range(8):
                        f = r * 8 + k
                        _tr(P, Btr2.t[:, k * 128:(k + 1) * 128], oa_bf[:, f * 128:(f + 1) * 128], idb[:, :],
                            inc=(Btr2.mm if k == 7 else None))
                    Btr2.await_data("scalar")
                    P.op("scalar", lambda e, d=oT[:, r * 8:(r + 1) * 8, :],
                         s=Btr2.t[:, :].rearrange("p (k j) -> p k j", j=128): e.copy(d, s),
                         inc=Btr2.rel("scalar"))
                oT_done = Btr2.rel("scalar").v
                P.wait("sync", Btr2.rel("scalar"), oT_done)
                oT_sp_tgt[c] = P.dma("sync", oT_sp[c], oT[:, :, :], splS)
                if dbg:
                    P.waitk("sync", pool_rel[c])
                    dbg_oa_tgt[c] = P.dma("sync", attn_dbg[c], oa[:, :], splS)
                if c + 2 < NCH:
                    plan_reload(c + 2)

        P.barrier(bar, scalar_scratch=rowscr[0:1, 255:256])
        P.wait("gpsimd", splS, splS.v)
        P.wait("sync", splS, splS.v)

        with ExitStack() as ph2b:
            psb = lambda n, s, d=F32: ph2b.enter_context(nc.sbuf_tensor(n, s, d))
            wo_sb = psb("wo_sb", [128, NFT, D], BF16)
            oT_rl = [psb(f"oT_rl{i}", [128, NFT, 128], BF16) for i in range(2)]
            x_res = [psb(f"x_res{i}", [128, D]) for i in range(2)]
            ln_in = psb("ln_in", [128, D])
            ln_sc = psb("ln_sc", [128, D])
            x1f = psb("x1f", [128, D])
            Bw = [Bank(P, ph2b, f"pw{i}", [128, 512], F32) for i in range(8)]
            P.wait("tensor", Btr2.mm)
            for _b in B2 + [Btr2]:
                for _s in _b.ev.values():
                    P.wait("tensor", _s)

            wo_tgt = None
            for f in range(NFT):
                wo_tgt = P.dma("gpsimd", wo_sb[:, f, :], wo_d[f * 128:(f + 1) * 128, :], ldG)

            oTr_tgt = [None] * NCH
            xr_tgt = [None] * NCH
            oTr_rel = [None] * NCH
            xres_rel = [None] * NCH
            x1fS_tgt = [None] * NCH

            def plan_reload2(c):
                p = c % 2
                if c >= 2:
                    P.waitk("sync", oTr_rel[c - 2])
                oTr_tgt[c] = P.dma("sync", oT_rl[p][:, :, :], oT_sp[c], rldS)
                if c >= 2:
                    P.wait("sync", xres_rel[c - 2][0], xres_rel[c - 2][1])
                xr_tgt[c] = P.dma("sync", x_res[p][:, :], x_d[c * 128:(c + 1) * 128, :], rldS)

            plan_reload2(0)
            plan_reload2(1)
            for c in range(NCH):
                p = c % 2
                bwg = Bw[0:4] if c % 2 == 0 else Bw[4:8]
                for b in bwg:
                    b.await_free()
                P.wait("tensor", rldS, oTr_tgt[c])
                if c == 0:
                    P.wait("tensor", ldG, wo_tgt)
                for f in range(NFT):
                    for oc in range(4):
                        _mm(P, bwg[oc].t[:, :], oT_rl[p][:, f, :], wo_sb[:, f, oc * 512:(oc + 1) * 512],
                            start=(f == 0), stop=(f == NFT - 1), inc=(bwg[oc].mm if f == NFT - 1 else None))
                oTr_rel[c] = (bwg[3].mm, bwg[3].mm.v)
                P.wait("vector", rldS, xr_tgt[c])
                for oc in range(4):
                    bwg[oc].await_data("vector")
                    P.op("vector", lambda e, o=ln_in[:, oc * 512:(oc + 1) * 512], a=bwg[oc].t[:, :],
                         b=x_res[p][:, oc * 512:(oc + 1) * 512]: e.tensor_tensor(o, a, b, ALU.add),
                         inc=bwg[oc].rel("vector"))
                xres_rel[c] = (bwg[3].rel("vector"), bwg[3].rel("vector").v)
                P.opw("vector", lambda e: e.engine_nop())
                P.opw("vector", lambda e, o=ln_m[:, :], s=ln_in[:, :]: e.reduce_sum(o, s, axis=AX))
                P.opw("vector", lambda e, o=ln_m[:, :], s=ln_m[:, :]:
                      e.tensor_single_scalar(o, s, 1.0 / D, ALU.mult))
                xm_tok = P.opw("vector", lambda e, o=ln_sc[:, :], s=ln_in[:, :], a=ln_m[:, :]:
                               e.tensor_scalar(o, s, a, None, ALU.subtract))
                P.waitk("scalar", xm_tok)
                sq_tok = P.op("scalar", lambda e, o=ln_in[:, :], s=ln_sc[:, :], acc=ln_v[:, :]:
                              e.activation(o, s, AF.Square, accum_out=acc), inc=True)
                P.waitk("vector", sq_tok)
                P.opw("vector", lambda e, o=ln_v[:, :], s=ln_v[:, :]:
                      e.tensor_single_scalar(o, s, 1.0 / D, ALU.mult))
                P.opw("vector", lambda e, o=ln_v[:, :], s=ln_v[:, :]:
                      e.tensor_single_scalar(o, s, LN_EPS, ALU.add))
                rv_tok = P.opw("vector", lambda e, o=ln_r[:, :], s=ln_v[:, :]: e.reciprocal(o, s))
                P.waitk("scalar", rv_tok)
                rs_tok = P.op("scalar", lambda e, o=ln_r[:, :], s=ln_r[:, :]: e.activation(o, s, AF.Sqrt),
                              inc=True)
                P.waitk("vector", rs_tok)
                if c >= 1:
                    P.wait("vector", splS, x1fS_tgt[c - 1])
                x1_tok = P.op("vector", lambda e, o=x1f[:, :], s=ln_sc[:, :], a=ln_r[:, :]:
                              e.tensor_scalar(o, s, a, None, ALU.mult), inc=True)
                P.waitk("sync", x1_tok)
                x1fS_tgt[c] = P.dma("sync", x1f_sp[c], x1f[:, :], splS)
                if c + 2 < NCH:
                    plan_reload2(c + 2)

        P.barrier(bar, scalar_scratch=rowscr[0:1, 255:256])
        P.wait("gpsimd", splS, splS.v)
        P.wait("gpsimd", splG, splG.v)
        P.wait("sync", splS, splS.v)
        P.wait("sync", splG, splG.v)

        with ExitStack() as ph3:
            psb = lambda n, s, d=F32: ph3.enter_context(nc.sbuf_tensor(n, s, d))
            x1T = psb("x1T", [128, NFT, HALF], BF16)
            acc = psb("acc", [128, 8, D])
            cb = psb("cb", [128, 9, HALF], BF16)
            gs = psb("gs", [128, 4, HALF], BF16)
            u_tmp = psb("u_tmp", [128, 2, HALF], BF16)
            wbuf = [psb(f"wbuf{i}", [128, 8192], BF16) for i in range(2)]
            gsg2 = psb("gsg2", [128, 9])
            comb = psb("comb", [128, 9])
            comb_rT = psb("comb_rT", [9, 8, 128])
            l2_in = psb("l2sc", [128, D])
            sel_sb = psb("sel_sb", [9, 9, 128])
            gsg_sb = psb("gsg_sb", [128, NFT, 9], BF16)
            gsg2b = psb("gsg2b", [128, 9])
            sel_tgt = P.dma("sync", sel_sb[:, :, :], sel_d, rldS)
            gsg_tgt = None
            for f in range(NFT):
                gsg_tgt = P.dma("gpsimd", gsg_sb[:, f, :], gsg_d[f * 128:(f + 1) * 128, :], ldG)

            with ExitStack() as ph3p:
                Bp = [Bank(P, ph3p, f"p3_{i}", [128, 512], F32) for i in range(8)]
                P.wait("tensor", Bw[7].mm)
                for _b in Bw:
                    for _s in _b.ev.values():
                        P.wait("tensor", _s)


                xb_rel = [None] * 8
                acc_tgt = [None] * 8
                dbg_cmb_tgt = [None] * 16

                def emit_half_prep(half):
                    if half == 1:
                        P.wait("sync", stS, stS.v)
                    for tt in range(8):
                        acc_tgt[tt] = P.dma("sync", acc[:, tt, :], x1f_sp[half * 8 + tt], rldS)
                    for tt in range(8):
                        P.wait("tensor", rldS, acc_tgt[tt])
                        for r in range(4):
                            Bt = Bp[2 + (r % 2)]
                            Bt.await_free()
                            for k in range(4):
                                f = r * 4 + k
                                _tr(P, Bt.t[:, k * 128:(k + 1) * 128],
                                    acc[:, tt, f * 128:(f + 1) * 128], idf[:, :],
                                    inc=(Bt.mm if k == 3 else None))
                            Bt.await_data("scalar")
                            P.op("scalar", lambda e, d=x1T[:, r * 4:(r + 1) * 4, tt * 128:(tt + 1) * 128],
                                 s=Bt.t[:, :].rearrange("p (k j) -> p k j", j=128): e.copy(d, s),
                                 inc=Bt.rel("scalar"))
                    x1T_tgt_a = Bp[2].rel("scalar").v
                    x1T_tgt_b = Bp[3].rel("scalar").v
                    P.wait("tensor", Bp[2].rel("scalar"), x1T_tgt_a)
                    P.wait("tensor", Bp[3].rel("scalar"), x1T_tgt_b)
                    if dbg:
                        P.wait("sync", Bp[2].rel("scalar"), x1T_tgt_a)
                        P.wait("sync", Bp[3].rel("scalar"), x1T_tgt_b)
                        x1T_dump = P.dma("sync", dbg_x1T[half], x1T[:, :, :], rldS)
                        P.wait("tensor", rldS, x1T_dump)

                    for tt in range(8):
                        Bg = Bp[tt % 2]
                        Bg.await_free()
                        if tt == 0:
                            P.wait("tensor", ldG, gsg_tgt)
                        for f in range(NFT):
                            _mm(P, Bg.t[:, 0:9], x1T[:, f, tt * 128:(tt + 1) * 128], gsg_sb[:, f, :],
                                start=(f == 0), stop=(f == NFT - 1), inc=(Bg.mm if f == NFT - 1 else None))
                        Bg.await_data("vector")
                        lg_tok = P.opw("vector", lambda e, o=gsg2[:, :], s=Bg.t[:, 0:9]: e.tensor_copy(o, s))
                        if dbg:
                            Bg.await_data("scalar")
                            lg2_tok = P.op("scalar", lambda e, o=gsg2b[:, :], s=Bg.t[:, 0:9]:
                                           e.copy(o, s), inc=True)
                            P.waitk("sync", lg2_tok)
                            lgt2_t = P.dma("sync", dbg_lgt2[half * 8 + tt], gsg2b[:, :], rldS)
                            P.wait("vector", rldS, lgt2_t)
                            P.wait("scalar", rldS, lgt2_t)
                        P.op("vector", lambda e: e.engine_nop(), inc=Bg.rel("vector"))
                        if dbg:
                            P.waitk("sync", lg_tok)
                            lgt_t = P.dma("sync", dbg_lgt[half * 8 + tt], gsg2[:, :], rldS)
                            P.wait("vector", rldS, lgt_t)
                        P.opw("vector", lambda e, o=ln_m[:, :], s=gsg2[:, 0:8]: e.reduce_max(o, s, axis=AX))
                        sm1 = P.opw("vector", lambda e, o=comb[:, 0:8], s=gsg2[:, 0:8], a=ln_m[:, :]:
                                    e.tensor_scalar(o, s, a, None, ALU.subtract))
                        P.waitk("scalar", sm1)
                        sm2 = P.op("scalar", lambda e, o=comb[:, 0:8], s=comb[:, 0:8]:
                                   e.activation(o, s, AF.Exp), inc=True)
                        sg1 = P.op("scalar", lambda e, o=comb[:, 8:9], s=gsg2[:, 8:9]:
                                   e.activation(o, s, AF.Sigmoid), inc=True)
                        P.waitk("vector", sg1)
                        P.opw("vector", lambda e, o=ln_m[:, :], s=comb[:, 0:8]: e.reduce_sum(o, s, axis=AX))
                        P.opw("vector", lambda e, o=ln_v[:, :], s=ln_m[:, :]: e.reciprocal(o, s))
                        P.opw("vector", lambda e, o=comb[:, 0:8], s=comb[:, 0:8], a=ln_v[:, :]:
                              e.tensor_scalar(o, s, a, None, ALU.mult))
                        P.opw("vector", lambda e, o=ln_m[:, :], s=comb[:, 0:8]: e.reduce_max(o, s, axis=AX))
                        P.opw("vector", lambda e, o=gsg2[:, 0:8], s=comb[:, 0:8], a=ln_m[:, :]:
                              e.tensor_scalar(o, s, a, None, ALU.subtract))
                        P.opw("vector", lambda e, o=gsg2[:, 0:8], s=gsg2[:, 0:8]:
                              e.tensor_single_scalar(o, s, 0.0, ALU.is_ge))
                        P.opw("vector", lambda e, o=gsg2[:, 0:8], s=gsg2[:, 0:8], b=comb[:, 0:8]:
                              e.scalar_tensor_tensor(o, s, -1e30, b, ALU.mult, ALU.add))
                        P.opw("vector", lambda e, o=ln_v[:, :], s=gsg2[:, 0:8]: e.reduce_max(o, s, axis=AX))
                        P.opw("vector", lambda e, o=gsg2[:, 0:8], s=comb[:, 0:8], a=ln_v[:, :]:
                              e.tensor_scalar(o, s, a, None, ALU.subtract))
                        P.opw("vector", lambda e, o=gsg2[:, 0:8], s=gsg2[:, 0:8]:
                              e.tensor_single_scalar(o, s, 0.0, ALU.is_ge))
                        P.opw("vector", lambda e, o=ln_m[:, :], a=ln_m[:, :], b=ln_v[:, :]:
                              e.tensor_tensor(o, a, b, ALU.add))
                        P.opw("vector", lambda e, o=ln_v[:, :], s=ln_m[:, :]: e.reciprocal(o, s))
                        P.opw("vector", lambda e, o=comb[:, 0:8], a=comb[:, 0:8], b=gsg2[:, 0:8]:
                              e.tensor_tensor(o, a, b, ALU.mult))
                        cmb = P.opw("vector", lambda e, o=comb[:, 0:8], s=comb[:, 0:8], a=ln_v[:, :]:
                                    e.tensor_scalar(o, s, a, None, ALU.mult))
                        if dbg:
                            P.waitk("sync", cmb)
                            dbg_cmb_tgt[half * 8 + tt] = P.dma("sync", dbg_cmb[half * 8 + tt],
                                                               comb[:, :], rldS)
                        Bt = Bp[2]
                        Bt.await_free()
                        P.waitk("tensor", cmb)
                        if dbg:
                            P.wait("tensor", rldS, dbg_cmb_tgt[half * 8 + tt])
                        _tr(P, Bt.t[0:9, 0:128], comb[:, 0:9], idf[:, :], inc=Bt.mm)
                        Bt.await_data("vector")
                        P.op("vector", lambda e, o=comb_rT[:, tt, :], s=Bt.t[0:9, 0:128]:
                             e.tensor_copy(o, s), inc=Bt.rel("vector"))
                    if dbg:
                        gl_done = P.mark("vector")
                        P.waitk("sync", gl_done)
                        x1T_d2 = P.dma("sync", dbg_x1T2[half], x1T[:, :, :], rldS)
                        P.wait("tensor", rldS, x1T_d2)
                    crT_done = Bp[2].rel("vector").v
                    for e9 in range(9):
                        for tc in range(2):
                            Bt = Bp[3]
                            Bt.await_free()
                            if e9 == 0 and tc == 0:
                                P.wait("tensor", Bp[2].rel("vector"), crT_done)
                                P.wait("tensor", rldS, sel_tgt)
                            _mm(P, Bt.t[:, :], sel_sb[:, e9, :], comb_rT[:, tc * 4:(tc + 1) * 4, :],
                                True, True, inc=Bt.mm)
                            Bt.await_data("vector")
                            P.op("vector", lambda e, o=cb[:, e9, tc * 512:(tc + 1) * 512], s=Bt.t[:, :]:
                                 e.tensor_copy(o, s), inc=Bt.rel("vector"))
                    P.opw("vector", lambda e: e.engine_nop())
                    if dbg:
                        Bx = Bp[1]
                        Bx.await_free()
                        for f in range(NFT):
                            _mm(P, Bx.t[:, 0:9], x1T[:, f, 0:128], gsg_sb[:, f, :],
                                start=(f == 0), stop=(f == NFT - 1), inc=(Bx.mm if f == NFT - 1 else None))
                        Bx.await_data("vector")
                        l3 = P.opw("vector", lambda e, o=gsg2[:, :], s=Bx.t[:, 0:9]: e.tensor_copy(o, s))
                        P.op("vector", lambda e: e.engine_nop(), inc=Bx.rel("vector"))
                        P.waitk("sync", l3)
                        l3t = P.dma("sync", dbg_lgt3[half], gsg2[:, :], rldS)
                        P.wait("vector", rldS, l3t)

                def emit_half_experts(half):
                    subs = []
                    for e9 in range(E_MOE):
                        subs += [(e9, 0, 512), (e9, 512, 512), (e9, 1024, 384)]
                    subs += [(8, g * 512, 512) for g in range(11)]
                    wload_tgt = {}

                    def load_gu(which, sub_idx, slot):
                        e9, off, sz = subs[sub_idx]
                        if e9 < E_MOE:
                            src = (wg_d if which == "g" else wu_d)
                            aps = [src[e9, f * 128:(f + 1) * 128, off:off + sz] for f in range(NFT)]
                        else:
                            src = (swg_d if which == "g" else swu_d)
                            aps = [src[f * 128:(f + 1) * 128, off:off + sz] for f in range(NFT)]
                        tg = None
                        for f in range(NFT):
                            tg = P.dma("gpsimd", wbuf[slot][:, f * 512:f * 512 + sz], aps[f], ldG)
                        wload_tgt[(which, sub_idx)] = tg

                    def load_d(sub_idx, slot):
                        e9, off, sz = subs[sub_idx]
                        nit = sz // 128
                        tg = None
                        for it in range(nit):
                            if e9 < E_MOE:
                                ap = wd_d[e9, off + it * 128: off + (it + 1) * 128, :]
                            else:
                                ap = swd_d[off + it * 128: off + (it + 1) * 128, :]
                            tg = P.dma("gpsimd", wbuf[slot][:, it * D:(it + 1) * D], ap, ldG)
                        wload_tgt[("d", sub_idx)] = tg

                    load_gu("g", 0, 0)
                    load_gu("u", 0, 1)
                    hm_prev = [None, None]

                    for si, (e9, off, sz) in enumerate(subs):
                        nit = sz // 128
                        sg_slot = (3 * si) % 2
                        su_slot = (3 * si + 1) % 2
                        sd_slot = (3 * si + 2) % 2

                        P.wait("tensor", ldG, wload_tgt[("g", si)])
                        for it in range(nit):
                            bp = [Bp[0], Bp[1]] if it % 2 == 0 else [Bp[2], Bp[3]]
                            for b in bp:
                                b.await_free()
                            for f in range(NFT):
                                lh = wbuf[sg_slot][:, f * 512 + it * 128: f * 512 + (it + 1) * 128]
                                _mm(P, bp[0].t[:, :], lh, x1T[:, f, 0:512], start=(f == 0),
                                    stop=(f == NFT - 1), inc=(bp[0].mm if f == NFT - 1 else None))
                                _mm(P, bp[1].t[:, :], lh, x1T[:, f, 512:1024], start=(f == 0),
                                    stop=(f == NFT - 1), inc=(bp[1].mm if f == NFT - 1 else None))
                            for tc in range(2):
                                bp[tc].await_data("scalar")
                                P.op("scalar", lambda e, o=gs[:, it, tc * 512:(tc + 1) * 512],
                                     s=bp[tc].t[:, :]: e.activation(o, s, AF.Silu), inc=bp[tc].rel("scalar"))
                        lb = Bp[1] if nit % 2 == 1 else Bp[3]
                        g_done = (lb.mm, lb.mm.v)
                        P.waitk("gpsimd", g_done)
                        load_d(si, sg_slot)

                        P.wait("tensor", ldG, wload_tgt[("u", si)])
                        for it in range(nit):
                            bp = [Bp[0], Bp[1]] if it % 2 == 0 else [Bp[2], Bp[3]]
                            for b in bp:
                                b.await_free()
                            for f in range(NFT):
                                lh = wbuf[su_slot][:, f * 512 + it * 128: f * 512 + (it + 1) * 128]
                                _mm(P, bp[0].t[:, :], lh, x1T[:, f, 0:512], start=(f == 0),
                                    stop=(f == NFT - 1), inc=(bp[0].mm if f == NFT - 1 else None))
                                _mm(P, bp[1].t[:, :], lh, x1T[:, f, 512:1024], start=(f == 0),
                                    stop=(f == NFT - 1), inc=(bp[1].mm if f == NFT - 1 else None))
                            for tc in range(2):
                                bp[tc].await_data("vector")
                                if hm_prev[it % 2] is not None:
                                    P.waitk("vector", hm_prev[it % 2])
                                ut = P.op("vector", lambda e, o=u_tmp[:, it % 2, tc * 512:(tc + 1) * 512],
                                          a=bp[tc].t[:, :], b=cb[:, e9, tc * 512:(tc + 1) * 512]:
                                          e.tensor_tensor(o, a, b, ALU.mult), inc=bp[tc].rel("vector"))
                                P.wait("gpsimd", bp[tc].rel("vector"), ut)
                                hm_prev[it % 2] = P.op("gpsimd", lambda e, o=gs[:, it, tc * 512:(tc + 1) * 512],
                                     a=gs[:, it, tc * 512:(tc + 1) * 512],
                                     b=u_tmp[:, it % 2, tc * 512:(tc + 1) * 512]:
                                     e.tensor_tensor(o, a, b, ALU.mult), inc=True)
                        lb = Bp[1] if nit % 2 == 1 else Bp[3]
                        u_done = (lb.mm, lb.mm.v)
                        P.waitk("gpsimd", u_done)
                        if si + 1 < len(subs):
                            load_gu("g", si + 1, su_slot)
                        h_done = P.mark("gpsimd")

                        P.wait("tensor", ldG, wload_tgt[("d", si)])
                        P.waitk("tensor", h_done)
                        for tt in range(8):
                            bq = [Bp[4], Bp[5], Bp[6], Bp[7]] if tt % 2 == 0 else [Bp[0], Bp[1], Bp[2], Bp[3]]
                            for b in bq:
                                b.await_free()
                            for it in range(nit):
                                lh = gs[:, it, tt * 128:(tt + 1) * 128]
                                for oc in range(4):
                                    _mm(P, bq[oc].t[:, :], lh,
                                        wbuf[sd_slot][:, it * D + oc * 512: it * D + (oc + 1) * 512],
                                        start=(it == 0), stop=(it == nit - 1),
                                        inc=(bq[oc].mm if it == nit - 1 else None))
                            if si == 0:
                                P.wait("vector", rldS, acc_tgt[tt])
                            for oc in range(4):
                                bq[oc].await_data("vector")
                                P.op("vector", lambda e, o=acc[:, tt, oc * 512:(oc + 1) * 512],
                                     a=acc[:, tt, oc * 512:(oc + 1) * 512], b=bq[oc].t[:, :]:
                                     e.tensor_tensor(o, a, b, ALU.add), inc=bq[oc].rel("vector"))
                        d_done = (bq[3].mm, bq[3].mm.v)
                        P.waitk("gpsimd", d_done)
                        if si + 1 < len(subs):
                            load_gu("u", si + 1, sd_slot)

                def emit_half_ln(half):
                    if dbg and half == 0:
                        acc_done = P.mark("vector")
                        P.waitk("sync", acc_done)
                        tgt = P.dma("sync", dbg_acc, acc[:, 0, :], rldS)
                        P.wait("vector", rldS, tgt)
                    for tt in range(8):
                        P.opw("vector", lambda e: e.engine_nop())
                        P.opw("vector", lambda e, o=ln_m[:, :], s=acc[:, tt, :]: e.reduce_sum(o, s, axis=AX))
                        P.opw("vector", lambda e, o=ln_m[:, :], s=ln_m[:, :]:
                              e.tensor_single_scalar(o, s, 1.0 / D, ALU.mult))
                        xm = P.opw("vector", lambda e, o=l2_in[:, :], s=acc[:, tt, :], a=ln_m[:, :]:
                                   e.tensor_scalar(o, s, a, None, ALU.subtract))
                        P.waitk("scalar", xm)
                        sq2 = P.op("scalar", lambda e, o=acc[:, tt, :], s=l2_in[:, :], a2=ln_v[:, :]:
                                   e.activation(o, s, AF.Square, accum_out=a2), inc=True)
                        P.waitk("vector", sq2)
                        P.opw("vector", lambda e, o=ln_v[:, :], s=ln_v[:, :]:
                              e.tensor_single_scalar(o, s, 1.0 / D, ALU.mult))
                        P.opw("vector", lambda e, o=ln_v[:, :], s=ln_v[:, :]:
                              e.tensor_single_scalar(o, s, LN_EPS, ALU.add))
                        rv = P.opw("vector", lambda e, o=ln_r[:, :], s=ln_v[:, :]: e.reciprocal(o, s))
                        P.waitk("scalar", rv)
                        rs = P.op("scalar", lambda e, o=ln_r[:, :], s=ln_r[:, :]:
                                  e.activation(o, s, AF.Sqrt), inc=True)
                        P.waitk("vector", rs)
                        ob = P.op("vector", lambda e, o=acc[:, tt, :], s=l2_in[:, :], a=ln_r[:, :]:
                                  e.tensor_scalar(o, s, a, None, ALU.mult), inc=True)
                        P.waitk("sync", ob)
                        P.dma("sync", out_d[(half * 8 + tt) * 128:(half * 8 + tt + 1) * 128, :],
                              acc[:, tt, :], stS)

                for half in range(2):
                    emit_half_prep(half)
                    emit_half_experts(half)
                    emit_half_ln(half)
                    if half == 0:
                        P.barrier(bar, scalar_scratch=rowscr[0:1, 255:256])

        P.wait("sync", stS, stS.v)

        with nc.Block() as block:
            @block.sync
            def _(e):
                for fn in P.q["sync"]:
                    fn(e)

            @block.tensor
            def _(e):
                for fn in P.q["tensor"]:
                    fn(e)

            @block.vector
            def _(e):
                for fn in P.q["vector"]:
                    fn(e)

            @block.scalar
            def _(e):
                for fn in P.q["scalar"]:
                    fn(e)

            @block.gpsimd
            def _(e):
                for fn in P.q["gpsimd"]:
                    fn(e)

    return nc



def make_in_maps(inputs):
    f32 = np.float32
    asf = lambda a: np.ascontiguousarray(np.asarray(a), dtype=f32)
    x_full = asf(inputs["L0_attn_hidden_states"]).reshape(S_TOT, D)
    wq_t = np.ascontiguousarray(asf(inputs["L0_attn_w_q"]).T)
    wk_t = np.ascontiguousarray(asf(inputs["L0_attn_w_k"]).T)
    wv_t = np.ascontiguousarray(asf(inputs["L0_attn_w_v"]).T)
    wo_t = np.ascontiguousarray(asf(inputs["L0_attn_w_o"]).T)
    egw1_t = np.ascontiguousarray(asf(inputs["L0_attn_eg_w1"]).T)
    egb1_c = np.ascontiguousarray(asf(inputs["L0_attn_eg_b1"]).reshape(4, 128).T)
    egw2_c = np.ascontiguousarray(asf(inputs["L0_attn_eg_w2"]).reshape(4, 128).T)
    egb2_c = asf(inputs["L0_attn_eg_b2"]).reshape(1, 1)
    memgate = asf(inputs["L0_attn_memory_gate"]).reshape(1, NH)
    fmem = asf(inputs["L0_attn_fine_memories"])
    fn = asf(inputs["L0_attn_fine_memory_norms"])
    fnorm_c = np.zeros((128, NH, 5), f32)
    for m in range(NM):
        fnorm_c[:, :, m] = fn[m].T
    fnorm_c = np.ascontiguousarray(fnorm_c.reshape(128, NH * 5))
    gsg_t = np.ascontiguousarray(
        np.concatenate([asf(inputs["L0_ffn_gate_w"]).T, asf(inputs["L0_ffn_shared_gate_w"])], axis=1))
    wg = asf(inputs["L0_ffn_w_gate"])
    wu = asf(inputs["L0_ffn_w_up"])
    wd = asf(inputs["L0_ffn_w_down"])
    swg = asf(inputs["L0_ffn_sw_gate"])
    swu = asf(inputs["L0_ffn_sw_up"])
    swd = asf(inputs["L0_ffn_sw_down"])

    ident_bf = np.eye(128, dtype=BF16NP)
    ident_f32 = np.eye(128, dtype=f32)
    maskT = np.triu(np.ones((128, 128), f32))
    ones_row = np.ones((1, 128), f32)
    ones_col = np.ones((128, 1), f32)
    sel9 = np.zeros((9, 9 * 128), f32)
    for e in range(9):
        sel9[e, e * 128:(e + 1) * 128] = 1.0

    in_maps = []
    for cid in range(NCORES):
        pm = np.zeros((128, NCORES), f32)
        pm[:, :cid] = 1.0
        in_maps.append({
            "x": np.ascontiguousarray(x_full[cid * TPC:(cid + 1) * TPC, :]),
            "wq_t": wq_t, "wk_t": wk_t, "wv_t": wv_t, "wo_t": wo_t,
            "egw1_t": egw1_t, "egb1_c": egb1_c, "egw2_c": egw2_c, "egb2_c": egb2_c,
            "memgate": memgate, "fmem": fmem, "fnorm_c": fnorm_c, "gsg_t": gsg_t,
            "wg": wg, "wu": wu, "wd": wd, "swg": swg, "swu": swu, "swd": swd,
            "ident_bf": ident_bf, "ident_f32": ident_f32, "maskT": maskT,
            "ones_row": ones_row, "ones_col": ones_col, "sel9": sel9, "pmask": pm,
        })
    return in_maps


def kernel(**inputs):
    nc = build(dbg=False)
    in_maps = make_in_maps(inputs)
    res = run_bass_kernel_spmd(nc, in_maps, core_ids=list(range(NCORES)))
    out = np.concatenate([res.results[c]["out"] for c in range(NCORES)], axis=0)
    return out.reshape(1, S_TOT, D).astype(np.float32)


# version 07
# speedup vs baseline: 1.0101x; 1.0026x over previous
\

import numpy as np
import ml_dtypes
from contextlib import ExitStack

import concourse.bass as bass
import concourse.mybir as mybir
from concourse.bass_utils import run_bass_kernel_spmd

F32 = mybir.dt.float32
BF16 = mybir.dt.bfloat16
AF = mybir.ActivationFunctionType
ALU = mybir.AluOpType
AX = mybir.AxisListType.X
BF16NP = ml_dtypes.bfloat16

NCORES = 8
S_TOT = 16384
D = 2048
NH, HD, NM = 16, 128, 4
TPC = S_TOT // NCORES
NCH = TPC // 128
NFT = D // 128
EPS = 1e-6
LN_EPS = 1e-5
E_MOE, I_MOE, I_SH = 8, 1408, 5632
HALF = TPC // 2
REL_SCALE = 1.0 / float(S_TOT)

ENGINES = ("sync", "tensor", "vector", "scalar", "gpsimd")


class Sem:
    __slots__ = ("h", "v", "name")

    def __init__(self, h, name):
        self.h = h
        self.v = 0
        self.name = name


class Planner:
    def __init__(self, nc, ctx):
        self.nc = nc
        self.ctx = ctx
        self.q = {e: [] for e in ENGINES}
        self.ax = {e: self.sem("ax_" + e) for e in ENGINES}

    def sem(self, name):
        return Sem(self.ctx.enter_context(self.nc.semaphore(name)), name)

    def wait(self, eng, sem, val=None):
        v = sem.v if val is None else val
        if v is None or v <= 0:
            return
        self.q[eng].append(lambda e, h=sem.h, v=v: e.wait_ge(h, v))

    def waitk(self, eng, tok):
        if tok is None:
            return
        self.wait(eng, tok[0], tok[1])

    def op(self, eng, fn, inc=None, by=1):
        if inc is True:
            inc = self.ax[eng]
            ret_tok = True
        else:
            ret_tok = False
        if inc is not None:
            inc.v += by

            def g(e, fn=fn, h=inc.h, by=by):
                fn(e).then_inc(h, by)
        else:
            def g(e, fn=fn):
                fn(e)
        self.q[eng].append(g)
        if inc is None:
            return None
        return (inc, inc.v) if ret_tok else inc.v

    def opw(self, eng, fn):
        tok = self.op(eng, fn, inc=True)
        self.waitk(eng, tok)
        return tok

    def mark(self, eng):
        if eng == "vector":
            return self.op(eng, lambda e: e.engine_nop(), inc=True)
        return self.op(eng, lambda e: e.nop(), inc=True)

    def dma(self, eng, out, in_, sem, **kw):
        return self.op(eng, lambda e, o=out, i=in_, kw=dict(kw): e.dma_start(out=o, in_=i, **kw), inc=sem, by=16)

    def barrier(self, bar, scalar_scratch=None):
        self.op("vector", lambda e: e.engine_nop(), inc=bar)
        self.op("gpsimd", lambda e: e.engine_nop(), inc=bar)
        if scalar_scratch is not None:
            self.op("scalar", lambda e, o=scalar_scratch: e.copy(o, o), inc=bar)
        else:
            self.op("scalar", lambda e, h=bar.h: e.sem_inc(h, 1))
            bar.v += 1
        self.op("tensor", lambda e, h=bar.h: e.sem_inc(h, 1))
        self.op("sync", lambda e, h=bar.h: e.sem_inc(h, 1))
        bar.v += 2
        tgt = bar.v
        for eng in ENGINES:
            self.wait(eng, bar, tgt)


class Bank:

    def __init__(self, P, stack, name, shape, dtype):
        self.t = stack.enter_context(P.nc.psum_tensor(name, shape, dtype))
        self.mm = P.sem(name + "_mm")
        self.ev = {}
        self.name = name
        self.P = P

    def _ev(self, eng):
        if eng not in self.ev:
            self.ev[eng] = self.P.sem(self.name + "_ev_" + eng)
        return self.ev[eng]

    def await_free(self, eng="tensor"):
        for s in self.ev.values():
            self.P.wait(eng, s)

    def await_data(self, eng):
        self.P.wait(eng, self.mm)

    def rel(self, eng):
        return self._ev(eng)


def _mm(P, out, lhsT, rhs, start, stop, inc=None):
    return P.op("tensor", lambda e, o=out, l=lhsT, r=rhs, s=start, t=stop:
                e.matmul(o, l, r, start=s, stop=t), inc=inc)


def _tr(P, out, in_, ident, inc=None):
    return P.op("tensor", lambda e, o=out, i=in_, d=ident: e.transpose(o, i, d), inc=inc)


def build(dbg=False):
    nc = bass.Bass("TRN2", target_bir_lowering=False)

    def din(name, shape, dt=F32):
        return nc.dram_tensor(name, shape, dt, kind="ExternalInput").ap()

    def dout(name, shape, dt=F32):
        return nc.dram_tensor(name, shape, dt, kind="ExternalOutput").ap()

    def dint(name, shape, dt=F32):
        k = "ExternalOutput" if dbg else "Internal"
        return nc.dram_tensor(name, shape, dt, kind=k).ap()

    x_d = din("x", [TPC, D])
    wq_d = din("wq_t", [D, D])
    wk_d = din("wk_t", [D, D])
    wv_d = din("wv_t", [D, D])
    wo_d = din("wo_t", [D, D])
    egw1_d = din("egw1_t", [D, 512])
    egb1_d = din("egb1_c", [128, 4])
    egw2_d = din("egw2_c", [128, 4])
    egb2_d = din("egb2_c", [1, 1])
    memgate_d = din("memgate", [1, NH])
    fmem_d = din("fmem", [NM, NH, HD, HD])
    fnorm_d = din("fnorm_c", [128, NH * 5])
    gsg_d = din("gsg_t", [D, 9])
    wg_d = din("wg", [E_MOE, D, I_MOE])
    wu_d = din("wu", [E_MOE, D, I_MOE])
    wd_d = din("wd", [E_MOE, I_MOE, D])
    swg_d = din("swg", [D, I_SH])
    swu_d = din("swu", [D, I_SH])
    swd_d = din("swd", [I_SH, D])
    idb_d = din("ident_bf", [128, 128], BF16)
    idf_d = din("ident_f32", [128, 128])
    mask_d = din("maskT", [128, 128])
    onesrow_d = din("ones_row", [1, 128])
    onescol_d = din("ones_col", [128, 1])
    sel_d = din("sel9", [9, 9 * 128])
    pmask_d = din("pmask", [128, NCORES])

    out_d = dout("out", [TPC, D])

    sq_sp = dint("sq_sp", [NCH, 128, NH * 128], BF16)
    sk_sp = dint("sk_sp", [NCH, 128, NH * 128], BF16)
    v_sp = dint("v_sp", [NCH, 128, NH * 129], BF16)
    oT_sp = dint("oT_sp", [NCH, 128, NH * 128], BF16)
    x1f_sp = dint("x1f_sp", [NCH, 128, D])
    attn_dbg = dout("attn_dbg", [NCH, 128, D], BF16) if dbg else None
    if dbg:
        dbg_n = dout("dbg_n", [128, NH * 5])
        dbg_cm = dout("dbg_cm", [128, NH * 4])
        dbg_crf = dout("dbg_crf", [128, D], BF16)
        dbg_af = dout("dbg_af", [128, NH * 640], BF16)
        dbg_S = dout("dbg_S", [128, NH * 129])
        dbg_misc = dout("dbg_misc", [128, 128])
        dbg_sqrl = dout("dbg_sqrl", [128, NH * 128], BF16)
        dbg_cmb = dout("dbg_cmb", [16, 128, 9])
        dbg_lgt = dout("dbg_lgt", [16, 128, 9])
        dbg_x1T = dout("dbg_x1T", [2, 128, NFT * HALF], BF16)
        dbg_x1T2 = dout("dbg_x1T2", [2, 128, NFT * HALF], BF16)
        dbg_lgt2 = dout("dbg_lgt2", [16, 128, 9])
        dbg_lgt3 = dout("dbg_lgt3", [2, 128, 9])
        dbg_acc = dout("dbg_acc", [128, D])
        dbg_fm = dout("dbg_fm", [128, NH * 640], BF16)

    cc_in = nc.dram_tensor("cc_in", [128, NH * 130], F32).ap()
    cc_out = nc.dram_tensor("cc_out", [NCORES * 128, NH * 130], F32).ap()

    root = ExitStack()
    with root as ctx:
        P = Planner(nc, ctx)
        sb = lambda n, s, d=F32: ctx.enter_context(nc.sbuf_tensor(n, s, d))

        idb = sb("idb", [128, 128], BF16)
        idf = sb("idf", [128, 128])
        mask_sb = sb("mask_sb", [128, 128])
        onesrow = sb("onesrow", [1, 128])
        onescol = sb("onescol", [128, 1])
        pmask_sb = sb("pmask_sb", [128, NCORES])
        memgate_sb = sb("memgate_sb", [1, NH])
        egb1_sb = sb("egb1_sb", [128, 4])
        egb2_sb = sb("egb2_sb", [128, 1])
        fnorm_sb = sb("fnorm_sb", [128, NH, 5])
        nrhs_bf = sb("nrhs_bf", [128, NH, 5], BF16)
        S_f32 = sb("S_f32", [128, NH, 129])
        S_bf = sb("S_bf", [128, NH, 129], BF16)
        sq_own = sb("sq_own", [128, NH])
        sqsumg = sb("sqsumg", [128, NH])
        sqsum4 = sb("sqsum4", [128, NH, 4])
        prod_sb = sb("prod_sb", [128, NH, 4])
        gate_cols = sb("gate_cols", [128, NH])
        omgate_cols = sb("omgate_cols", [128, NH])
        coeffbase = sb("coeffbase", [128, NH, 5])
        zalive_col = sb("zalive_col", [128, 1])
        rowscr = sb("rowscr", [1, 512])
        g_col = sb("g_col", [128, 1])
        omg_col = sb("omg_col", [128, 1])
        c_loc = sb("c_loc", [128, 1])
        den_t = sb("den_tmp", [128, 1])
        crcoefs = sb("crcoefs", [128, NH])
        c2cols = sb("c2cols", [128, NH])
        n_all = [sb(f"n_all{i}", [128, NH, 5]) for i in range(2)]
        r_all = sb("r_all", [128, NH, 5])
        coeffm = sb("coeffm", [128, NH, 4])
        ln_m = sb("ln_m", [128, 1])
        ln_v = sb("ln_v", [128, 1])
        ln_r = sb("ln_r", [128, 1])

        ldS = P.sem("ldS")
        splS = P.sem("splS")
        rldS = P.sem("rldS")
        stS = P.sem("stS")
        ldG = P.sem("ldG")
        splG = P.sem("splG")
        ccs = P.sem("ccs")
        bar = P.sem("bar")

        for dst, src in [(idb[:, :], idb_d), (idf[:, :], idf_d), (mask_sb[:, :], mask_d),
                         (onesrow[:, :], onesrow_d), (onescol[:, :], onescol_d),
                         (pmask_sb[:, :], pmask_d),
                         (memgate_sb[:, :], memgate_d), (egb1_sb[:, :], egb1_d),
                         (fnorm_sb[:, :, :], fnorm_d), (rowscr[0:1, 0:1], egb2_d)]:
            P.dma("sync", dst, src, ldS)
        ld0_done = ldS.v

        with ExitStack() as ph1:
            psb = lambda n, s, d=F32: ph1.enter_context(nc.sbuf_tensor(n, s, d))
            xT = psb("xT", [128, NFT, TPC], BF16)
            A = psb("A", [128, NFT, TPC], BF16)
            wsb = psb("wsb", [128, NFT, 1024], BF16)
            xstage = [psb(f"xst{i}", [128, D], BF16) for i in range(2)]
            v_stage = [psb(f"vst{i}", [128, 8, 129], BF16) for i in range(2)]
            skc_st = psb("skc_st", [128, 8, 128], BF16)
            elu_t1 = [psb(f"elu1_{i}", [128, 512]) for i in range(4)]

            B = [Bank(P, ph1, f"ps{i}", [128, 512], F32) for i in range(7)]
            Btr = Bank(P, ph1, "pstr", [128, 1024], BF16)

            x_tgt = [None] * NCH
            x_tgt[0] = P.dma("gpsimd", xstage[0][:, :], x_d[0:128, :], ldG)
            x_tgt[1] = P.dma("gpsimd", xstage[1][:, :], x_d[128:256, :], ldG)
            for t in range(NCH):
                P.wait("tensor", ldG, x_tgt[t])
                for r in range(2):
                    Btr.await_free()
                    for k in range(8):
                        f = r * 8 + k
                        _tr(P, Btr.t[:, k * 128:(k + 1) * 128], xstage[t % 2][:, f * 128:(f + 1) * 128],
                            idb[:, :], inc=(Btr.mm if k == 7 else None))
                    Btr.await_data("scalar")
                    P.op("scalar", lambda e, d=xT[:, r * 8:(r + 1) * 8, t * 128:(t + 1) * 128],
                         s=Btr.t[:, :].rearrange("p (k j) -> p k j", j=128): e.copy(d, s),
                         inc=Btr.rel("scalar"))
                tok = (Btr.mm, Btr.mm.v)
                if t + 2 < NCH:
                    P.waitk("gpsimd", tok)
                    x_tgt[t + 2] = P.dma("gpsimd", xstage[t % 2][:, :],
                                         x_d[(t + 2) * 128:(t + 3) * 128, :], ldG)

            def load_w_half(dram, hw):
                tg = None
                for f in range(NFT):
                    tg = P.dma("gpsimd", wsb[:, f, :],
                               dram[f * 128:(f + 1) * 128, hw * 1024:(hw + 1) * 1024], ldG)
                return tg

            def proj_featmajor(dst, dram, prev_rel):
                P.wait("tensor", Btr.rel("scalar"))
                for hw in range(2):
                    if hw == 0 and prev_rel is not None:
                        P.waitk("gpsimd", prev_rel)
                    w_tgt = load_w_half(dram, hw)
                    for il in range(8):
                        i = hw * 8 + il
                        bs = B[0:4] if (il % 2 == 0) else [B[4], B[5], B[6], B[0]]
                        for b in bs:
                            b.await_free()
                        if il == 0:
                            P.wait("tensor", ldG, w_tgt)
                        for f in range(NFT):
                            for tc in range(4):
                                _mm(P, bs[tc].t[:, :], wsb[:, f, il * 128:(il + 1) * 128],
                                    xT[:, f, tc * 512:(tc + 1) * 512], start=(f == 0),
                                    stop=(f == NFT - 1), inc=(bs[tc].mm if (f == NFT - 1) else None))
                        t1 = None
                        for tc in range(4):
                            b = bs[tc]
                            b.await_data("vector")
                            t1 = P.op("vector", lambda e, o=elu_t1[tc][:, :], s=b.t[:, :]:
                                      e.tensor_single_scalar(o, s, 0.0, ALU.min), inc=True)
                        P.waitk("scalar", t1)
                        t2 = None
                        for tc in range(4):
                            t2 = P.op("scalar", lambda e, o=elu_t1[tc][:, :], s=elu_t1[tc][:, :]:
                                      e.activation(o, s, AF.Exp), inc=True)
                        P.waitk("vector", t2)
                        for tc in range(4):
                            b = bs[tc]
                            dslice = dst[:, i, tc * 512:(tc + 1) * 512]
                            P.op("vector", lambda e, o=dslice, s=b.t[:, :], x2=elu_t1[tc][:, :]:
                                 e.scalar_tensor_tensor(o, s, 0.0, x2, ALU.max, ALU.add),
                                 inc=b.rel("vector"))
                    if hw == 0:
                        lb = B[0]
                        half_rel = (lb.mm, lb.mm.v)
                        P.waitk("gpsimd", half_rel)
                lb = B[0]
                return (lb.mm, lb.mm.v)

            xT_done = P.ax["scalar"].v
            wk_rel = proj_featmajor(A, wk_d, None)
            sk_wr = P.mark("vector")
            P.waitk("tensor", sk_wr)

            P.waitk("sync", sk_wr)
            sk_sp_tgt = [None] * NCH
            for c in range(NCH):
                sk_sp_tgt[c] = P.dma("sync", sk_sp[c], A[:, :, c * 128:(c + 1) * 128], splS)

            for i in range(2):
                P.op("vector", lambda e, o=v_stage[i][:, :, 128:129]: e.memset(o, 1.0))
            vinit = P.op("vector", lambda e, o=S_f32[:, :, :]: e.memset(o, 0.0), inc=True)
            P.waitk("tensor", vinit)
            P.waitk("sync", vinit)

            prev_rel = wk_rel
            last_vspill = {0: None, 1: None}
            for hw in range(2):
                P.waitk("gpsimd", prev_rel)
                wv_tgt = load_w_half(wv_d, hw)
                for t in range(NCH):
                    par = t % 2
                    bs = B[0:2] if (t % 2 == 0) else B[2:4]
                    for b in bs:
                        b.await_free()
                    if t == 0:
                        P.wait("tensor", ldG, wv_tgt)
                    for f in range(NFT):
                        for oc in range(2):
                            _mm(P, bs[oc].t[:, :], xT[:, f, t * 128:(t + 1) * 128],
                                wsb[:, f, oc * 512:(oc + 1) * 512], start=(f == 0),
                                stop=(f == NFT - 1), inc=(bs[oc].mm if f == NFT - 1 else None))
                    if last_vspill[par] is not None:
                        P.wait("vector", splS, last_vspill[par])
                    for oc in range(2):
                        b = bs[oc]
                        b.await_data("vector")
                        P.op("vector", lambda e, o=v_stage[par][:, 4 * oc:4 * oc + 4, 0:128],
                             s=b.t[:, :].rearrange("p (h e) -> p h e", h=4): e.tensor_copy(o, s),
                             inc=b.rel("vector"))
                    vdata = P.mark("vector")
                    Btr.await_free()
                    for k in range(8):
                        h = hw * 8 + k
                        _tr(P, Btr.t[:, k * 128:(k + 1) * 128], A[:, h, t * 128:(t + 1) * 128],
                            idb[:, :], inc=(Btr.mm if k == 7 else None))
                    Btr.await_data("scalar")
                    P.op("scalar", lambda e, d=skc_st[:, :, :],
                         s=Btr.t[:, :].rearrange("p (k j) -> p k j", j=128): e.copy(d, s),
                         inc=Btr.rel("scalar"))
                    sk_tr = Btr.rel("scalar").v
                    for k in range(8):
                        h = hw * 8 + k
                        bd = B[4 + (k % 2)]
                        bd.await_free()
                        if k == 0:
                            P.wait("tensor", Btr.rel("scalar"), sk_tr)
                            P.waitk("tensor", vdata)
                        _mm(P, bd.t[:, 0:129], skc_st[:, k, :], v_stage[par][:, k, :],
                            start=True, stop=True, inc=bd.mm)
                        bd.await_data("vector")
                        P.op("vector", lambda e, o=S_f32[:, h, :], s=bd.t[:, 0:129]:
                             e.scalar_tensor_tensor(o, s, 1.0, o, ALU.mult, ALU.add),
                             inc=bd.rel("vector"))
                    P.waitk("sync", vdata)
                    last_vspill[par] = P.dma("sync", v_sp[t][:, hw * 8 * 129:(hw * 8 + 8) * 129],
                                             v_stage[par][:, :, :], splS)
                prev_rel = (B[5].mm, B[5].mm.v)

            P.wait("vector", splS, sk_sp_tgt[NCH - 1])
            ph1_mm_tok = proj_featmajor(A, wq_d, prev_rel)
            sq_sp_tgt = [None] * NCH
            sq_wr = P.mark("vector")
            P.waitk("sync", sq_wr)
            for c in range(NCH):
                sq_sp_tgt[c] = P.dma("sync", sq_sp[c], A[:, :, c * 128:(c + 1) * 128], splS)
            P.waitk("vector", sq_wr)
            sqs_tok = None
            for h in range(NH):
                sqs_tok = P.op("vector", lambda e, o=sq_own[:, h:h + 1], s=A[:, h, :]:
                               e.reduce_sum(o, s, axis=AX), inc=True)

            P.waitk("gpsimd", sqs_tok)
            cc1 = P.dma("gpsimd", cc_in[:, 0:NH * 129], S_f32[:, :, :], ldG)
            cc1 = P.dma("gpsimd", cc_in[:, NH * 129:NH * 130], sq_own[:, :], ldG)
            P.wait("gpsimd", ldG, cc1)
            P.op("gpsimd", lambda e: e.collective_compute(
                "AllGather", ALU.bypass, replica_groups=[list(range(NCORES))],
                ins=[cc_in.opt()], outs=[cc_out.opt()]), inc=ccs, by=1)
            cc_done = ccs.v

        P.barrier(bar, scalar_scratch=rowscr[0:1, 255:256])
        P.wait("gpsimd", splS, splS.v)
        P.wait("sync", splS, splS.v)

        with ExitStack() as ph2:
            psb = lambda n, s, d=F32: ph2.enter_context(nc.sbuf_tensor(n, s, d))
            fmem_sb = psb("fmem_sb", [128, NH, 640], BF16)
            msum_f = psb("msum_f", [128, NH, 128])
            egw1_sb = psb("egw1_sb", [128, NFT, 512], BF16)
            egw2_sb = psb("egw2_sb", [128, 4], BF16)
            af_sb = [psb(f"af_sb{i}", [128, NH, 640], BF16) for i in range(2)]
            shard_st = [psb(f"shard{i}", [128, NH * 130]) for i in range(2)]
            sq_rl = [psb(f"sq_rl{i}", [128, NH, 128], BF16) for i in range(2)]
            sk_rl = [psb(f"sk_rl{i}", [128, NH, 128], BF16) for i in range(2)]
            v_rl = [psb(f"v_rl{i}", [128, NH, 129], BF16) for i in range(2)]
            skc2 = psb("skc2", [128, NH, 128], BF16)
            msT = psb("msT", [128, NH, 128], BF16)
            den16 = psb("den16", [128, NH])
            c_loc16 = psb("c_loc16", [128, NH])
            num_sb = psb("num_sb", [128, NH, 128], BF16)
            crf = [psb(f"crf{i}", [128, D], BF16) for i in range(2)]
            crT = psb("crT", [128, NFT, 128], BF16)
            h1_bf = psb("h1_bf", [128, 4, 128], BF16)
            oa = psb("oa", [128, D], BF16)
            oT = psb("oT", [128, NFT, 128], BF16)

            B2 = [Bank(P, ph2, f"p2_{i}", [128, 512], F32) for i in range(5)]
            Btr2 = Bank(P, ph2, "p2tr", [128, 1024], BF16)
            Baf, Bb, Bsc, Bnum, Bdel = B2
            P.waitk("tensor", ph1_mm_tok)

            egw1_tgt = None
            for f in range(NFT):
                egw1_tgt = P.dma("gpsimd", egw1_sb[:, f, :], egw1_d[f * 128:(f + 1) * 128, :], ldG)
            egw2_tgt = P.dma("gpsimd", egw2_sb[:, :], egw2_d, ldG)
            fm_tgt = None
            for m in range(NM):
                fm_tgt = P.dma("gpsimd", fmem_sb[:, :, m * 128:(m + 1) * 128],
                               fmem_d[m].rearrange("h d e -> d h e"), ldG)
            ms_tgt = None
            for m in range(NM):
                if m > 0:
                    P.wait("gpsimd", ldG, ms_tgt)
                kw = {} if m == 0 else {"accum_op": ALU.add}
                ms_tgt = P.dma("gpsimd", msum_f[:, :, :], fmem_d[m].rearrange("h d e -> d h e"), ldG, **kw)
            P.wait("vector", ldG, ms_tgt)
            msum_tok = P.op("vector", lambda e, o=fmem_sb[:, :, 512:640], s=msum_f[:, :, :]:
                            e.tensor_copy(o, s), inc=True)

            P.wait("vector", ldS, ld0_done)
            P.opw("vector", lambda e, o=fnorm_sb[:, :, 4], a=fnorm_sb[:, :, 0], b=fnorm_sb[:, :, 1]:
                  e.tensor_tensor(o, a, b, ALU.add))
            P.opw("vector", lambda e, o=r_all[:, :, 0], a=fnorm_sb[:, :, 2], b=fnorm_sb[:, :, 3]:
                  e.tensor_tensor(o, a, b, ALU.add))
            P.opw("vector", lambda e, o=fnorm_sb[:, :, 4], a=fnorm_sb[:, :, 4], b=r_all[:, :, 0]:
                  e.tensor_tensor(o, a, b, ALU.add))
            nrhs_tok = P.op("vector", lambda e, o=nrhs_bf[:, :, :], s=fnorm_sb[:, :, :]:
                            e.tensor_copy(o, s), inc=True)
            dsum_tok = None
            for m in range(NM):
                dsum_tok = P.op("vector", lambda e, o=sqsum4[:, 0, m:m + 1], s=fnorm_sb[:, :, m]:
                                e.reduce_sum(o, s, axis=AX), inc=True)
            P.waitk("vector", dsum_tok)
            Bsc.await_free()
            P.waitk("tensor", dsum_tok)
            _mm(P, Bsc.t[0:1, 0:4], onescol[:, :], sqsum4[:, 0, :], True, True, inc=Bsc.mm)
            Bsc.await_data("vector")
            P.opw("vector", lambda e, o=rowscr[0:1, 16:20], s=Bsc.t[0:1, 0:4]: e.tensor_copy(o, s))
            P.op("vector", lambda e: e.engine_nop(), inc=Bsc.rel("vector"))
            P.opw("vector", lambda e, o=rowscr[0:1, 20:24], s=rowscr[0:1, 16:20]:
                  e.tensor_single_scalar(o, s, EPS, ALU.is_ge))
            P.opw("vector", lambda e, o=rowscr[0:1, 24:25], s=rowscr[0:1, 16:20]:
                  e.reduce_sum(o, s, axis=AX))
            za_tok = P.opw("vector", lambda e, o=rowscr[0:1, 25:26], s=rowscr[0:1, 24:25]:
                           e.tensor_single_scalar(o, s, EPS, ALU.is_ge))
            P.wait("scalar", ldS, ld0_done)
            gr_tok = P.op("scalar", lambda e, o=rowscr[0:1, 32:48], s=memgate_sb[0:1, :]:
                          e.activation(o, s, AF.Sigmoid), inc=True)
            Bnum.await_free()
            P.waitk("tensor", gr_tok)
            P.waitk("tensor", za_tok)
            P.wait("tensor", ldS, ld0_done)
            _mm(P, Bnum.t[:, 0:16], onesrow[0:1, :], rowscr[0:1, 32:48], True, True)
            _mm(P, Bnum.t[:, 16:17], onesrow[0:1, :], rowscr[0:1, 25:26], True, True)
            _mm(P, Bnum.t[:, 17:18], onesrow[0:1, :], rowscr[0:1, 0:1], True, True, inc=Bnum.mm)
            Bnum.await_data("vector")
            P.op("vector", lambda e, o=gate_cols[:, :], s=Bnum.t[:, 0:16]: e.tensor_copy(o, s))
            P.op("vector", lambda e, o=zalive_col[:, :], s=Bnum.t[:, 16:17]: e.tensor_copy(o, s))
            P.opw("vector", lambda e, o=egb2_sb[:, :], s=Bnum.t[:, 17:18]: e.tensor_copy(o, s))
            P.op("vector", lambda e: e.engine_nop(), inc=Bnum.rel("vector"))
            P.op("vector", lambda e, o=omgate_cols[:, :], s=gate_cols[:, :]:
                 e.tensor_scalar(o, s, -1.0, 1.0, ALU.mult, ALU.add))

            P.wait("gpsimd", ccs, cc_done)
            sh_tgt = [None] * NCORES
            sh_rel = [None] * NCORES
            sh_tgt[0] = P.dma("gpsimd", shard_st[0][:, :], cc_out[0:128, :], ldG)
            sh_tgt[1] = P.dma("gpsimd", shard_st[1][:, :], cc_out[128:256, :], ldG)
            for r in range(NCORES):
                P.wait("vector", ldG, sh_tgt[r])
                ss = shard_st[r % 2]
                P.op("vector", lambda e, o=S_f32[:, :, :],
                     s=ss[:, 0:NH * 129].rearrange("p (h c) -> p h c", h=NH),
                     m=pmask_sb[:, r:r + 1], op1=(ALU.bypass if r == 0 else ALU.add):
                     e.scalar_tensor_tensor(o, s, m, o, ALU.mult, op1))
                if r == 0:
                    sh_rel[r] = P.opw("vector", lambda e, o=sqsumg[:, :], s=ss[:, NH * 129:NH * 130]:
                                      e.tensor_copy(o, s))
                else:
                    sh_rel[r] = P.opw("vector", lambda e, o=sqsumg[:, :], s=ss[:, NH * 129:NH * 130],
                                      b=sqsumg[:, :]: e.tensor_tensor(o, b, s, ALU.add))
                if r + 2 < NCORES:
                    P.waitk("gpsimd", sh_rel[r])
                    sh_tgt[r + 2] = P.dma("gpsimd", shard_st[r % 2][:, :],
                                          cc_out[(r + 2) * 128:(r + 3) * 128, :], ldG)
            sbf_init = P.op("vector", lambda e, o=S_bf[:, :, :], s=S_f32[:, :, :]: e.tensor_copy(o, s),
                            inc=True)

            P.waitk("vector", sh_rel[NCORES - 1])
            for m in range(NM):
                P.op("vector", lambda e, o=sqsum4[:, :, m], s=sqsumg[:, :]: e.tensor_copy(o, s))
            P.opw("vector", lambda e: e.engine_nop())
            P.opw("vector", lambda e, o=prod_sb[:, :, :], a=fnorm_sb[:, :, 0:4], b=sqsum4[:, :, :]:
                  e.tensor_tensor(o, a, b, ALU.mult))
            prod_tok = P.opw("vector", lambda e, o=prod_sb[:, :, :], s=prod_sb[:, :, :]:
                             e.tensor_single_scalar(o, s, REL_SCALE, ALU.mult))
            Bsc.await_free()
            P.waitk("tensor", prod_tok)
            _mm(P, Bsc.t[0:1, 0:64], onescol[:, :], prod_sb[:, :, :], True, True, inc=Bsc.mm)
            Bsc.await_data("vector")
            P.opw("vector", lambda e, o=rowscr[0:1, 64:128], s=Bsc.t[0:1, 0:64]: e.tensor_copy(o, s))
            P.op("vector", lambda e: e.engine_nop(), inc=Bsc.rel("vector"))
            rel = rowscr[0:1, 64:128].rearrange("p (h m) -> p h m", m=4)
            mxe = rowscr[0:1, 160:224].rearrange("p (h m) -> p h m", m=4)
            P.opw("vector", lambda e, o=rowscr[0:1, 128:144], a=rel[:, :, 0], b=rel[:, :, 1]:
                  e.tensor_tensor(o, a, b, ALU.max))
            P.opw("vector", lambda e, o=rowscr[0:1, 144:160], a=rel[:, :, 2], b=rel[:, :, 3]:
                  e.tensor_tensor(o, a, b, ALU.max))
            P.opw("vector", lambda e, o=rowscr[0:1, 128:144], a=rowscr[0:1, 128:144], b=rowscr[0:1, 144:160]:
                  e.tensor_tensor(o, a, b, ALU.max))
            for m in range(NM):
                P.op("vector", lambda e, o=mxe[:, :, m], s=rowscr[0:1, 128:144]: e.tensor_copy(o, s))
            P.opw("vector", lambda e: e.engine_nop())
            sm_tok = P.opw("vector", lambda e, o=rowscr[0:1, 64:128], a=rowscr[0:1, 64:128],
                           b=rowscr[0:1, 160:224]: e.tensor_tensor(o, a, b, ALU.subtract))
            P.waitk("scalar", sm_tok)
            ex_tok = P.op("scalar", lambda e, o=rowscr[0:1, 64:128], s=rowscr[0:1, 64:128]:
                          e.activation(o, s, AF.Exp), inc=True)
            P.waitk("vector", ex_tok)
            P.opw("vector", lambda e, o=rowscr[0:1, 128:144], a=rel[:, :, 0], b=rel[:, :, 1]:
                  e.tensor_tensor(o, a, b, ALU.add))
            P.opw("vector", lambda e, o=rowscr[0:1, 144:160], a=rel[:, :, 2], b=rel[:, :, 3]:
                  e.tensor_tensor(o, a, b, ALU.add))
            P.opw("vector", lambda e, o=rowscr[0:1, 128:144], a=rowscr[0:1, 128:144], b=rowscr[0:1, 144:160]:
                  e.tensor_tensor(o, a, b, ALU.add))
            P.opw("vector", lambda e, o=rowscr[0:1, 144:160], s=rowscr[0:1, 128:144]: e.reciprocal(o, s))
            for m in range(NM):
                P.op("vector", lambda e, o=mxe[:, :, m], s=rowscr[0:1, 144:160]: e.tensor_copy(o, s))
            P.opw("vector", lambda e: e.engine_nop())
            P.opw("vector", lambda e, o=rowscr[0:1, 64:128], a=rowscr[0:1, 64:128], b=rowscr[0:1, 160:224]:
                  e.tensor_tensor(o, a, b, ALU.mult))
            for m in range(NM):
                P.opw("vector", lambda e, o=rel[:, :, m], s=rel[:, :, m], a=rowscr[0:1, 20 + m:21 + m]:
                      e.tensor_scalar(o, s, a, None, ALU.mult))
            wrow_tok = P.mark("vector")
            Bsc.await_free()
            P.waitk("tensor", wrow_tok)
            _mm(P, Bsc.t[:, 0:64], onesrow[0:1, :], rowscr[0:1, 64:128], True, True, inc=Bsc.mm)
            Bsc.await_data("vector")
            P.opw("vector", lambda e, o=coeffm[:, :, :], s=Bsc.t[:, 0:64].rearrange("p (h m) -> p h m", m=4):
                  e.tensor_copy(o, s))
            P.op("vector", lambda e: e.engine_nop(), inc=Bsc.rel("vector"))
            for h in range(NH):
                P.op("vector", lambda e, o=coeffbase[:, h, 0:4], s=coeffm[:, h, :], a=gate_cols[:, h:h + 1]:
                     e.tensor_scalar(o, s, a, None, ALU.mult))
            cbase_tok = P.opw("vector", lambda e, o=coeffbase[:, :, 4], s=gate_cols[:, :], a=zalive_col[:, :]:
                              e.tensor_scalar(o, s, a, None, ALU.mult))

            sqr_tgt = [None] * NCH
            skr_tgt = [None] * NCH
            vr_tgt = [None] * NCH
            rl_rel = [None] * NCH
            pool_rel = [None] * NCH
            dbg_oa_tgt = [None] * NCH
            af_sc_tok = [None] * NCH
            dbg_gate = [None]
            oab_rel = [None] * NCH
            oT_sp_tgt = [None] * NCH
            sbf_tok = [[None] * NH for _ in range(NCH)]

            def plan_reload(c):
                p = c % 2
                if c >= 2:
                    P.waitk("sync", rl_rel[c - 2])
                sqr_tgt[c] = P.dma("sync", sq_rl[p][:, :, :], sq_sp[c], rldS)
                skr_tgt[c] = P.dma("sync", sk_rl[p][:, :, :], sk_sp[c], rldS)
                vr_tgt[c] = P.dma("sync", v_rl[p][:, :, :], v_sp[c], rldS)

            plan_reload(0)
            plan_reload(1)

            def emit_stage1(c):
                p = c % 2
                P.wait("tensor", rldS, sqr_tgt[c])
                if c == 0:
                    P.wait("tensor", ldG, fm_tgt)
                    P.waitk("tensor", msum_tok)
                    P.waitk("tensor", nrhs_tok)
                if c >= 2:
                    P.waitk("scalar", pool_rel[c - 2])
                    P.waitk("vector", pool_rel[c - 2])
                for h in range(NH):
                    Baf.await_free()
                    Bb.await_free()
                    lh = sq_rl[p][:, h, :]
                    _mm(P, Baf.t[:, :], lh, fmem_sb[:, h, 0:512], True, True, inc=Baf.mm)
                    _mm(P, Bb.t[:, 0:128], lh, fmem_sb[:, h, 512:640], True, True)
                    _mm(P, Bb.t[:, 128:133], lh, nrhs_bf[:, h, :], True, True, inc=Bb.mm)
                    Baf.await_data("scalar")
                    P.op("scalar", lambda e, o=af_sb[p][:, h, 0:512], s=Baf.t[:, :]: e.copy(o, s),
                         inc=Baf.rel("scalar"))
                    Bb.await_data("scalar")
                    P.op("scalar", lambda e, o=af_sb[p][:, h, 512:640], s=Bb.t[:, 0:128]: e.copy(o, s),
                         inc=Bb.rel("scalar"))
                    Bb.await_data("vector")
                    tok_n = P.op("vector", lambda e, o=n_all[p][:, h, :], s=Bb.t[:, 128:133]:
                                 e.tensor_copy(o, s), inc=Bb.rel("vector"))
                    if h == NH - 1:
                        P.wait("vector", Bb.rel("vector"), tok_n)
                af_sc_tok[c] = (Bb.rel("scalar"), Bb.rel("scalar").v)

            emit_stage1(0)
            for c in range(NCH):
                p = c % 2
                P.opw("vector", lambda e, o=n_all[p][:, :, :], s=n_all[p][:, :, :]:
                      e.tensor_single_scalar(o, s, EPS, ALU.max))
                P.opw("vector", lambda e, o=r_all[:, :, :], s=n_all[p][:, :, :]: e.reciprocal(o, s))
                if c == 0:
                    P.waitk("vector", cbase_tok)
                crc_tok = P.opw("vector", lambda e, o=crcoefs[:, :], s=r_all[:, :, 4], a=zalive_col[:, :]:
                                e.tensor_scalar(o, s, a, None, ALU.mult))
                P.waitk("vector", af_sc_tok[c])
                cr_tok = None
                for h in range(NH):
                    cr_tok = P.op("vector", lambda e, o=crf[p][:, h * 128:(h + 1) * 128],
                                  s=af_sb[p][:, h, 512:640], a=crcoefs[:, h:h + 1]:
                                  e.scalar_tensor_tensor(o, s, a, o, ALU.mult, ALU.bypass), inc=True)
                crb_tok = cr_tok
                P.wait("tensor", rldS, skr_tgt[c])
                P.wait("tensor", rldS, vr_tgt[c])
                if c > 0:
                    P.waitk("scalar", pool_rel[c - 1])
                for r in range(2):
                    Btr2.await_free()
                    for k in range(8):
                        h = r * 8 + k
                        _tr(P, Btr2.t[:, k * 128:(k + 1) * 128], sk_rl[p][:, h, :], idb[:, :],
                            inc=(Btr2.mm if k == 7 else None))
                    Btr2.await_data("scalar")
                    P.op("scalar", lambda e, d=skc2[:, r * 8:(r + 1) * 8, :],
                         s=Btr2.t[:, :].rearrange("p (k j) -> p k j", j=128): e.copy(d, s),
                         inc=Btr2.rel("scalar"))
                skc2_tgt = Btr2.rel("scalar").v
                P.wait("tensor", Btr2.rel("scalar"), skc2_tgt)
                for h in range(NH):
                    bs_ = Bsc if h % 2 == 0 else Baf
                    bs_.await_free()
                    _mm(P, bs_.t[:, 0:128], sk_rl[p][:, h, :], sq_rl[p][:, h, :], True, True, inc=bs_.mm)
                    bs_.await_data("vector")
                    P.op("vector", lambda e, o=msT[:, h, :], a=bs_.t[:, 0:128], b=mask_sb[:, :]:
                         e.tensor_tensor(o, a, b, ALU.mult), inc=bs_.rel("vector"))
                ms_all = P.mark("vector")
                P.waitk("tensor", ms_all)
                num_tok = None
                for h in range(NH):
                    bn_ = Bnum if h % 2 == 0 else Bb
                    bd_ = Bdel if h % 2 == 0 else Bsc
                    bn_.await_free()
                    if c > 0:
                        P.waitk("tensor", sbf_tok[c - 1][h])
                    elif h == 0:
                        P.waitk("tensor", sbf_init)
                    _mm(P, bn_.t[:, 0:129], msT[:, h, :], v_rl[p][:, h, :], True, False)
                    _mm(P, bn_.t[:, 0:129], sq_rl[p][:, h, :], S_bf[:, h, :], False, True, inc=bn_.mm)
                    bn_.await_data("scalar")
                    P.op("scalar", lambda e, o=den16[:, h:h + 1], s=bn_.t[:, 128:129]:
                         e.copy(o, s))
                    num_tok = P.op("scalar", lambda e, o=num_sb[:, h, :], s=bn_.t[:, 0:128]:
                                   e.copy(o, s), inc=bn_.rel("scalar"))
                    nt_sem = bn_.rel("scalar")
                    bd_.await_free()
                    _mm(P, bd_.t[:, 0:129], skc2[:, h, :], v_rl[p][:, h, :], True, True, inc=bd_.mm)
                    bd_.await_data("vector")
                    su_tok = P.op("vector", lambda e, o=S_f32[:, h, :], s=bd_.t[:, 0:129]:
                                  e.scalar_tensor_tensor(o, s, 1.0, o, ALU.mult, ALU.add),
                                  inc=bd_.rel("vector"))
                    P.wait("scalar", bd_.rel("vector"), su_tok)
                    sbf_tok[c][h] = P.op("scalar", lambda e, o=S_bf[:, h, :], s=S_f32[:, h, :]:
                                         e.copy(o, s), inc=True)
                rl_rel[c] = (Bdel.mm, Bdel.mm.v)
                P.wait("vector", nt_sem, num_tok)
                P.opw("vector", lambda e: e.engine_nop())
                P.opw("vector", lambda e, o=den16[:, :], s=den16[:, :]:
                      e.tensor_single_scalar(o, s, EPS, ALU.max))
                P.opw("vector", lambda e, o=c_loc16[:, :], s=den16[:, :]: e.reciprocal(o, s))
                P.opw("vector", lambda e, o=c_loc16[:, :], a=c_loc16[:, :], b=omgate_cols[:, :]:
                      e.tensor_tensor(o, a, b, ALU.mult))

                if c + 1 < NCH:
                    emit_stage1(c + 1)

                P.waitk("tensor", crb_tok)
                for r in range(2):
                    Btr2.await_free()
                    for k in range(8):
                        f = r * 8 + k
                        _tr(P, Btr2.t[:, k * 128:(k + 1) * 128], crf[p][:, f * 128:(f + 1) * 128],
                            idb[:, :], inc=(Btr2.mm if k == 7 else None))
                    Btr2.await_data("scalar")
                    P.op("scalar", lambda e, d=crT[:, r * 8:(r + 1) * 8, :],
                         s=Btr2.t[:, :].rearrange("p (k j) -> p k j", j=128): e.copy(d, s),
                         inc=Btr2.rel("scalar"))
                crT_tgt = Btr2.rel("scalar").v
                Bsc.await_free()
                P.wait("tensor", Btr2.rel("scalar"), crT_tgt)
                if c == 0:
                    P.wait("tensor", ldG, egw1_tgt)
                    P.wait("tensor", ldG, egw2_tgt)
                for ut in range(4):
                    for f in range(NFT):
                        _mm(P, Bsc.t[:, ut * 128:(ut + 1) * 128], egw1_sb[:, f, ut * 128:(ut + 1) * 128],
                            crT[:, f, :], start=(f == 0), stop=(f == NFT - 1),
                            inc=(Bsc.mm if (ut == 3 and f == NFT - 1) else None))
                Bsc.await_data("scalar")
                for ut in range(4):
                    P.op("scalar", lambda e, o=h1_bf[:, ut, :], s=Bsc.t[:, ut * 128:(ut + 1) * 128],
                         b=egb1_sb[:, ut:ut + 1]: e.activation(o, s, AF.Relu, bias=b),
                         inc=(Bsc.rel("scalar") if ut == 3 else None))
                Bnum.await_free()
                P.wait("tensor", Bsc.rel("scalar"))
                for ut in range(4):
                    _mm(P, Bnum.t[:, 0:1], h1_bf[:, ut, :], egw2_sb[:, ut:ut + 1],
                        start=(ut == 0), stop=(ut == 3), inc=(Bnum.mm if ut == 3 else None))
                Bnum.await_data("scalar")
                gc_tok = P.op("scalar", lambda e, o=g_col[:, :], s=Bnum.t[:, 0:1], b=egb2_sb[:, :]:
                              e.activation(o, s, AF.Sigmoid, bias=b), inc=Bnum.rel("scalar"))
                P.wait("vector", Bnum.rel("scalar"), gc_tok)
                omg_tok = P.opw("vector", lambda e, o=omg_col[:, :], s=g_col[:, :]:
                                e.tensor_scalar(o, s, -1.0, 1.0, ALU.mult, ALU.add))
                P.opw("vector", lambda e, o=coeffm[:, :, :], a=r_all[:, :, 0:4], b=coeffbase[:, :, 0:4]:
                      e.tensor_tensor(o, a, b, ALU.mult))
                P.opw("vector", lambda e, o=coeffm[:, :, :], s=coeffm[:, :, :], a=g_col[:, :]:
                      e.tensor_scalar(o, s, a, None, ALU.mult))
                cm_tok = P.opw("vector", lambda e, o=c2cols[:, :], s=gate_cols[:, :], a=omg_col[:, :]:
                               e.tensor_scalar(o, s, a, None, ALU.mult))

                if c > 0:
                    P.waitk("vector", oab_rel[c - 1])
                    if dbg and dbg_oa_tgt[c - 1]:
                        P.wait("vector", splS, dbg_oa_tgt[c - 1])
                for h in range(NH):
                    P.op("vector", lambda e, o=oa[:, h * 128:(h + 1) * 128], s=num_sb[:, h, :],
                         a=c_loc16[:, h:h + 1]: e.scalar_tensor_tensor(o, s, a, o, ALU.mult, ALU.bypass))
                for m in range(NM):
                    for h in range(NH):
                        P.op("vector", lambda e, o=oa[:, h * 128:(h + 1) * 128],
                             s=af_sb[p][:, h, m * 128:(m + 1) * 128], a=coeffm[:, h, m:m + 1]:
                             e.scalar_tensor_tensor(o, s, a, o, ALU.mult, ALU.add))
                for h in range(NH):
                    P.op("vector", lambda e, o=oa[:, h * 128:(h + 1) * 128],
                         s=crf[p][:, h * 128:(h + 1) * 128],
                         a=c2cols[:, h:h + 1]: e.scalar_tensor_tensor(o, s, a, o, ALU.mult, ALU.add),
                         inc=(True if h == NH - 1 else None))
                pool_rel[c] = (P.ax["vector"], P.ax["vector"].v)

                P.waitk("tensor", pool_rel[c])
                if c >= 1:
                    P.wait("scalar", splS, oT_sp_tgt[c - 1])
                for r in range(2):
                    Btr2.await_free()
                    for k in range(8):
                        f = r * 8 + k
                        _tr(P, Btr2.t[:, k * 128:(k + 1) * 128], oa[:, f * 128:(f + 1) * 128], idb[:, :],
                            inc=(Btr2.mm if k == 7 else None))
                    Btr2.await_data("scalar")
                    P.op("scalar", lambda e, d=oT[:, r * 8:(r + 1) * 8, :],
                         s=Btr2.t[:, :].rearrange("p (k j) -> p k j", j=128): e.copy(d, s),
                         inc=Btr2.rel("scalar"))
                oT_done = Btr2.rel("scalar").v
                oab_rel[c] = (Btr2.mm, Btr2.mm.v)
                P.wait("sync", Btr2.rel("scalar"), oT_done)
                oT_sp_tgt[c] = P.dma("sync", oT_sp[c], oT[:, :, :], splS)
                if dbg:
                    P.waitk("sync", pool_rel[c])
                    dbg_oa_tgt[c] = P.dma("sync", attn_dbg[c], oa[:, :], splS)
                if c + 2 < NCH:
                    plan_reload(c + 2)

        P.barrier(bar, scalar_scratch=rowscr[0:1, 255:256])
        P.wait("gpsimd", splS, splS.v)
        P.wait("sync", splS, splS.v)

        with ExitStack() as ph2b:
            psb = lambda n, s, d=F32: ph2b.enter_context(nc.sbuf_tensor(n, s, d))
            wo_sb = psb("wo_sb", [128, NFT, D], BF16)
            oT_rl = [psb(f"oT_rl{i}", [128, NFT, 128], BF16) for i in range(2)]
            x_res = [psb(f"x_res{i}", [128, D]) for i in range(2)]
            ln_in = psb("ln_in", [128, D])
            ln_sc = psb("ln_sc", [128, D])
            x1f = psb("x1f", [128, D])
            Bw = [Bank(P, ph2b, f"pw{i}", [128, 512], F32) for i in range(8)]
            P.wait("tensor", Btr2.mm)
            for _b in B2 + [Btr2]:
                for _s in _b.ev.values():
                    P.wait("tensor", _s)

            wo_tgt = None
            for f in range(NFT):
                wo_tgt = P.dma("gpsimd", wo_sb[:, f, :], wo_d[f * 128:(f + 1) * 128, :], ldG)

            oTr_tgt = [None] * NCH
            xr_tgt = [None] * NCH
            oTr_rel = [None] * NCH
            xres_rel = [None] * NCH
            x1fS_tgt = [None] * NCH

            def plan_reload2(c):
                p = c % 2
                if c >= 2:
                    P.waitk("sync", oTr_rel[c - 2])
                oTr_tgt[c] = P.dma("sync", oT_rl[p][:, :, :], oT_sp[c], rldS)
                if c >= 2:
                    P.wait("sync", xres_rel[c - 2][0], xres_rel[c - 2][1])
                xr_tgt[c] = P.dma("sync", x_res[p][:, :], x_d[c * 128:(c + 1) * 128, :], rldS)

            plan_reload2(0)
            plan_reload2(1)
            for c in range(NCH):
                p = c % 2
                bwg = Bw[0:4] if c % 2 == 0 else Bw[4:8]
                for b in bwg:
                    b.await_free()
                P.wait("tensor", rldS, oTr_tgt[c])
                if c == 0:
                    P.wait("tensor", ldG, wo_tgt)
                for f in range(NFT):
                    for oc in range(4):
                        _mm(P, bwg[oc].t[:, :], oT_rl[p][:, f, :], wo_sb[:, f, oc * 512:(oc + 1) * 512],
                            start=(f == 0), stop=(f == NFT - 1), inc=(bwg[oc].mm if f == NFT - 1 else None))
                oTr_rel[c] = (bwg[3].mm, bwg[3].mm.v)
                P.wait("vector", rldS, xr_tgt[c])
                for oc in range(4):
                    bwg[oc].await_data("vector")
                    P.op("vector", lambda e, o=ln_in[:, oc * 512:(oc + 1) * 512], a=bwg[oc].t[:, :],
                         b=x_res[p][:, oc * 512:(oc + 1) * 512]: e.tensor_tensor(o, a, b, ALU.add),
                         inc=bwg[oc].rel("vector"))
                xres_rel[c] = (bwg[3].rel("vector"), bwg[3].rel("vector").v)
                P.opw("vector", lambda e: e.engine_nop())
                P.opw("vector", lambda e, o=ln_m[:, :], s=ln_in[:, :]: e.reduce_sum(o, s, axis=AX))
                P.opw("vector", lambda e, o=ln_m[:, :], s=ln_m[:, :]:
                      e.tensor_single_scalar(o, s, 1.0 / D, ALU.mult))
                xm_tok = P.opw("vector", lambda e, o=ln_sc[:, :], s=ln_in[:, :], a=ln_m[:, :]:
                               e.tensor_scalar(o, s, a, None, ALU.subtract))
                P.waitk("scalar", xm_tok)
                sq_tok = P.op("scalar", lambda e, o=ln_in[:, :], s=ln_sc[:, :], acc=ln_v[:, :]:
                              e.activation(o, s, AF.Square, accum_out=acc), inc=True)
                P.waitk("vector", sq_tok)
                P.opw("vector", lambda e, o=ln_v[:, :], s=ln_v[:, :]:
                      e.tensor_single_scalar(o, s, 1.0 / D, ALU.mult))
                P.opw("vector", lambda e, o=ln_v[:, :], s=ln_v[:, :]:
                      e.tensor_single_scalar(o, s, LN_EPS, ALU.add))
                rv_tok = P.opw("vector", lambda e, o=ln_r[:, :], s=ln_v[:, :]: e.reciprocal(o, s))
                P.waitk("scalar", rv_tok)
                rs_tok = P.op("scalar", lambda e, o=ln_r[:, :], s=ln_r[:, :]: e.activation(o, s, AF.Sqrt),
                              inc=True)
                P.waitk("vector", rs_tok)
                if c >= 1:
                    P.wait("vector", splS, x1fS_tgt[c - 1])
                x1_tok = P.op("vector", lambda e, o=x1f[:, :], s=ln_sc[:, :], a=ln_r[:, :]:
                              e.tensor_scalar(o, s, a, None, ALU.mult), inc=True)
                P.waitk("sync", x1_tok)
                x1fS_tgt[c] = P.dma("sync", x1f_sp[c], x1f[:, :], splS)
                if c + 2 < NCH:
                    plan_reload2(c + 2)

        P.barrier(bar, scalar_scratch=rowscr[0:1, 255:256])
        P.wait("gpsimd", splS, splS.v)
        P.wait("gpsimd", splG, splG.v)
        P.wait("sync", splS, splS.v)
        P.wait("sync", splG, splG.v)

        with ExitStack() as ph3:
            psb = lambda n, s, d=F32: ph3.enter_context(nc.sbuf_tensor(n, s, d))
            x1T = psb("x1T", [128, NFT, HALF], BF16)
            acc = psb("acc", [128, 8, D])
            cb = psb("cb", [128, 9, HALF], BF16)
            gs = psb("gs", [128, 4, HALF], BF16)
            u_tmp = psb("u_tmp", [128, 2, HALF], BF16)
            wbuf = [psb(f"wbuf{i}", [128, 8192], BF16) for i in range(2)]
            gsg2 = psb("gsg2", [128, 9])
            comb = psb("comb", [128, 9])
            comb_rT = psb("comb_rT", [9, 8, 128])
            l2_in = psb("l2sc", [128, D])
            sel_sb = psb("sel_sb", [9, 9, 128])
            gsg_sb = psb("gsg_sb", [128, NFT, 9], BF16)
            gsg2b = psb("gsg2b", [128, 9])
            sel_tgt = P.dma("sync", sel_sb[:, :, :], sel_d, rldS)
            gsg_tgt = None
            for f in range(NFT):
                gsg_tgt = P.dma("gpsimd", gsg_sb[:, f, :], gsg_d[f * 128:(f + 1) * 128, :], ldG)

            with ExitStack() as ph3p:
                Bp = [Bank(P, ph3p, f"p3_{i}", [128, 512], F32) for i in range(8)]
                P.wait("tensor", Bw[7].mm)
                for _b in Bw:
                    for _s in _b.ev.values():
                        P.wait("tensor", _s)


                xb_rel = [None] * 8
                acc_tgt = [None] * 8
                dbg_cmb_tgt = [None] * 16

                def emit_half_prep(half):
                    if half == 1:
                        P.wait("sync", stS, stS.v)
                    for tt in range(8):
                        acc_tgt[tt] = P.dma("sync", acc[:, tt, :], x1f_sp[half * 8 + tt], rldS)
                    for tt in range(8):
                        P.wait("tensor", rldS, acc_tgt[tt])
                        for r in range(4):
                            Bt = Bp[2 + (r % 2)]
                            Bt.await_free()
                            for k in range(4):
                                f = r * 4 + k
                                _tr(P, Bt.t[:, k * 128:(k + 1) * 128],
                                    acc[:, tt, f * 128:(f + 1) * 128], idf[:, :],
                                    inc=(Bt.mm if k == 3 else None))
                            Bt.await_data("scalar")
                            P.op("scalar", lambda e, d=x1T[:, r * 4:(r + 1) * 4, tt * 128:(tt + 1) * 128],
                                 s=Bt.t[:, :].rearrange("p (k j) -> p k j", j=128): e.copy(d, s),
                                 inc=Bt.rel("scalar"))
                    x1T_tgt_a = Bp[2].rel("scalar").v
                    x1T_tgt_b = Bp[3].rel("scalar").v
                    P.wait("tensor", Bp[2].rel("scalar"), x1T_tgt_a)
                    P.wait("tensor", Bp[3].rel("scalar"), x1T_tgt_b)
                    if dbg:
                        P.wait("sync", Bp[2].rel("scalar"), x1T_tgt_a)
                        P.wait("sync", Bp[3].rel("scalar"), x1T_tgt_b)
                        x1T_dump = P.dma("sync", dbg_x1T[half], x1T[:, :, :], rldS)
                        P.wait("tensor", rldS, x1T_dump)

                    for tt in range(8):
                        Bg = Bp[tt % 2]
                        Bg.await_free()
                        if tt == 0:
                            P.wait("tensor", ldG, gsg_tgt)
                        for f in range(NFT):
                            _mm(P, Bg.t[:, 0:9], x1T[:, f, tt * 128:(tt + 1) * 128], gsg_sb[:, f, :],
                                start=(f == 0), stop=(f == NFT - 1), inc=(Bg.mm if f == NFT - 1 else None))
                        Bg.await_data("vector")
                        lg_tok = P.opw("vector", lambda e, o=gsg2[:, :], s=Bg.t[:, 0:9]: e.tensor_copy(o, s))
                        if dbg:
                            Bg.await_data("scalar")
                            lg2_tok = P.op("scalar", lambda e, o=gsg2b[:, :], s=Bg.t[:, 0:9]:
                                           e.copy(o, s), inc=True)
                            P.waitk("sync", lg2_tok)
                            lgt2_t = P.dma("sync", dbg_lgt2[half * 8 + tt], gsg2b[:, :], rldS)
                            P.wait("vector", rldS, lgt2_t)
                            P.wait("scalar", rldS, lgt2_t)
                        P.op("vector", lambda e: e.engine_nop(), inc=Bg.rel("vector"))
                        if dbg:
                            P.waitk("sync", lg_tok)
                            lgt_t = P.dma("sync", dbg_lgt[half * 8 + tt], gsg2[:, :], rldS)
                            P.wait("vector", rldS, lgt_t)
                        P.opw("vector", lambda e, o=ln_m[:, :], s=gsg2[:, 0:8]: e.reduce_max(o, s, axis=AX))
                        sm1 = P.opw("vector", lambda e, o=comb[:, 0:8], s=gsg2[:, 0:8], a=ln_m[:, :]:
                                    e.tensor_scalar(o, s, a, None, ALU.subtract))
                        P.waitk("scalar", sm1)
                        sm2 = P.op("scalar", lambda e, o=comb[:, 0:8], s=comb[:, 0:8]:
                                   e.activation(o, s, AF.Exp), inc=True)
                        sg1 = P.op("scalar", lambda e, o=comb[:, 8:9], s=gsg2[:, 8:9]:
                                   e.activation(o, s, AF.Sigmoid), inc=True)
                        P.waitk("vector", sg1)
                        P.opw("vector", lambda e, o=ln_m[:, :], s=comb[:, 0:8]: e.reduce_sum(o, s, axis=AX))
                        P.opw("vector", lambda e, o=ln_v[:, :], s=ln_m[:, :]: e.reciprocal(o, s))
                        P.opw("vector", lambda e, o=comb[:, 0:8], s=comb[:, 0:8], a=ln_v[:, :]:
                              e.tensor_scalar(o, s, a, None, ALU.mult))
                        P.opw("vector", lambda e, o=ln_m[:, :], s=comb[:, 0:8]: e.reduce_max(o, s, axis=AX))
                        P.opw("vector", lambda e, o=gsg2[:, 0:8], s=comb[:, 0:8], a=ln_m[:, :]:
                              e.tensor_scalar(o, s, a, None, ALU.subtract))
                        P.opw("vector", lambda e, o=gsg2[:, 0:8], s=gsg2[:, 0:8]:
                              e.tensor_single_scalar(o, s, 0.0, ALU.is_ge))
                        P.opw("vector", lambda e, o=gsg2[:, 0:8], s=gsg2[:, 0:8], b=comb[:, 0:8]:
                              e.scalar_tensor_tensor(o, s, -1e30, b, ALU.mult, ALU.add))
                        P.opw("vector", lambda e, o=ln_v[:, :], s=gsg2[:, 0:8]: e.reduce_max(o, s, axis=AX))
                        P.opw("vector", lambda e, o=gsg2[:, 0:8], s=comb[:, 0:8], a=ln_v[:, :]:
                              e.tensor_scalar(o, s, a, None, ALU.subtract))
                        P.opw("vector", lambda e, o=gsg2[:, 0:8], s=gsg2[:, 0:8]:
                              e.tensor_single_scalar(o, s, 0.0, ALU.is_ge))
                        P.opw("vector", lambda e, o=ln_m[:, :], a=ln_m[:, :], b=ln_v[:, :]:
                              e.tensor_tensor(o, a, b, ALU.add))
                        P.opw("vector", lambda e, o=ln_v[:, :], s=ln_m[:, :]: e.reciprocal(o, s))
                        P.opw("vector", lambda e, o=comb[:, 0:8], a=comb[:, 0:8], b=gsg2[:, 0:8]:
                              e.tensor_tensor(o, a, b, ALU.mult))
                        cmb = P.opw("vector", lambda e, o=comb[:, 0:8], s=comb[:, 0:8], a=ln_v[:, :]:
                                    e.tensor_scalar(o, s, a, None, ALU.mult))
                        if dbg:
                            P.waitk("sync", cmb)
                            dbg_cmb_tgt[half * 8 + tt] = P.dma("sync", dbg_cmb[half * 8 + tt],
                                                               comb[:, :], rldS)
                        Bt = Bp[2]
                        Bt.await_free()
                        P.waitk("tensor", cmb)
                        if dbg:
                            P.wait("tensor", rldS, dbg_cmb_tgt[half * 8 + tt])
                        _tr(P, Bt.t[0:9, 0:128], comb[:, 0:9], idf[:, :], inc=Bt.mm)
                        Bt.await_data("vector")
                        P.op("vector", lambda e, o=comb_rT[:, tt, :], s=Bt.t[0:9, 0:128]:
                             e.tensor_copy(o, s), inc=Bt.rel("vector"))
                    if dbg:
                        gl_done = P.mark("vector")
                        P.waitk("sync", gl_done)
                        x1T_d2 = P.dma("sync", dbg_x1T2[half], x1T[:, :, :], rldS)
                        P.wait("tensor", rldS, x1T_d2)
                    crT_done = Bp[2].rel("vector").v
                    for e9 in range(9):
                        for tc in range(2):
                            Bt = Bp[3]
                            Bt.await_free()
                            if e9 == 0 and tc == 0:
                                P.wait("tensor", Bp[2].rel("vector"), crT_done)
                                P.wait("tensor", rldS, sel_tgt)
                            _mm(P, Bt.t[:, :], sel_sb[:, e9, :], comb_rT[:, tc * 4:(tc + 1) * 4, :],
                                True, True, inc=Bt.mm)
                            Bt.await_data("vector")
                            P.op("vector", lambda e, o=cb[:, e9, tc * 512:(tc + 1) * 512], s=Bt.t[:, :]:
                                 e.tensor_copy(o, s), inc=Bt.rel("vector"))
                    P.opw("vector", lambda e: e.engine_nop())
                    if dbg:
                        Bx = Bp[1]
                        Bx.await_free()
                        for f in range(NFT):
                            _mm(P, Bx.t[:, 0:9], x1T[:, f, 0:128], gsg_sb[:, f, :],
                                start=(f == 0), stop=(f == NFT - 1), inc=(Bx.mm if f == NFT - 1 else None))
                        Bx.await_data("vector")
                        l3 = P.opw("vector", lambda e, o=gsg2[:, :], s=Bx.t[:, 0:9]: e.tensor_copy(o, s))
                        P.op("vector", lambda e: e.engine_nop(), inc=Bx.rel("vector"))
                        P.waitk("sync", l3)
                        l3t = P.dma("sync", dbg_lgt3[half], gsg2[:, :], rldS)
                        P.wait("vector", rldS, l3t)

                def emit_half_experts(half):
                    subs = []
                    for e9 in range(E_MOE):
                        subs += [(e9, 0, 512), (e9, 512, 512), (e9, 1024, 384)]
                    subs += [(8, g * 512, 512) for g in range(11)]
                    wload_tgt = {}

                    def load_gu(which, sub_idx, slot):
                        e9, off, sz = subs[sub_idx]
                        if e9 < E_MOE:
                            src = (wg_d if which == "g" else wu_d)
                            aps = [src[e9, f * 128:(f + 1) * 128, off:off + sz] for f in range(NFT)]
                        else:
                            src = (swg_d if which == "g" else swu_d)
                            aps = [src[f * 128:(f + 1) * 128, off:off + sz] for f in range(NFT)]
                        tg = None
                        for f in range(NFT):
                            tg = P.dma("gpsimd", wbuf[slot][:, f * 512:f * 512 + sz], aps[f], ldG)
                        wload_tgt[(which, sub_idx)] = tg

                    def load_d(sub_idx, slot):
                        e9, off, sz = subs[sub_idx]
                        nit = sz // 128
                        tg = None
                        for it in range(nit):
                            if e9 < E_MOE:
                                ap = wd_d[e9, off + it * 128: off + (it + 1) * 128, :]
                            else:
                                ap = swd_d[off + it * 128: off + (it + 1) * 128, :]
                            tg = P.dma("gpsimd", wbuf[slot][:, it * D:(it + 1) * D], ap, ldG)
                        wload_tgt[("d", sub_idx)] = tg

                    load_gu("g", 0, 0)
                    load_gu("u", 0, 1)
                    hm_prev = [None, None]

                    for si, (e9, off, sz) in enumerate(subs):
                        nit = sz // 128
                        sg_slot = (3 * si) % 2
                        su_slot = (3 * si + 1) % 2
                        sd_slot = (3 * si + 2) % 2

                        P.wait("tensor", ldG, wload_tgt[("g", si)])
                        for it in range(nit):
                            bp = [Bp[0], Bp[1]] if it % 2 == 0 else [Bp[2], Bp[3]]
                            for b in bp:
                                b.await_free()
                            for f in range(NFT):
                                lh = wbuf[sg_slot][:, f * 512 + it * 128: f * 512 + (it + 1) * 128]
                                _mm(P, bp[0].t[:, :], lh, x1T[:, f, 0:512], start=(f == 0),
                                    stop=(f == NFT - 1), inc=(bp[0].mm if f == NFT - 1 else None))
                                _mm(P, bp[1].t[:, :], lh, x1T[:, f, 512:1024], start=(f == 0),
                                    stop=(f == NFT - 1), inc=(bp[1].mm if f == NFT - 1 else None))
                            for tc in range(2):
                                bp[tc].await_data("scalar")
                                P.op("scalar", lambda e, o=gs[:, it, tc * 512:(tc + 1) * 512],
                                     s=bp[tc].t[:, :]: e.activation(o, s, AF.Silu), inc=bp[tc].rel("scalar"))
                        lb = Bp[1] if nit % 2 == 1 else Bp[3]
                        g_done = (lb.mm, lb.mm.v)
                        P.waitk("gpsimd", g_done)
                        load_d(si, sg_slot)

                        P.wait("tensor", ldG, wload_tgt[("u", si)])
                        for it in range(nit):
                            bp = [Bp[0], Bp[1]] if it % 2 == 0 else [Bp[2], Bp[3]]
                            for b in bp:
                                b.await_free()
                            for f in range(NFT):
                                lh = wbuf[su_slot][:, f * 512 + it * 128: f * 512 + (it + 1) * 128]
                                _mm(P, bp[0].t[:, :], lh, x1T[:, f, 0:512], start=(f == 0),
                                    stop=(f == NFT - 1), inc=(bp[0].mm if f == NFT - 1 else None))
                                _mm(P, bp[1].t[:, :], lh, x1T[:, f, 512:1024], start=(f == 0),
                                    stop=(f == NFT - 1), inc=(bp[1].mm if f == NFT - 1 else None))
                            for tc in range(2):
                                bp[tc].await_data("vector")
                                if hm_prev[it % 2] is not None:
                                    P.waitk("vector", hm_prev[it % 2])
                                ut = P.op("vector", lambda e, o=u_tmp[:, it % 2, tc * 512:(tc + 1) * 512],
                                          a=bp[tc].t[:, :], b=cb[:, e9, tc * 512:(tc + 1) * 512]:
                                          e.tensor_tensor(o, a, b, ALU.mult), inc=bp[tc].rel("vector"))
                                P.wait("gpsimd", bp[tc].rel("vector"), ut)
                                hm_prev[it % 2] = P.op("gpsimd", lambda e, o=gs[:, it, tc * 512:(tc + 1) * 512],
                                     a=gs[:, it, tc * 512:(tc + 1) * 512],
                                     b=u_tmp[:, it % 2, tc * 512:(tc + 1) * 512]:
                                     e.tensor_tensor(o, a, b, ALU.mult), inc=True)
                        lb = Bp[1] if nit % 2 == 1 else Bp[3]
                        u_done = (lb.mm, lb.mm.v)
                        P.waitk("gpsimd", u_done)
                        if si + 1 < len(subs):
                            load_gu("g", si + 1, su_slot)
                        h_done = P.mark("gpsimd")

                        P.wait("tensor", ldG, wload_tgt[("d", si)])
                        P.waitk("tensor", h_done)
                        for tt in range(8):
                            bq = [Bp[4], Bp[5], Bp[6], Bp[7]] if tt % 2 == 0 else [Bp[0], Bp[1], Bp[2], Bp[3]]
                            for b in bq:
                                b.await_free()
                            for it in range(nit):
                                lh = gs[:, it, tt * 128:(tt + 1) * 128]
                                for oc in range(4):
                                    _mm(P, bq[oc].t[:, :], lh,
                                        wbuf[sd_slot][:, it * D + oc * 512: it * D + (oc + 1) * 512],
                                        start=(it == 0), stop=(it == nit - 1),
                                        inc=(bq[oc].mm if it == nit - 1 else None))
                            if si == 0:
                                P.wait("vector", rldS, acc_tgt[tt])
                            for oc in range(4):
                                bq[oc].await_data("vector")
                                P.op("vector", lambda e, o=acc[:, tt, oc * 512:(oc + 1) * 512],
                                     a=acc[:, tt, oc * 512:(oc + 1) * 512], b=bq[oc].t[:, :]:
                                     e.tensor_tensor(o, a, b, ALU.add), inc=bq[oc].rel("vector"))
                        d_done = (bq[3].mm, bq[3].mm.v)
                        P.waitk("gpsimd", d_done)
                        if si + 1 < len(subs):
                            load_gu("u", si + 1, sd_slot)

                def emit_half_ln(half):
                    if dbg and half == 0:
                        acc_done = P.mark("vector")
                        P.waitk("sync", acc_done)
                        tgt = P.dma("sync", dbg_acc, acc[:, 0, :], rldS)
                        P.wait("vector", rldS, tgt)
                    for tt in range(8):
                        P.opw("vector", lambda e: e.engine_nop())
                        P.opw("vector", lambda e, o=ln_m[:, :], s=acc[:, tt, :]: e.reduce_sum(o, s, axis=AX))
                        P.opw("vector", lambda e, o=ln_m[:, :], s=ln_m[:, :]:
                              e.tensor_single_scalar(o, s, 1.0 / D, ALU.mult))
                        xm = P.opw("vector", lambda e, o=l2_in[:, :], s=acc[:, tt, :], a=ln_m[:, :]:
                                   e.tensor_scalar(o, s, a, None, ALU.subtract))
                        P.waitk("scalar", xm)
                        sq2 = P.op("scalar", lambda e, o=acc[:, tt, :], s=l2_in[:, :], a2=ln_v[:, :]:
                                   e.activation(o, s, AF.Square, accum_out=a2), inc=True)
                        P.waitk("vector", sq2)
                        P.opw("vector", lambda e, o=ln_v[:, :], s=ln_v[:, :]:
                              e.tensor_single_scalar(o, s, 1.0 / D, ALU.mult))
                        P.opw("vector", lambda e, o=ln_v[:, :], s=ln_v[:, :]:
                              e.tensor_single_scalar(o, s, LN_EPS, ALU.add))
                        rv = P.opw("vector", lambda e, o=ln_r[:, :], s=ln_v[:, :]: e.reciprocal(o, s))
                        P.waitk("scalar", rv)
                        rs = P.op("scalar", lambda e, o=ln_r[:, :], s=ln_r[:, :]:
                                  e.activation(o, s, AF.Sqrt), inc=True)
                        P.waitk("vector", rs)
                        ob = P.op("vector", lambda e, o=acc[:, tt, :], s=l2_in[:, :], a=ln_r[:, :]:
                                  e.tensor_scalar(o, s, a, None, ALU.mult), inc=True)
                        P.waitk("sync", ob)
                        P.dma("sync", out_d[(half * 8 + tt) * 128:(half * 8 + tt + 1) * 128, :],
                              acc[:, tt, :], stS)

                for half in range(2):
                    emit_half_prep(half)
                    emit_half_experts(half)
                    emit_half_ln(half)
                    if half == 0:
                        P.barrier(bar, scalar_scratch=rowscr[0:1, 255:256])

        P.wait("sync", stS, stS.v)

        with nc.Block() as block:
            @block.sync
            def _(e):
                for fn in P.q["sync"]:
                    fn(e)

            @block.tensor
            def _(e):
                for fn in P.q["tensor"]:
                    fn(e)

            @block.vector
            def _(e):
                for fn in P.q["vector"]:
                    fn(e)

            @block.scalar
            def _(e):
                for fn in P.q["scalar"]:
                    fn(e)

            @block.gpsimd
            def _(e):
                for fn in P.q["gpsimd"]:
                    fn(e)

    return nc



def make_in_maps(inputs):
    f32 = np.float32
    asf = lambda a: np.ascontiguousarray(np.asarray(a), dtype=f32)
    x_full = asf(inputs["L0_attn_hidden_states"]).reshape(S_TOT, D)
    wq_t = np.ascontiguousarray(asf(inputs["L0_attn_w_q"]).T)
    wk_t = np.ascontiguousarray(asf(inputs["L0_attn_w_k"]).T)
    wv_t = np.ascontiguousarray(asf(inputs["L0_attn_w_v"]).T)
    wo_t = np.ascontiguousarray(asf(inputs["L0_attn_w_o"]).T)
    egw1_t = np.ascontiguousarray(asf(inputs["L0_attn_eg_w1"]).T)
    egb1_c = np.ascontiguousarray(asf(inputs["L0_attn_eg_b1"]).reshape(4, 128).T)
    egw2_c = np.ascontiguousarray(asf(inputs["L0_attn_eg_w2"]).reshape(4, 128).T)
    egb2_c = asf(inputs["L0_attn_eg_b2"]).reshape(1, 1)
    memgate = asf(inputs["L0_attn_memory_gate"]).reshape(1, NH)
    fmem = asf(inputs["L0_attn_fine_memories"])
    fn = asf(inputs["L0_attn_fine_memory_norms"])
    fnorm_c = np.zeros((128, NH, 5), f32)
    for m in range(NM):
        fnorm_c[:, :, m] = fn[m].T
    fnorm_c = np.ascontiguousarray(fnorm_c.reshape(128, NH * 5))
    gsg_t = np.ascontiguousarray(
        np.concatenate([asf(inputs["L0_ffn_gate_w"]).T, asf(inputs["L0_ffn_shared_gate_w"])], axis=1))
    wg = asf(inputs["L0_ffn_w_gate"])
    wu = asf(inputs["L0_ffn_w_up"])
    wd = asf(inputs["L0_ffn_w_down"])
    swg = asf(inputs["L0_ffn_sw_gate"])
    swu = asf(inputs["L0_ffn_sw_up"])
    swd = asf(inputs["L0_ffn_sw_down"])

    ident_bf = np.eye(128, dtype=BF16NP)
    ident_f32 = np.eye(128, dtype=f32)
    maskT = np.triu(np.ones((128, 128), f32))
    ones_row = np.ones((1, 128), f32)
    ones_col = np.ones((128, 1), f32)
    sel9 = np.zeros((9, 9 * 128), f32)
    for e in range(9):
        sel9[e, e * 128:(e + 1) * 128] = 1.0

    in_maps = []
    for cid in range(NCORES):
        pm = np.zeros((128, NCORES), f32)
        pm[:, :cid] = 1.0
        in_maps.append({
            "x": np.ascontiguousarray(x_full[cid * TPC:(cid + 1) * TPC, :]),
            "wq_t": wq_t, "wk_t": wk_t, "wv_t": wv_t, "wo_t": wo_t,
            "egw1_t": egw1_t, "egb1_c": egb1_c, "egw2_c": egw2_c, "egb2_c": egb2_c,
            "memgate": memgate, "fmem": fmem, "fnorm_c": fnorm_c, "gsg_t": gsg_t,
            "wg": wg, "wu": wu, "wd": wd, "swg": swg, "swu": swu, "swd": swd,
            "ident_bf": ident_bf, "ident_f32": ident_f32, "maskT": maskT,
            "ones_row": ones_row, "ones_col": ones_col, "sel9": sel9, "pmask": pm,
        })
    return in_maps


def kernel(**inputs):
    nc = build(dbg=False)
    in_maps = make_in_maps(inputs)
    res = run_bass_kernel_spmd(nc, in_maps, core_ids=list(range(NCORES)))
    out = np.concatenate([res.results[c]["out"] for c in range(NCORES)], axis=0)
    return out.reshape(1, S_TOT, D).astype(np.float32)


# version 08 (final)
# speedup vs baseline: 1.0737x; 1.0630x over previous
\

import numpy as np
import ml_dtypes
from contextlib import ExitStack

import concourse.bass as bass
import concourse.mybir as mybir
from concourse.bass_utils import run_bass_kernel_spmd

F32 = mybir.dt.float32
BF16 = mybir.dt.bfloat16
AF = mybir.ActivationFunctionType
ALU = mybir.AluOpType
AX = mybir.AxisListType.X
BF16NP = ml_dtypes.bfloat16

NCORES = 8
S_TOT = 16384
D = 2048
NH, HD, NM = 16, 128, 4
TPC = S_TOT // NCORES
NCH = TPC // 128
NFT = D // 128
EPS = 1e-6
LN_EPS = 1e-5
E_MOE, I_MOE, I_SH = 8, 1408, 5632
HALF = TPC // 2
REL_SCALE = 1.0 / float(S_TOT)

ENGINES = ("sync", "tensor", "vector", "scalar", "gpsimd")


class Sem:
    __slots__ = ("h", "v", "name")

    def __init__(self, h, name):
        self.h = h
        self.v = 0
        self.name = name


class Planner:
    def __init__(self, nc, ctx):
        self.nc = nc
        self.ctx = ctx
        self.q = {e: [] for e in ENGINES}
        self.ax = {e: self.sem("ax_" + e) for e in ENGINES}

    def sem(self, name):
        return Sem(self.ctx.enter_context(self.nc.semaphore(name)), name)

    def wait(self, eng, sem, val=None):
        v = sem.v if val is None else val
        if v is None or v <= 0:
            return
        self.q[eng].append(lambda e, h=sem.h, v=v: e.wait_ge(h, v))

    def waitk(self, eng, tok):
        if tok is None:
            return
        self.wait(eng, tok[0], tok[1])

    def op(self, eng, fn, inc=None, by=1):
        if inc is True:
            inc = self.ax[eng]
            ret_tok = True
        else:
            ret_tok = False
        if inc is not None:
            inc.v += by

            def g(e, fn=fn, h=inc.h, by=by):
                fn(e).then_inc(h, by)
        else:
            def g(e, fn=fn):
                fn(e)
        self.q[eng].append(g)
        if inc is None:
            return None
        return (inc, inc.v) if ret_tok else inc.v

    def opw(self, eng, fn):
        tok = self.op(eng, fn, inc=True)
        self.waitk(eng, tok)
        return tok

    def mark(self, eng):
        if eng == "vector":
            return self.op(eng, lambda e: e.engine_nop(), inc=True)
        return self.op(eng, lambda e: e.nop(), inc=True)

    def dma(self, eng, out, in_, sem, **kw):
        return self.op(eng, lambda e, o=out, i=in_, kw=dict(kw): e.dma_start(out=o, in_=i, **kw), inc=sem, by=16)

    def barrier(self, bar, scalar_scratch=None):
        self.op("vector", lambda e: e.engine_nop(), inc=bar)
        self.op("gpsimd", lambda e: e.engine_nop(), inc=bar)
        if scalar_scratch is not None:
            self.op("scalar", lambda e, o=scalar_scratch: e.copy(o, o), inc=bar)
        else:
            self.op("scalar", lambda e, h=bar.h: e.sem_inc(h, 1))
            bar.v += 1
        self.op("tensor", lambda e, h=bar.h: e.sem_inc(h, 1))
        self.op("sync", lambda e, h=bar.h: e.sem_inc(h, 1))
        bar.v += 2
        tgt = bar.v
        for eng in ENGINES:
            self.wait(eng, bar, tgt)


class Bank:

    def __init__(self, P, stack, name, shape, dtype):
        self.t = stack.enter_context(P.nc.psum_tensor(name, shape, dtype))
        self.mm = P.sem(name + "_mm")
        self.ev = {}
        self.name = name
        self.P = P

    def _ev(self, eng):
        if eng not in self.ev:
            self.ev[eng] = self.P.sem(self.name + "_ev_" + eng)
        return self.ev[eng]

    def await_free(self, eng="tensor"):
        for s in self.ev.values():
            self.P.wait(eng, s)

    def await_data(self, eng):
        self.P.wait(eng, self.mm)

    def rel(self, eng):
        return self._ev(eng)


def _mm(P, out, lhsT, rhs, start, stop, inc=None):
    return P.op("tensor", lambda e, o=out, l=lhsT, r=rhs, s=start, t=stop:
                e.matmul(o, l, r, start=s, stop=t), inc=inc)


def _tr(P, out, in_, ident, inc=None):
    return P.op("tensor", lambda e, o=out, i=in_, d=ident: e.transpose(o, i, d), inc=inc)


def build(dbg=False):
    nc = bass.Bass("TRN2", target_bir_lowering=False)

    def din(name, shape, dt=F32):
        return nc.dram_tensor(name, shape, dt, kind="ExternalInput").ap()

    def dout(name, shape, dt=F32):
        return nc.dram_tensor(name, shape, dt, kind="ExternalOutput").ap()

    def dint(name, shape, dt=F32):
        k = "ExternalOutput" if dbg else "Internal"
        return nc.dram_tensor(name, shape, dt, kind=k).ap()

    x_d = din("x", [TPC, D])
    wq_d = din("wq_t", [D, D])
    wk_d = din("wk_t", [D, D])
    wv_d = din("wv_t", [D, D])
    wo_d = din("wo_t", [D, D])
    egw1_d = din("egw1_t", [D, 512])
    egb1_d = din("egb1_c", [128, 4])
    egw2_d = din("egw2_c", [128, 4])
    egb2_d = din("egb2_c", [1, 1])
    memgate_d = din("memgate", [1, NH])
    fmem_d = din("fmem", [NM, NH, HD, HD])
    fnorm_d = din("fnorm_c", [128, NH * 5])
    gsg_d = din("gsg_t", [D, 9])
    wg_d = din("wg", [E_MOE, D, I_MOE])
    wu_d = din("wu", [E_MOE, D, I_MOE])
    wd_d = din("wd", [E_MOE, I_MOE, D])
    swg_d = din("swg", [D, I_SH])
    swu_d = din("swu", [D, I_SH])
    swd_d = din("swd", [I_SH, D])
    idb_d = din("ident_bf", [128, 128], BF16)
    idf_d = din("ident_f32", [128, 128])
    mask_d = din("maskT", [128, 128])
    onesrow_d = din("ones_row", [1, 128])
    onescol_d = din("ones_col", [128, 1])
    sel_d = din("sel9", [9, 9 * 128])
    pmask_d = din("pmask", [128, NCORES])

    out_d = dout("out", [TPC, D])

    sq_sp = dint("sq_sp", [NCH, 128, NH * 128], BF16)
    sk_sp = dint("sk_sp", [NCH, 128, NH * 128], BF16)
    v_sp = dint("v_sp", [NCH, 128, NH * 129], BF16)
    oT_sp = dint("oT_sp", [NCH, 128, NH * 128], BF16)
    x1f_sp = dint("x1f_sp", [NCH, 128, D])
    attn_dbg = dout("attn_dbg", [NCH, 128, D], BF16) if dbg else None
    if dbg:
        dbg_n = dout("dbg_n", [128, NH * 5])
        dbg_cm = dout("dbg_cm", [128, NH * 4])
        dbg_crf = dout("dbg_crf", [128, D], BF16)
        dbg_af = dout("dbg_af", [128, NH * 640], BF16)
        dbg_S = dout("dbg_S", [128, NH * 129])
        dbg_misc = dout("dbg_misc", [128, 128])
        dbg_sqrl = dout("dbg_sqrl", [128, NH * 128], BF16)
        dbg_cmb = dout("dbg_cmb", [16, 128, 9])
        dbg_lgt = dout("dbg_lgt", [16, 128, 9])
        dbg_x1T = dout("dbg_x1T", [2, 128, NFT * HALF], BF16)
        dbg_x1T2 = dout("dbg_x1T2", [2, 128, NFT * HALF], BF16)
        dbg_lgt2 = dout("dbg_lgt2", [16, 128, 9])
        dbg_lgt3 = dout("dbg_lgt3", [2, 128, 9])
        dbg_acc = dout("dbg_acc", [128, D])
        dbg_fm = dout("dbg_fm", [128, NH * 640], BF16)

    cc_in = nc.dram_tensor("cc_in", [128, NH * 130], F32).ap()
    cc_out = nc.dram_tensor("cc_out", [NCORES * 128, NH * 130], F32).ap()

    root = ExitStack()
    with root as ctx:
        P = Planner(nc, ctx)
        sb = lambda n, s, d=F32: ctx.enter_context(nc.sbuf_tensor(n, s, d))

        idb = sb("idb", [128, 128], BF16)
        idf = sb("idf", [128, 128])
        mask_sb = sb("mask_sb", [128, 128])
        onesrow = sb("onesrow", [1, 128])
        onescol = sb("onescol", [128, 1])
        pmask_sb = sb("pmask_sb", [128, NCORES])
        memgate_sb = sb("memgate_sb", [1, NH])
        egb1_sb = sb("egb1_sb", [128, 4])
        egb2_sb = sb("egb2_sb", [128, 1])
        fnorm_sb = sb("fnorm_sb", [128, NH, 5])
        nrhs_bf = sb("nrhs_bf", [128, NH, 5], BF16)
        S_f32 = sb("S_f32", [128, NH, 129])
        S_bf = sb("S_bf", [128, NH, 129], BF16)
        sq_own = sb("sq_own", [128, NH])
        sqsumg = sb("sqsumg", [128, NH])
        sqsum4 = sb("sqsum4", [128, NH, 4])
        prod_sb = sb("prod_sb", [128, NH, 4])
        gate_cols = sb("gate_cols", [128, NH])
        omgate_cols = sb("omgate_cols", [128, NH])
        coeffbase = sb("coeffbase", [128, NH, 5])
        zalive_col = sb("zalive_col", [128, 1])
        rowscr = sb("rowscr", [1, 512])
        g_col = sb("g_col", [128, 1])
        omg_col = sb("omg_col", [128, 1])
        c_loc = sb("c_loc", [128, 1])
        den_t = sb("den_tmp", [128, 1])
        crcoefs = sb("crcoefs", [128, NH])
        c2cols = sb("c2cols", [128, NH])
        n_all = [sb(f"n_all{i}", [128, NH, 5]) for i in range(2)]
        r_all = sb("r_all", [128, NH, 5])
        coeffm = sb("coeffm", [128, NH, 4])
        ln_m = sb("ln_m", [128, 1])
        ln_v = sb("ln_v", [128, 1])
        ln_r = sb("ln_r", [128, 1])

        ldS = P.sem("ldS")
        splS = P.sem("splS")
        rldS = P.sem("rldS")
        stS = P.sem("stS")
        ldG = P.sem("ldG")
        splG = P.sem("splG")
        ccs = P.sem("ccs")
        bar = P.sem("bar")

        for dst, src in [(idb[:, :], idb_d), (idf[:, :], idf_d), (mask_sb[:, :], mask_d),
                         (onesrow[:, :], onesrow_d), (onescol[:, :], onescol_d),
                         (pmask_sb[:, :], pmask_d),
                         (memgate_sb[:, :], memgate_d), (egb1_sb[:, :], egb1_d),
                         (fnorm_sb[:, :, :], fnorm_d), (rowscr[0:1, 0:1], egb2_d)]:
            P.dma("sync", dst, src, ldS)
        ld0_done = ldS.v

        with ExitStack() as ph1:
            psb = lambda n, s, d=F32: ph1.enter_context(nc.sbuf_tensor(n, s, d))
            xT = psb("xT", [128, NFT, TPC], BF16)
            A = psb("A", [128, NFT, TPC], BF16)
            wsb = psb("wsb", [128, NFT, 1024], BF16)
            xstage = [psb(f"xst{i}", [128, D], BF16) for i in range(2)]
            v_stage = [psb(f"vst{i}", [128, 8, 129], BF16) for i in range(2)]
            skc_st = psb("skc_st", [128, 8, 128], BF16)
            elu_t1 = [psb(f"elu1_{i}", [128, 512]) for i in range(4)]

            B = [Bank(P, ph1, f"ps{i}", [128, 512], F32) for i in range(7)]
            Btr = Bank(P, ph1, "pstr", [128, 1024], BF16)

            x_tgt = [None] * NCH
            x_tgt[0] = P.dma("gpsimd", xstage[0][:, :], x_d[0:128, :], ldG)
            x_tgt[1] = P.dma("gpsimd", xstage[1][:, :], x_d[128:256, :], ldG)
            for t in range(NCH):
                P.wait("tensor", ldG, x_tgt[t])
                for r in range(2):
                    Btr.await_free()
                    for k in range(8):
                        f = r * 8 + k
                        _tr(P, Btr.t[:, k * 128:(k + 1) * 128], xstage[t % 2][:, f * 128:(f + 1) * 128],
                            idb[:, :], inc=(Btr.mm if k == 7 else None))
                    Btr.await_data("scalar")
                    P.op("scalar", lambda e, d=xT[:, r * 8:(r + 1) * 8, t * 128:(t + 1) * 128],
                         s=Btr.t[:, :].rearrange("p (k j) -> p k j", j=128): e.copy(d, s),
                         inc=Btr.rel("scalar"))
                tok = (Btr.mm, Btr.mm.v)
                if t + 2 < NCH:
                    P.waitk("gpsimd", tok)
                    x_tgt[t + 2] = P.dma("gpsimd", xstage[t % 2][:, :],
                                         x_d[(t + 2) * 128:(t + 3) * 128, :], ldG)

            def load_w_half(dram, hw):
                tg = None
                for f in range(NFT):
                    tg = P.dma("gpsimd", wsb[:, f, :],
                               dram[f * 128:(f + 1) * 128, hw * 1024:(hw + 1) * 1024], ldG)
                return tg

            def proj_featmajor(dst, dram, prev_rel):
                P.wait("tensor", Btr.rel("scalar"))
                for hw in range(2):
                    if hw == 0 and prev_rel is not None:
                        P.waitk("gpsimd", prev_rel)
                    w_tgt = load_w_half(dram, hw)
                    for il in range(8):
                        i = hw * 8 + il
                        bs = B[0:4] if (il % 2 == 0) else [B[4], B[5], B[6], B[0]]
                        for b in bs:
                            b.await_free()
                        if il == 0:
                            P.wait("tensor", ldG, w_tgt)
                        for f in range(NFT):
                            for tc in range(4):
                                _mm(P, bs[tc].t[:, :], wsb[:, f, il * 128:(il + 1) * 128],
                                    xT[:, f, tc * 512:(tc + 1) * 512], start=(f == 0),
                                    stop=(f == NFT - 1), inc=(bs[tc].mm if (f == NFT - 1) else None))
                        t1 = None
                        for tc in range(4):
                            b = bs[tc]
                            b.await_data("vector")
                            t1 = P.op("vector", lambda e, o=elu_t1[tc][:, :], s=b.t[:, :]:
                                      e.tensor_single_scalar(o, s, 0.0, ALU.min), inc=True)
                        P.waitk("scalar", t1)
                        t2 = None
                        for tc in range(4):
                            t2 = P.op("scalar", lambda e, o=elu_t1[tc][:, :], s=elu_t1[tc][:, :]:
                                      e.activation(o, s, AF.Exp), inc=True)
                        P.waitk("vector", t2)
                        for tc in range(4):
                            b = bs[tc]
                            dslice = dst[:, i, tc * 512:(tc + 1) * 512]
                            P.op("vector", lambda e, o=dslice, s=b.t[:, :], x2=elu_t1[tc][:, :]:
                                 e.scalar_tensor_tensor(o, s, 0.0, x2, ALU.max, ALU.add),
                                 inc=b.rel("vector"))
                    if hw == 0:
                        lb = B[0]
                        half_rel = (lb.mm, lb.mm.v)
                        P.waitk("gpsimd", half_rel)
                lb = B[0]
                return (lb.mm, lb.mm.v)

            xT_done = P.ax["scalar"].v
            wk_rel = proj_featmajor(A, wk_d, None)
            sk_wr = P.mark("vector")
            P.waitk("tensor", sk_wr)

            P.waitk("sync", sk_wr)
            sk_sp_tgt = [None] * NCH
            for c in range(NCH):
                sk_sp_tgt[c] = P.dma("sync", sk_sp[c], A[:, :, c * 128:(c + 1) * 128], splS)

            for i in range(2):
                P.op("vector", lambda e, o=v_stage[i][:, :, 128:129]: e.memset(o, 1.0))
            vinit = P.op("vector", lambda e, o=S_f32[:, :, :]: e.memset(o, 0.0), inc=True)
            P.waitk("tensor", vinit)
            P.waitk("sync", vinit)

            prev_rel = wk_rel
            last_vspill = {0: None, 1: None}
            for hw in range(2):
                P.waitk("gpsimd", prev_rel)
                wv_tgt = load_w_half(wv_d, hw)
                for t in range(NCH):
                    par = t % 2
                    bs = B[0:2] if (t % 2 == 0) else B[2:4]
                    for b in bs:
                        b.await_free()
                    if t == 0:
                        P.wait("tensor", ldG, wv_tgt)
                    for f in range(NFT):
                        for oc in range(2):
                            _mm(P, bs[oc].t[:, :], xT[:, f, t * 128:(t + 1) * 128],
                                wsb[:, f, oc * 512:(oc + 1) * 512], start=(f == 0),
                                stop=(f == NFT - 1), inc=(bs[oc].mm if f == NFT - 1 else None))
                    if last_vspill[par] is not None:
                        P.wait("vector", splS, last_vspill[par])
                    for oc in range(2):
                        b = bs[oc]
                        b.await_data("vector")
                        P.op("vector", lambda e, o=v_stage[par][:, 4 * oc:4 * oc + 4, 0:128],
                             s=b.t[:, :].rearrange("p (h e) -> p h e", h=4): e.tensor_copy(o, s),
                             inc=b.rel("vector"))
                    vdata = P.mark("vector")
                    Btr.await_free()
                    for k in range(8):
                        h = hw * 8 + k
                        _tr(P, Btr.t[:, k * 128:(k + 1) * 128], A[:, h, t * 128:(t + 1) * 128],
                            idb[:, :], inc=(Btr.mm if k == 7 else None))
                    Btr.await_data("scalar")
                    P.op("scalar", lambda e, d=skc_st[:, :, :],
                         s=Btr.t[:, :].rearrange("p (k j) -> p k j", j=128): e.copy(d, s),
                         inc=Btr.rel("scalar"))
                    sk_tr = Btr.rel("scalar").v
                    for k in range(8):
                        h = hw * 8 + k
                        bd = B[4 + (k % 2)]
                        bd.await_free()
                        if k == 0:
                            P.wait("tensor", Btr.rel("scalar"), sk_tr)
                            P.waitk("tensor", vdata)
                        _mm(P, bd.t[:, 0:129], skc_st[:, k, :], v_stage[par][:, k, :],
                            start=True, stop=True, inc=bd.mm)
                        bd.await_data("vector")
                        P.op("vector", lambda e, o=S_f32[:, h, :], s=bd.t[:, 0:129]:
                             e.scalar_tensor_tensor(o, s, 1.0, o, ALU.mult, ALU.add),
                             inc=bd.rel("vector"))
                    P.waitk("sync", vdata)
                    last_vspill[par] = P.dma("sync", v_sp[t][:, hw * 8 * 129:(hw * 8 + 8) * 129],
                                             v_stage[par][:, :, :], splS)
                prev_rel = (B[5].mm, B[5].mm.v)

            P.wait("vector", splS, sk_sp_tgt[NCH - 1])
            ph1_mm_tok = proj_featmajor(A, wq_d, prev_rel)
            sq_sp_tgt = [None] * NCH
            sq_wr = P.mark("vector")
            P.waitk("sync", sq_wr)
            for c in range(NCH):
                sq_sp_tgt[c] = P.dma("sync", sq_sp[c], A[:, :, c * 128:(c + 1) * 128], splS)
            P.waitk("vector", sq_wr)
            sqs_tok = None
            for h in range(NH):
                sqs_tok = P.op("vector", lambda e, o=sq_own[:, h:h + 1], s=A[:, h, :]:
                               e.reduce_sum(o, s, axis=AX), inc=True)

            P.waitk("gpsimd", sqs_tok)
            cc1 = P.dma("gpsimd", cc_in[:, 0:NH * 129], S_f32[:, :, :], ldG)
            cc1 = P.dma("gpsimd", cc_in[:, NH * 129:NH * 130], sq_own[:, :], ldG)
            P.wait("gpsimd", ldG, cc1)
            P.op("gpsimd", lambda e: e.collective_compute(
                "AllGather", ALU.bypass, replica_groups=[list(range(NCORES))],
                ins=[cc_in.opt()], outs=[cc_out.opt()]), inc=ccs, by=1)
            cc_done = ccs.v

        P.barrier(bar, scalar_scratch=rowscr[0:1, 255:256])
        P.wait("gpsimd", splS, splS.v)
        P.wait("sync", splS, splS.v)

        with ExitStack() as ph2:
            psb = lambda n, s, d=F32: ph2.enter_context(nc.sbuf_tensor(n, s, d))
            fmem_sb = psb("fmem_sb", [128, NH, 640], BF16)
            msum_f = psb("msum_f", [128, NH, 128])
            egw1_sb = psb("egw1_sb", [128, NFT, 512], BF16)
            egw2_sb = psb("egw2_sb", [128, 4], BF16)
            af_sb = [psb(f"af_sb{i}", [128, NH, 640], BF16) for i in range(2)]
            shard_st = [psb(f"shard{i}", [128, NH * 130]) for i in range(2)]
            sq_rl = [psb(f"sq_rl{i}", [128, NH, 128], BF16) for i in range(2)]
            sk_rl = [psb(f"sk_rl{i}", [128, NH, 128], BF16) for i in range(2)]
            v_rl = [psb(f"v_rl{i}", [128, NH, 129], BF16) for i in range(2)]
            skc2 = psb("skc2", [128, NH, 128], BF16)
            msT = psb("msT", [128, NH, 128], BF16)
            den16 = psb("den16", [128, NH])
            c_loc16 = psb("c_loc16", [128, NH])
            num_sb = psb("num_sb", [128, NH, 128], BF16)
            crf = [psb(f"crf{i}", [128, D], BF16) for i in range(2)]
            crT = psb("crT", [128, NFT, 128], BF16)
            h1_bf = psb("h1_bf", [128, 4, 128], BF16)
            oa = psb("oa", [128, D], BF16)
            oT = psb("oT", [128, NFT, 128], BF16)

            B2 = [Bank(P, ph2, f"p2_{i}", [128, 512], F32) for i in range(5)]
            Btr2 = Bank(P, ph2, "p2tr", [128, 1024], BF16)
            Baf, Bb, Bsc, Bnum, Bdel = B2
            P.waitk("tensor", ph1_mm_tok)

            egw1_tgt = None
            for f in range(NFT):
                egw1_tgt = P.dma("gpsimd", egw1_sb[:, f, :], egw1_d[f * 128:(f + 1) * 128, :], ldG)
            egw2_tgt = P.dma("gpsimd", egw2_sb[:, :], egw2_d, ldG)
            fm_tgt = None
            for m in range(NM):
                fm_tgt = P.dma("gpsimd", fmem_sb[:, :, m * 128:(m + 1) * 128],
                               fmem_d[m].rearrange("h d e -> d h e"), ldG)
            ms_tgt = None
            for m in range(NM):
                if m > 0:
                    P.wait("gpsimd", ldG, ms_tgt)
                kw = {} if m == 0 else {"accum_op": ALU.add}
                ms_tgt = P.dma("gpsimd", msum_f[:, :, :], fmem_d[m].rearrange("h d e -> d h e"), ldG, **kw)
            P.wait("vector", ldG, ms_tgt)
            msum_tok = P.op("vector", lambda e, o=fmem_sb[:, :, 512:640], s=msum_f[:, :, :]:
                            e.tensor_copy(o, s), inc=True)

            P.wait("vector", ldS, ld0_done)
            P.opw("vector", lambda e, o=fnorm_sb[:, :, 4], a=fnorm_sb[:, :, 0], b=fnorm_sb[:, :, 1]:
                  e.tensor_tensor(o, a, b, ALU.add))
            P.opw("vector", lambda e, o=r_all[:, :, 0], a=fnorm_sb[:, :, 2], b=fnorm_sb[:, :, 3]:
                  e.tensor_tensor(o, a, b, ALU.add))
            P.opw("vector", lambda e, o=fnorm_sb[:, :, 4], a=fnorm_sb[:, :, 4], b=r_all[:, :, 0]:
                  e.tensor_tensor(o, a, b, ALU.add))
            nrhs_tok = P.op("vector", lambda e, o=nrhs_bf[:, :, :], s=fnorm_sb[:, :, :]:
                            e.tensor_copy(o, s), inc=True)
            dsum_tok = None
            for m in range(NM):
                dsum_tok = P.op("vector", lambda e, o=sqsum4[:, 0, m:m + 1], s=fnorm_sb[:, :, m]:
                                e.reduce_sum(o, s, axis=AX), inc=True)
            P.waitk("vector", dsum_tok)
            Bsc.await_free()
            P.waitk("tensor", dsum_tok)
            _mm(P, Bsc.t[0:1, 0:4], onescol[:, :], sqsum4[:, 0, :], True, True, inc=Bsc.mm)
            Bsc.await_data("vector")
            P.opw("vector", lambda e, o=rowscr[0:1, 16:20], s=Bsc.t[0:1, 0:4]: e.tensor_copy(o, s))
            P.op("vector", lambda e: e.engine_nop(), inc=Bsc.rel("vector"))
            P.opw("vector", lambda e, o=rowscr[0:1, 20:24], s=rowscr[0:1, 16:20]:
                  e.tensor_single_scalar(o, s, EPS, ALU.is_ge))
            P.opw("vector", lambda e, o=rowscr[0:1, 24:25], s=rowscr[0:1, 16:20]:
                  e.reduce_sum(o, s, axis=AX))
            za_tok = P.opw("vector", lambda e, o=rowscr[0:1, 25:26], s=rowscr[0:1, 24:25]:
                           e.tensor_single_scalar(o, s, EPS, ALU.is_ge))
            P.wait("scalar", ldS, ld0_done)
            gr_tok = P.op("scalar", lambda e, o=rowscr[0:1, 32:48], s=memgate_sb[0:1, :]:
                          e.activation(o, s, AF.Sigmoid), inc=True)
            Bnum.await_free()
            P.waitk("tensor", gr_tok)
            P.waitk("tensor", za_tok)
            P.wait("tensor", ldS, ld0_done)
            _mm(P, Bnum.t[:, 0:16], onesrow[0:1, :], rowscr[0:1, 32:48], True, True)
            _mm(P, Bnum.t[:, 16:17], onesrow[0:1, :], rowscr[0:1, 25:26], True, True)
            _mm(P, Bnum.t[:, 17:18], onesrow[0:1, :], rowscr[0:1, 0:1], True, True, inc=Bnum.mm)
            Bnum.await_data("vector")
            P.op("vector", lambda e, o=gate_cols[:, :], s=Bnum.t[:, 0:16]: e.tensor_copy(o, s))
            P.op("vector", lambda e, o=zalive_col[:, :], s=Bnum.t[:, 16:17]: e.tensor_copy(o, s))
            P.opw("vector", lambda e, o=egb2_sb[:, :], s=Bnum.t[:, 17:18]: e.tensor_copy(o, s))
            P.op("vector", lambda e: e.engine_nop(), inc=Bnum.rel("vector"))
            P.op("vector", lambda e, o=omgate_cols[:, :], s=gate_cols[:, :]:
                 e.tensor_scalar(o, s, -1.0, 1.0, ALU.mult, ALU.add))

            P.wait("gpsimd", ccs, cc_done)
            sh_tgt = [None] * NCORES
            sh_rel = [None] * NCORES
            sh_tgt[0] = P.dma("gpsimd", shard_st[0][:, :], cc_out[0:128, :], ldG)
            sh_tgt[1] = P.dma("gpsimd", shard_st[1][:, :], cc_out[128:256, :], ldG)
            for r in range(NCORES):
                P.wait("vector", ldG, sh_tgt[r])
                ss = shard_st[r % 2]
                P.op("vector", lambda e, o=S_f32[:, :, :],
                     s=ss[:, 0:NH * 129].rearrange("p (h c) -> p h c", h=NH),
                     m=pmask_sb[:, r:r + 1], op1=(ALU.bypass if r == 0 else ALU.add):
                     e.scalar_tensor_tensor(o, s, m, o, ALU.mult, op1))
                if r == 0:
                    sh_rel[r] = P.opw("vector", lambda e, o=sqsumg[:, :], s=ss[:, NH * 129:NH * 130]:
                                      e.tensor_copy(o, s))
                else:
                    sh_rel[r] = P.opw("vector", lambda e, o=sqsumg[:, :], s=ss[:, NH * 129:NH * 130],
                                      b=sqsumg[:, :]: e.tensor_tensor(o, b, s, ALU.add))
                if r + 2 < NCORES:
                    P.waitk("gpsimd", sh_rel[r])
                    sh_tgt[r + 2] = P.dma("gpsimd", shard_st[r % 2][:, :],
                                          cc_out[(r + 2) * 128:(r + 3) * 128, :], ldG)
            sbf_init = P.op("vector", lambda e, o=S_bf[:, :, :], s=S_f32[:, :, :]: e.tensor_copy(o, s),
                            inc=True)

            P.waitk("vector", sh_rel[NCORES - 1])
            for m in range(NM):
                P.op("vector", lambda e, o=sqsum4[:, :, m], s=sqsumg[:, :]: e.tensor_copy(o, s))
            P.opw("vector", lambda e: e.engine_nop())
            P.opw("vector", lambda e, o=prod_sb[:, :, :], a=fnorm_sb[:, :, 0:4], b=sqsum4[:, :, :]:
                  e.tensor_tensor(o, a, b, ALU.mult))
            prod_tok = P.opw("vector", lambda e, o=prod_sb[:, :, :], s=prod_sb[:, :, :]:
                             e.tensor_single_scalar(o, s, REL_SCALE, ALU.mult))
            Bsc.await_free()
            P.waitk("tensor", prod_tok)
            _mm(P, Bsc.t[0:1, 0:64], onescol[:, :], prod_sb[:, :, :], True, True, inc=Bsc.mm)
            Bsc.await_data("vector")
            P.opw("vector", lambda e, o=rowscr[0:1, 64:128], s=Bsc.t[0:1, 0:64]: e.tensor_copy(o, s))
            P.op("vector", lambda e: e.engine_nop(), inc=Bsc.rel("vector"))
            rel = rowscr[0:1, 64:128].rearrange("p (h m) -> p h m", m=4)
            mxe = rowscr[0:1, 160:224].rearrange("p (h m) -> p h m", m=4)
            P.opw("vector", lambda e, o=rowscr[0:1, 128:144], a=rel[:, :, 0], b=rel[:, :, 1]:
                  e.tensor_tensor(o, a, b, ALU.max))
            P.opw("vector", lambda e, o=rowscr[0:1, 144:160], a=rel[:, :, 2], b=rel[:, :, 3]:
                  e.tensor_tensor(o, a, b, ALU.max))
            P.opw("vector", lambda e, o=rowscr[0:1, 128:144], a=rowscr[0:1, 128:144], b=rowscr[0:1, 144:160]:
                  e.tensor_tensor(o, a, b, ALU.max))
            for m in range(NM):
                P.op("vector", lambda e, o=mxe[:, :, m], s=rowscr[0:1, 128:144]: e.tensor_copy(o, s))
            P.opw("vector", lambda e: e.engine_nop())
            sm_tok = P.opw("vector", lambda e, o=rowscr[0:1, 64:128], a=rowscr[0:1, 64:128],
                           b=rowscr[0:1, 160:224]: e.tensor_tensor(o, a, b, ALU.subtract))
            P.waitk("scalar", sm_tok)
            ex_tok = P.op("scalar", lambda e, o=rowscr[0:1, 64:128], s=rowscr[0:1, 64:128]:
                          e.activation(o, s, AF.Exp), inc=True)
            P.waitk("vector", ex_tok)
            P.opw("vector", lambda e, o=rowscr[0:1, 128:144], a=rel[:, :, 0], b=rel[:, :, 1]:
                  e.tensor_tensor(o, a, b, ALU.add))
            P.opw("vector", lambda e, o=rowscr[0:1, 144:160], a=rel[:, :, 2], b=rel[:, :, 3]:
                  e.tensor_tensor(o, a, b, ALU.add))
            P.opw("vector", lambda e, o=rowscr[0:1, 128:144], a=rowscr[0:1, 128:144], b=rowscr[0:1, 144:160]:
                  e.tensor_tensor(o, a, b, ALU.add))
            P.opw("vector", lambda e, o=rowscr[0:1, 144:160], s=rowscr[0:1, 128:144]: e.reciprocal(o, s))
            for m in range(NM):
                P.op("vector", lambda e, o=mxe[:, :, m], s=rowscr[0:1, 144:160]: e.tensor_copy(o, s))
            P.opw("vector", lambda e: e.engine_nop())
            P.opw("vector", lambda e, o=rowscr[0:1, 64:128], a=rowscr[0:1, 64:128], b=rowscr[0:1, 160:224]:
                  e.tensor_tensor(o, a, b, ALU.mult))
            for m in range(NM):
                P.opw("vector", lambda e, o=rel[:, :, m], s=rel[:, :, m], a=rowscr[0:1, 20 + m:21 + m]:
                      e.tensor_scalar(o, s, a, None, ALU.mult))
            wrow_tok = P.mark("vector")
            Bsc.await_free()
            P.waitk("tensor", wrow_tok)
            _mm(P, Bsc.t[:, 0:64], onesrow[0:1, :], rowscr[0:1, 64:128], True, True, inc=Bsc.mm)
            Bsc.await_data("vector")
            P.opw("vector", lambda e, o=coeffm[:, :, :], s=Bsc.t[:, 0:64].rearrange("p (h m) -> p h m", m=4):
                  e.tensor_copy(o, s))
            P.op("vector", lambda e: e.engine_nop(), inc=Bsc.rel("vector"))
            for h in range(NH):
                P.op("vector", lambda e, o=coeffbase[:, h, 0:4], s=coeffm[:, h, :], a=gate_cols[:, h:h + 1]:
                     e.tensor_scalar(o, s, a, None, ALU.mult))
            cbase_tok = P.opw("vector", lambda e, o=coeffbase[:, :, 4], s=gate_cols[:, :], a=zalive_col[:, :]:
                              e.tensor_scalar(o, s, a, None, ALU.mult))

            sqr_tgt = [None] * NCH
            skr_tgt = [None] * NCH
            vr_tgt = [None] * NCH
            rl_rel = [None] * NCH
            pool_rel = [None] * NCH
            dbg_oa_tgt = [None] * NCH
            af_sc_tok = [None] * NCH
            dbg_gate = [None]
            oab_rel = [None] * NCH
            oT_sp_tgt = [None] * NCH
            sbf_tok = [[None] * NH for _ in range(NCH)]

            def plan_reload(c):
                p = c % 2
                if c >= 2:
                    P.waitk("sync", rl_rel[c - 2])
                sqr_tgt[c] = P.dma("sync", sq_rl[p][:, :, :], sq_sp[c], rldS)
                skr_tgt[c] = P.dma("sync", sk_rl[p][:, :, :], sk_sp[c], rldS)
                vr_tgt[c] = P.dma("sync", v_rl[p][:, :, :], v_sp[c], rldS)

            plan_reload(0)
            plan_reload(1)

            def emit_stage1(c):
                p = c % 2
                P.wait("tensor", rldS, sqr_tgt[c])
                if c == 0:
                    P.wait("tensor", ldG, fm_tgt)
                    P.waitk("tensor", msum_tok)
                    P.waitk("tensor", nrhs_tok)
                if c >= 2:
                    P.waitk("scalar", pool_rel[c - 2])
                    P.waitk("vector", pool_rel[c - 2])
                for h in range(NH):
                    Baf.await_free()
                    Bb.await_free()
                    lh = sq_rl[p][:, h, :]
                    _mm(P, Baf.t[:, :], lh, fmem_sb[:, h, 0:512], True, True, inc=Baf.mm)
                    _mm(P, Bb.t[:, 0:128], lh, fmem_sb[:, h, 512:640], True, True)
                    _mm(P, Bb.t[:, 128:133], lh, nrhs_bf[:, h, :], True, True, inc=Bb.mm)
                    Baf.await_data("scalar")
                    P.op("scalar", lambda e, o=af_sb[p][:, h, 0:512], s=Baf.t[:, :]: e.copy(o, s),
                         inc=Baf.rel("scalar"))
                    Bb.await_data("scalar")
                    P.op("scalar", lambda e, o=af_sb[p][:, h, 512:640], s=Bb.t[:, 0:128]: e.copy(o, s),
                         inc=Bb.rel("scalar"))
                    Bb.await_data("vector")
                    tok_n = P.op("vector", lambda e, o=n_all[p][:, h, :], s=Bb.t[:, 128:133]:
                                 e.tensor_copy(o, s), inc=Bb.rel("vector"))
                    if h == NH - 1:
                        P.wait("vector", Bb.rel("vector"), tok_n)
                af_sc_tok[c] = (Bb.rel("scalar"), Bb.rel("scalar").v)

            emit_stage1(0)
            for c in range(NCH):
                p = c % 2
                P.opw("vector", lambda e, o=n_all[p][:, :, :], s=n_all[p][:, :, :]:
                      e.tensor_single_scalar(o, s, EPS, ALU.max))
                P.opw("vector", lambda e, o=r_all[:, :, :], s=n_all[p][:, :, :]: e.reciprocal(o, s))
                if c == 0:
                    P.waitk("vector", cbase_tok)
                crc_tok = P.opw("vector", lambda e, o=crcoefs[:, :], s=r_all[:, :, 4], a=zalive_col[:, :]:
                                e.tensor_scalar(o, s, a, None, ALU.mult))
                P.waitk("vector", af_sc_tok[c])
                cr_tok = None
                for h in range(NH):
                    cr_tok = P.op("vector", lambda e, o=crf[p][:, h * 128:(h + 1) * 128],
                                  s=af_sb[p][:, h, 512:640], a=crcoefs[:, h:h + 1]:
                                  e.scalar_tensor_tensor(o, s, a, o, ALU.mult, ALU.bypass), inc=True)
                crb_tok = cr_tok
                P.wait("tensor", rldS, skr_tgt[c])
                P.wait("tensor", rldS, vr_tgt[c])
                if c > 0:
                    P.waitk("scalar", pool_rel[c - 1])
                for r in range(2):
                    Btr2.await_free()
                    for k in range(8):
                        h = r * 8 + k
                        _tr(P, Btr2.t[:, k * 128:(k + 1) * 128], sk_rl[p][:, h, :], idb[:, :],
                            inc=(Btr2.mm if k == 7 else None))
                    Btr2.await_data("scalar")
                    P.op("scalar", lambda e, d=skc2[:, r * 8:(r + 1) * 8, :],
                         s=Btr2.t[:, :].rearrange("p (k j) -> p k j", j=128): e.copy(d, s),
                         inc=Btr2.rel("scalar"))
                skc2_tgt = Btr2.rel("scalar").v
                P.wait("tensor", Btr2.rel("scalar"), skc2_tgt)
                for h in range(NH):
                    bs_ = Bsc if h % 2 == 0 else Baf
                    bs_.await_free()
                    _mm(P, bs_.t[:, 0:128], sk_rl[p][:, h, :], sq_rl[p][:, h, :], True, True, inc=bs_.mm)
                    bs_.await_data("vector")
                    P.op("vector", lambda e, o=msT[:, h, :], a=bs_.t[:, 0:128], b=mask_sb[:, :]:
                         e.tensor_tensor(o, a, b, ALU.mult), inc=bs_.rel("vector"))
                ms_all = P.mark("vector")
                P.waitk("tensor", ms_all)
                num_tok = None
                for h in range(NH):
                    bn_ = Bnum if h % 2 == 0 else Bb
                    bd_ = Bdel if h % 2 == 0 else Bsc
                    bn_.await_free()
                    if c > 0:
                        P.waitk("tensor", sbf_tok[c - 1][h])
                    elif h == 0:
                        P.waitk("tensor", sbf_init)
                    _mm(P, bn_.t[:, 0:129], msT[:, h, :], v_rl[p][:, h, :], True, False)
                    _mm(P, bn_.t[:, 0:129], sq_rl[p][:, h, :], S_bf[:, h, :], False, True, inc=bn_.mm)
                    bn_.await_data("scalar")
                    P.op("scalar", lambda e, o=den16[:, h:h + 1], s=bn_.t[:, 128:129]:
                         e.copy(o, s))
                    num_tok = P.op("scalar", lambda e, o=num_sb[:, h, :], s=bn_.t[:, 0:128]:
                                   e.copy(o, s), inc=bn_.rel("scalar"))
                    nt_sem = bn_.rel("scalar")
                    bd_.await_free()
                    _mm(P, bd_.t[:, 0:129], skc2[:, h, :], v_rl[p][:, h, :], True, True, inc=bd_.mm)
                    bd_.await_data("vector")
                    su_tok = P.op("vector", lambda e, o=S_f32[:, h, :], s=bd_.t[:, 0:129]:
                                  e.scalar_tensor_tensor(o, s, 1.0, o, ALU.mult, ALU.add),
                                  inc=bd_.rel("vector"))
                    P.wait("scalar", bd_.rel("vector"), su_tok)
                    sbf_tok[c][h] = P.op("scalar", lambda e, o=S_bf[:, h, :], s=S_f32[:, h, :]:
                                         e.copy(o, s), inc=True)
                rl_rel[c] = (Bdel.mm, Bdel.mm.v)
                P.wait("vector", nt_sem, num_tok)
                P.opw("vector", lambda e: e.engine_nop())
                P.opw("vector", lambda e, o=den16[:, :], s=den16[:, :]:
                      e.tensor_single_scalar(o, s, EPS, ALU.max))
                P.opw("vector", lambda e, o=c_loc16[:, :], s=den16[:, :]: e.reciprocal(o, s))
                P.opw("vector", lambda e, o=c_loc16[:, :], a=c_loc16[:, :], b=omgate_cols[:, :]:
                      e.tensor_tensor(o, a, b, ALU.mult))

                if c + 1 < NCH:
                    emit_stage1(c + 1)

                P.waitk("tensor", crb_tok)
                for r in range(2):
                    Btr2.await_free()
                    for k in range(8):
                        f = r * 8 + k
                        _tr(P, Btr2.t[:, k * 128:(k + 1) * 128], crf[p][:, f * 128:(f + 1) * 128],
                            idb[:, :], inc=(Btr2.mm if k == 7 else None))
                    Btr2.await_data("scalar")
                    P.op("scalar", lambda e, d=crT[:, r * 8:(r + 1) * 8, :],
                         s=Btr2.t[:, :].rearrange("p (k j) -> p k j", j=128): e.copy(d, s),
                         inc=Btr2.rel("scalar"))
                crT_tgt = Btr2.rel("scalar").v
                Bsc.await_free()
                P.wait("tensor", Btr2.rel("scalar"), crT_tgt)
                if c == 0:
                    P.wait("tensor", ldG, egw1_tgt)
                    P.wait("tensor", ldG, egw2_tgt)
                for ut in range(4):
                    for f in range(NFT):
                        _mm(P, Bsc.t[:, ut * 128:(ut + 1) * 128], egw1_sb[:, f, ut * 128:(ut + 1) * 128],
                            crT[:, f, :], start=(f == 0), stop=(f == NFT - 1),
                            inc=(Bsc.mm if (ut == 3 and f == NFT - 1) else None))
                Bsc.await_data("scalar")
                for ut in range(4):
                    P.op("scalar", lambda e, o=h1_bf[:, ut, :], s=Bsc.t[:, ut * 128:(ut + 1) * 128],
                         b=egb1_sb[:, ut:ut + 1]: e.activation(o, s, AF.Relu, bias=b),
                         inc=(Bsc.rel("scalar") if ut == 3 else None))
                Bnum.await_free()
                P.wait("tensor", Bsc.rel("scalar"))
                for ut in range(4):
                    _mm(P, Bnum.t[:, 0:1], h1_bf[:, ut, :], egw2_sb[:, ut:ut + 1],
                        start=(ut == 0), stop=(ut == 3), inc=(Bnum.mm if ut == 3 else None))
                Bnum.await_data("scalar")
                gc_tok = P.op("scalar", lambda e, o=g_col[:, :], s=Bnum.t[:, 0:1], b=egb2_sb[:, :]:
                              e.activation(o, s, AF.Sigmoid, bias=b), inc=Bnum.rel("scalar"))
                P.wait("vector", Bnum.rel("scalar"), gc_tok)
                omg_tok = P.opw("vector", lambda e, o=omg_col[:, :], s=g_col[:, :]:
                                e.tensor_scalar(o, s, -1.0, 1.0, ALU.mult, ALU.add))
                P.opw("vector", lambda e, o=coeffm[:, :, :], a=r_all[:, :, 0:4], b=coeffbase[:, :, 0:4]:
                      e.tensor_tensor(o, a, b, ALU.mult))
                P.opw("vector", lambda e, o=coeffm[:, :, :], s=coeffm[:, :, :], a=g_col[:, :]:
                      e.tensor_scalar(o, s, a, None, ALU.mult))
                cm_tok = P.opw("vector", lambda e, o=c2cols[:, :], s=gate_cols[:, :], a=omg_col[:, :]:
                               e.tensor_scalar(o, s, a, None, ALU.mult))

                if c > 0:
                    P.waitk("vector", oab_rel[c - 1])
                    if dbg and dbg_oa_tgt[c - 1]:
                        P.wait("vector", splS, dbg_oa_tgt[c - 1])
                for h in range(NH):
                    P.op("vector", lambda e, o=oa[:, h * 128:(h + 1) * 128], s=num_sb[:, h, :],
                         a=c_loc16[:, h:h + 1]: e.scalar_tensor_tensor(o, s, a, o, ALU.mult, ALU.bypass))
                for m in range(NM):
                    for h in range(NH):
                        P.op("vector", lambda e, o=oa[:, h * 128:(h + 1) * 128],
                             s=af_sb[p][:, h, m * 128:(m + 1) * 128], a=coeffm[:, h, m:m + 1]:
                             e.scalar_tensor_tensor(o, s, a, o, ALU.mult, ALU.add))
                for h in range(NH):
                    P.op("vector", lambda e, o=oa[:, h * 128:(h + 1) * 128],
                         s=crf[p][:, h * 128:(h + 1) * 128],
                         a=c2cols[:, h:h + 1]: e.scalar_tensor_tensor(o, s, a, o, ALU.mult, ALU.add),
                         inc=(True if h == NH - 1 else None))
                pool_rel[c] = (P.ax["vector"], P.ax["vector"].v)

                P.waitk("tensor", pool_rel[c])
                if c >= 1:
                    P.wait("scalar", splS, oT_sp_tgt[c - 1])
                for r in range(2):
                    Btr2.await_free()
                    for k in range(8):
                        f = r * 8 + k
                        _tr(P, Btr2.t[:, k * 128:(k + 1) * 128], oa[:, f * 128:(f + 1) * 128], idb[:, :],
                            inc=(Btr2.mm if k == 7 else None))
                    Btr2.await_data("scalar")
                    P.op("scalar", lambda e, d=oT[:, r * 8:(r + 1) * 8, :],
                         s=Btr2.t[:, :].rearrange("p (k j) -> p k j", j=128): e.copy(d, s),
                         inc=Btr2.rel("scalar"))
                oT_done = Btr2.rel("scalar").v
                oab_rel[c] = (Btr2.mm, Btr2.mm.v)
                P.wait("sync", Btr2.rel("scalar"), oT_done)
                oT_sp_tgt[c] = P.dma("sync", oT_sp[c], oT[:, :, :], splS)
                if dbg:
                    P.waitk("sync", pool_rel[c])
                    dbg_oa_tgt[c] = P.dma("sync", attn_dbg[c], oa[:, :], splS)
                if c + 2 < NCH:
                    plan_reload(c + 2)

        P.barrier(bar, scalar_scratch=rowscr[0:1, 255:256])
        P.wait("gpsimd", splS, splS.v)
        P.wait("sync", splS, splS.v)

        with ExitStack() as ph2b:
            psb = lambda n, s, d=F32: ph2b.enter_context(nc.sbuf_tensor(n, s, d))
            wo_sb = psb("wo_sb", [128, NFT, D], BF16)
            oT_rl = [psb(f"oT_rl{i}", [128, NFT, 128], BF16) for i in range(2)]
            x_res = [psb(f"x_res{i}", [128, D]) for i in range(2)]
            ln_in = psb("ln_in", [128, D])
            ln_sc = psb("ln_sc", [128, D])
            x1f = psb("x1f", [128, D])
            Bw = [Bank(P, ph2b, f"pw{i}", [128, 512], F32) for i in range(8)]
            P.wait("tensor", Btr2.mm)
            for _b in B2 + [Btr2]:
                for _s in _b.ev.values():
                    P.wait("tensor", _s)

            wo_tgt = None
            for f in range(NFT):
                wo_tgt = P.dma("gpsimd", wo_sb[:, f, :], wo_d[f * 128:(f + 1) * 128, :], ldG)

            oTr_tgt = [None] * NCH
            xr_tgt = [None] * NCH
            oTr_rel = [None] * NCH
            xres_rel = [None] * NCH
            x1fS_tgt = [None] * NCH

            def plan_reload2(c):
                p = c % 2
                if c >= 2:
                    P.waitk("sync", oTr_rel[c - 2])
                oTr_tgt[c] = P.dma("sync", oT_rl[p][:, :, :], oT_sp[c], rldS)
                if c >= 2:
                    P.wait("sync", xres_rel[c - 2][0], xres_rel[c - 2][1])
                xr_tgt[c] = P.dma("sync", x_res[p][:, :], x_d[c * 128:(c + 1) * 128, :], rldS)

            plan_reload2(0)
            plan_reload2(1)
            for c in range(NCH):
                p = c % 2
                bwg = Bw[0:4] if c % 2 == 0 else Bw[4:8]
                for b in bwg:
                    b.await_free()
                P.wait("tensor", rldS, oTr_tgt[c])
                if c == 0:
                    P.wait("tensor", ldG, wo_tgt)
                for f in range(NFT):
                    for oc in range(4):
                        _mm(P, bwg[oc].t[:, :], oT_rl[p][:, f, :], wo_sb[:, f, oc * 512:(oc + 1) * 512],
                            start=(f == 0), stop=(f == NFT - 1), inc=(bwg[oc].mm if f == NFT - 1 else None))
                oTr_rel[c] = (bwg[3].mm, bwg[3].mm.v)
                P.wait("vector", rldS, xr_tgt[c])
                for oc in range(4):
                    bwg[oc].await_data("vector")
                    P.op("vector", lambda e, o=ln_in[:, oc * 512:(oc + 1) * 512], a=bwg[oc].t[:, :],
                         b=x_res[p][:, oc * 512:(oc + 1) * 512]: e.tensor_tensor(o, a, b, ALU.add),
                         inc=bwg[oc].rel("vector"))
                xres_rel[c] = (bwg[3].rel("vector"), bwg[3].rel("vector").v)
                P.opw("vector", lambda e: e.engine_nop())
                P.opw("vector", lambda e, o=ln_m[:, :], s=ln_in[:, :]: e.reduce_sum(o, s, axis=AX))
                P.opw("vector", lambda e, o=ln_m[:, :], s=ln_m[:, :]:
                      e.tensor_single_scalar(o, s, 1.0 / D, ALU.mult))
                xm_tok = P.opw("vector", lambda e, o=ln_sc[:, :], s=ln_in[:, :], a=ln_m[:, :]:
                               e.tensor_scalar(o, s, a, None, ALU.subtract))
                P.waitk("scalar", xm_tok)
                sq_tok = P.op("scalar", lambda e, o=ln_in[:, :], s=ln_sc[:, :], acc=ln_v[:, :]:
                              e.activation(o, s, AF.Square, accum_out=acc), inc=True)
                P.waitk("vector", sq_tok)
                P.opw("vector", lambda e, o=ln_v[:, :], s=ln_v[:, :]:
                      e.tensor_single_scalar(o, s, 1.0 / D, ALU.mult))
                P.opw("vector", lambda e, o=ln_v[:, :], s=ln_v[:, :]:
                      e.tensor_single_scalar(o, s, LN_EPS, ALU.add))
                rv_tok = P.opw("vector", lambda e, o=ln_r[:, :], s=ln_v[:, :]: e.reciprocal(o, s))
                P.waitk("scalar", rv_tok)
                rs_tok = P.op("scalar", lambda e, o=ln_r[:, :], s=ln_r[:, :]: e.activation(o, s, AF.Sqrt),
                              inc=True)
                P.waitk("vector", rs_tok)
                if c >= 1:
                    P.wait("vector", splS, x1fS_tgt[c - 1])
                x1_tok = P.op("vector", lambda e, o=x1f[:, :], s=ln_sc[:, :], a=ln_r[:, :]:
                              e.tensor_scalar(o, s, a, None, ALU.mult), inc=True)
                P.waitk("sync", x1_tok)
                x1fS_tgt[c] = P.dma("sync", x1f_sp[c], x1f[:, :], splS)
                if c + 2 < NCH:
                    plan_reload2(c + 2)

        P.barrier(bar, scalar_scratch=rowscr[0:1, 255:256])
        P.wait("gpsimd", splS, splS.v)
        P.wait("gpsimd", splG, splG.v)
        P.wait("sync", splS, splS.v)
        P.wait("sync", splG, splG.v)

        with ExitStack() as ph3:
            psb = lambda n, s, d=F32: ph3.enter_context(nc.sbuf_tensor(n, s, d))
            x1T = psb("x1T", [128, NFT, HALF], BF16)
            acc = psb("acc", [128, 8, D])
            cb = psb("cb", [128, 9, HALF], BF16)
            gs = psb("gs", [128, 4, HALF], BF16)
            u_tmp = psb("u_tmp", [128, 2, HALF], BF16)
            wbuf = [psb(f"wbuf{i}", [128, 8192], BF16) for i in range(2)]
            gsg2 = psb("gsg2", [128, 9])
            comb = psb("comb", [128, 9])
            comb_rT = psb("comb_rT", [9, 8, 128])
            l2_in = psb("l2sc", [128, D])
            sel_sb = psb("sel_sb", [9, 9, 128])
            gsg_sb = psb("gsg_sb", [128, NFT, 9], BF16)
            gsg2b = psb("gsg2b", [128, 9])
            sel_tgt = P.dma("sync", sel_sb[:, :, :], sel_d, rldS)
            gsg_tgt = None
            for f in range(NFT):
                gsg_tgt = P.dma("gpsimd", gsg_sb[:, f, :], gsg_d[f * 128:(f + 1) * 128, :], ldG)

            with ExitStack() as ph3p:
                Bp = [Bank(P, ph3p, f"p3_{i}", [128, 512], F32) for i in range(8)]
                P.wait("tensor", Bw[7].mm)
                for _b in Bw:
                    for _s in _b.ev.values():
                        P.wait("tensor", _s)


                xb_rel = [None] * 8
                acc_tgt = [None] * 8
                dbg_cmb_tgt = [None] * 16

                def emit_half_prep(half):
                    if half == 1:
                        P.wait("sync", stS, stS.v)
                    for tt in range(8):
                        acc_tgt[tt] = P.dma("sync", acc[:, tt, :], x1f_sp[half * 8 + tt], rldS)
                    for tt in range(8):
                        P.wait("tensor", rldS, acc_tgt[tt])
                        for r in range(4):
                            Bt = Bp[2 + (r % 2)]
                            Bt.await_free()
                            for k in range(4):
                                f = r * 4 + k
                                _tr(P, Bt.t[:, k * 128:(k + 1) * 128],
                                    acc[:, tt, f * 128:(f + 1) * 128], idf[:, :],
                                    inc=(Bt.mm if k == 3 else None))
                            Bt.await_data("scalar")
                            P.op("scalar", lambda e, d=x1T[:, r * 4:(r + 1) * 4, tt * 128:(tt + 1) * 128],
                                 s=Bt.t[:, :].rearrange("p (k j) -> p k j", j=128): e.copy(d, s),
                                 inc=Bt.rel("scalar"))
                    x1T_tgt_a = Bp[2].rel("scalar").v
                    x1T_tgt_b = Bp[3].rel("scalar").v
                    P.wait("tensor", Bp[2].rel("scalar"), x1T_tgt_a)
                    P.wait("tensor", Bp[3].rel("scalar"), x1T_tgt_b)
                    if dbg:
                        P.wait("sync", Bp[2].rel("scalar"), x1T_tgt_a)
                        P.wait("sync", Bp[3].rel("scalar"), x1T_tgt_b)
                        x1T_dump = P.dma("sync", dbg_x1T[half], x1T[:, :, :], rldS)
                        P.wait("tensor", rldS, x1T_dump)

                    for tt in range(8):
                        Bg = Bp[tt % 2]
                        Bg.await_free()
                        if tt == 0:
                            P.wait("tensor", ldG, gsg_tgt)
                        for f in range(NFT):
                            _mm(P, Bg.t[:, 0:9], x1T[:, f, tt * 128:(tt + 1) * 128], gsg_sb[:, f, :],
                                start=(f == 0), stop=(f == NFT - 1), inc=(Bg.mm if f == NFT - 1 else None))
                        Bg.await_data("vector")
                        lg_tok = P.opw("vector", lambda e, o=gsg2[:, :], s=Bg.t[:, 0:9]: e.tensor_copy(o, s))
                        if dbg:
                            Bg.await_data("scalar")
                            lg2_tok = P.op("scalar", lambda e, o=gsg2b[:, :], s=Bg.t[:, 0:9]:
                                           e.copy(o, s), inc=True)
                            P.waitk("sync", lg2_tok)
                            lgt2_t = P.dma("sync", dbg_lgt2[half * 8 + tt], gsg2b[:, :], rldS)
                            P.wait("vector", rldS, lgt2_t)
                            P.wait("scalar", rldS, lgt2_t)
                        P.op("vector", lambda e: e.engine_nop(), inc=Bg.rel("vector"))
                        if dbg:
                            P.waitk("sync", lg_tok)
                            lgt_t = P.dma("sync", dbg_lgt[half * 8 + tt], gsg2[:, :], rldS)
                            P.wait("vector", rldS, lgt_t)
                        P.opw("vector", lambda e, o=ln_m[:, :], s=gsg2[:, 0:8]: e.reduce_max(o, s, axis=AX))
                        sm1 = P.opw("vector", lambda e, o=comb[:, 0:8], s=gsg2[:, 0:8], a=ln_m[:, :]:
                                    e.tensor_scalar(o, s, a, None, ALU.subtract))
                        P.waitk("scalar", sm1)
                        sm2 = P.op("scalar", lambda e, o=comb[:, 0:8], s=comb[:, 0:8]:
                                   e.activation(o, s, AF.Exp), inc=True)
                        sg1 = P.op("scalar", lambda e, o=comb[:, 8:9], s=gsg2[:, 8:9]:
                                   e.activation(o, s, AF.Sigmoid), inc=True)
                        P.waitk("vector", sg1)
                        P.opw("vector", lambda e, o=ln_m[:, :], s=comb[:, 0:8]: e.reduce_sum(o, s, axis=AX))
                        P.opw("vector", lambda e, o=ln_v[:, :], s=ln_m[:, :]: e.reciprocal(o, s))
                        P.opw("vector", lambda e, o=comb[:, 0:8], s=comb[:, 0:8], a=ln_v[:, :]:
                              e.tensor_scalar(o, s, a, None, ALU.mult))
                        P.opw("vector", lambda e, o=ln_m[:, :], s=comb[:, 0:8]: e.reduce_max(o, s, axis=AX))
                        P.opw("vector", lambda e, o=gsg2[:, 0:8], s=comb[:, 0:8], a=ln_m[:, :]:
                              e.tensor_scalar(o, s, a, None, ALU.subtract))
                        P.opw("vector", lambda e, o=gsg2[:, 0:8], s=gsg2[:, 0:8]:
                              e.tensor_single_scalar(o, s, 0.0, ALU.is_ge))
                        P.opw("vector", lambda e, o=gsg2[:, 0:8], s=gsg2[:, 0:8], b=comb[:, 0:8]:
                              e.scalar_tensor_tensor(o, s, -1e30, b, ALU.mult, ALU.add))
                        P.opw("vector", lambda e, o=ln_v[:, :], s=gsg2[:, 0:8]: e.reduce_max(o, s, axis=AX))
                        P.opw("vector", lambda e, o=gsg2[:, 0:8], s=comb[:, 0:8], a=ln_v[:, :]:
                              e.tensor_scalar(o, s, a, None, ALU.subtract))
                        P.opw("vector", lambda e, o=gsg2[:, 0:8], s=gsg2[:, 0:8]:
                              e.tensor_single_scalar(o, s, 0.0, ALU.is_ge))
                        P.opw("vector", lambda e, o=ln_m[:, :], a=ln_m[:, :], b=ln_v[:, :]:
                              e.tensor_tensor(o, a, b, ALU.add))
                        P.opw("vector", lambda e, o=ln_v[:, :], s=ln_m[:, :]: e.reciprocal(o, s))
                        P.opw("vector", lambda e, o=comb[:, 0:8], a=comb[:, 0:8], b=gsg2[:, 0:8]:
                              e.tensor_tensor(o, a, b, ALU.mult))
                        cmb = P.opw("vector", lambda e, o=comb[:, 0:8], s=comb[:, 0:8], a=ln_v[:, :]:
                                    e.tensor_scalar(o, s, a, None, ALU.mult))
                        if dbg:
                            P.waitk("sync", cmb)
                            dbg_cmb_tgt[half * 8 + tt] = P.dma("sync", dbg_cmb[half * 8 + tt],
                                                               comb[:, :], rldS)
                        Bt = Bp[2]
                        Bt.await_free()
                        P.waitk("tensor", cmb)
                        if dbg:
                            P.wait("tensor", rldS, dbg_cmb_tgt[half * 8 + tt])
                        _tr(P, Bt.t[0:9, 0:128], comb[:, 0:9], idf[:, :], inc=Bt.mm)
                        Bt.await_data("vector")
                        P.op("vector", lambda e, o=comb_rT[:, tt, :], s=Bt.t[0:9, 0:128]:
                             e.tensor_copy(o, s), inc=Bt.rel("vector"))
                    if dbg:
                        gl_done = P.mark("vector")
                        P.waitk("sync", gl_done)
                        x1T_d2 = P.dma("sync", dbg_x1T2[half], x1T[:, :, :], rldS)
                        P.wait("tensor", rldS, x1T_d2)
                    crT_done = Bp[2].rel("vector").v
                    for e9 in range(9):
                        for tc in range(2):
                            Bt = Bp[3]
                            Bt.await_free()
                            if e9 == 0 and tc == 0:
                                P.wait("tensor", Bp[2].rel("vector"), crT_done)
                                P.wait("tensor", rldS, sel_tgt)
                            _mm(P, Bt.t[:, :], sel_sb[:, e9, :], comb_rT[:, tc * 4:(tc + 1) * 4, :],
                                True, True, inc=Bt.mm)
                            Bt.await_data("vector")
                            P.op("vector", lambda e, o=cb[:, e9, tc * 512:(tc + 1) * 512], s=Bt.t[:, :]:
                                 e.tensor_copy(o, s), inc=Bt.rel("vector"))
                    P.opw("vector", lambda e: e.engine_nop())
                    if dbg:
                        Bx = Bp[1]
                        Bx.await_free()
                        for f in range(NFT):
                            _mm(P, Bx.t[:, 0:9], x1T[:, f, 0:128], gsg_sb[:, f, :],
                                start=(f == 0), stop=(f == NFT - 1), inc=(Bx.mm if f == NFT - 1 else None))
                        Bx.await_data("vector")
                        l3 = P.opw("vector", lambda e, o=gsg2[:, :], s=Bx.t[:, 0:9]: e.tensor_copy(o, s))
                        P.op("vector", lambda e: e.engine_nop(), inc=Bx.rel("vector"))
                        P.waitk("sync", l3)
                        l3t = P.dma("sync", dbg_lgt3[half], gsg2[:, :], rldS)
                        P.wait("vector", rldS, l3t)

                def emit_half_experts(half):
                    subs = []
                    for e9 in range(E_MOE):
                        subs += [(e9, 0, 512), (e9, 512, 512), (e9, 1024, 384)]
                    subs += [(8, g * 512, 512) for g in range(11)]
                    wload_tgt = {}

                    def load_gu(which, sub_idx, slot):
                        e9, off, sz = subs[sub_idx]
                        if e9 < E_MOE:
                            src = (wg_d if which == "g" else wu_d)
                            ap = src[e9, :, off:off + sz]
                        else:
                            src = (swg_d if which == "g" else swu_d)
                            ap = src[:, off:off + sz]
                        ap = ap.rearrange("(f p) i -> p f i", p=128)
                        dst = wbuf[slot][:, 0:NFT * 512].rearrange("p (f i) -> p f i", i=512)[:, :, 0:sz]
                        wload_tgt[(which, sub_idx)] = P.dma("gpsimd", dst, ap, ldG)

                    def load_d(sub_idx, slot):
                        e9, off, sz = subs[sub_idx]
                        nit = sz // 128
                        if e9 < E_MOE:
                            ap = wd_d[e9, off:off + sz, :]
                        else:
                            ap = swd_d[off:off + sz, :]
                        ap = ap.rearrange("(it p) o -> p it o", p=128)
                        dst = wbuf[slot][:, 0:nit * D].rearrange("p (it o) -> p it o", o=D)
                        wload_tgt[("d", sub_idx)] = P.dma("gpsimd", dst, ap, ldG)

                    load_gu("g", 0, 0)
                    load_gu("u", 0, 1)
                    hm_prev = [None, None]

                    for si, (e9, off, sz) in enumerate(subs):
                        nit = sz // 128
                        sg_slot = (3 * si) % 2
                        su_slot = (3 * si + 1) % 2
                        sd_slot = (3 * si + 2) % 2

                        P.wait("tensor", ldG, wload_tgt[("g", si)])
                        for it in range(nit):
                            bp = [Bp[0], Bp[1]] if it % 2 == 0 else [Bp[2], Bp[3]]
                            for b in bp:
                                b.await_free()
                            for f in range(NFT):
                                lh = wbuf[sg_slot][:, f * 512 + it * 128: f * 512 + (it + 1) * 128]
                                _mm(P, bp[0].t[:, :], lh, x1T[:, f, 0:512], start=(f == 0),
                                    stop=(f == NFT - 1), inc=(bp[0].mm if f == NFT - 1 else None))
                                _mm(P, bp[1].t[:, :], lh, x1T[:, f, 512:1024], start=(f == 0),
                                    stop=(f == NFT - 1), inc=(bp[1].mm if f == NFT - 1 else None))
                            for tc in range(2):
                                bp[tc].await_data("scalar")
                                P.op("scalar", lambda e, o=gs[:, it, tc * 512:(tc + 1) * 512],
                                     s=bp[tc].t[:, :]: e.activation(o, s, AF.Silu), inc=bp[tc].rel("scalar"))
                        lb = Bp[1] if nit % 2 == 1 else Bp[3]
                        g_done = (lb.mm, lb.mm.v)
                        P.waitk("gpsimd", g_done)
                        load_d(si, sg_slot)

                        P.wait("tensor", ldG, wload_tgt[("u", si)])
                        for it in range(nit):
                            bp = [Bp[0], Bp[1]] if it % 2 == 0 else [Bp[2], Bp[3]]
                            for b in bp:
                                b.await_free()
                            for f in range(NFT):
                                lh = wbuf[su_slot][:, f * 512 + it * 128: f * 512 + (it + 1) * 128]
                                _mm(P, bp[0].t[:, :], lh, x1T[:, f, 0:512], start=(f == 0),
                                    stop=(f == NFT - 1), inc=(bp[0].mm if f == NFT - 1 else None))
                                _mm(P, bp[1].t[:, :], lh, x1T[:, f, 512:1024], start=(f == 0),
                                    stop=(f == NFT - 1), inc=(bp[1].mm if f == NFT - 1 else None))
                            for tc in range(2):
                                bp[tc].await_data("vector")
                                if hm_prev[it % 2] is not None:
                                    P.waitk("vector", hm_prev[it % 2])
                                ut = P.op("vector", lambda e, o=u_tmp[:, it % 2, tc * 512:(tc + 1) * 512],
                                          a=bp[tc].t[:, :], b=cb[:, e9, tc * 512:(tc + 1) * 512]:
                                          e.tensor_tensor(o, a, b, ALU.mult), inc=bp[tc].rel("vector"))
                                P.wait("gpsimd", bp[tc].rel("vector"), ut)
                                hm_prev[it % 2] = P.op("gpsimd", lambda e, o=gs[:, it, tc * 512:(tc + 1) * 512],
                                     a=gs[:, it, tc * 512:(tc + 1) * 512],
                                     b=u_tmp[:, it % 2, tc * 512:(tc + 1) * 512]:
                                     e.tensor_tensor(o, a, b, ALU.mult), inc=True)
                        lb = Bp[1] if nit % 2 == 1 else Bp[3]
                        u_done = (lb.mm, lb.mm.v)
                        P.waitk("gpsimd", u_done)
                        if si + 1 < len(subs):
                            load_gu("g", si + 1, su_slot)
                        h_done = P.mark("gpsimd")

                        P.wait("tensor", ldG, wload_tgt[("d", si)])
                        P.waitk("tensor", h_done)
                        for tt in range(8):
                            bq = [Bp[4], Bp[5], Bp[6], Bp[7]] if tt % 2 == 0 else [Bp[0], Bp[1], Bp[2], Bp[3]]
                            for b in bq:
                                b.await_free()
                            for it in range(nit):
                                lh = gs[:, it, tt * 128:(tt + 1) * 128]
                                for oc in range(4):
                                    _mm(P, bq[oc].t[:, :], lh,
                                        wbuf[sd_slot][:, it * D + oc * 512: it * D + (oc + 1) * 512],
                                        start=(it == 0), stop=(it == nit - 1),
                                        inc=(bq[oc].mm if it == nit - 1 else None))
                            if si == 0:
                                P.wait("vector", rldS, acc_tgt[tt])
                            for oc in range(4):
                                bq[oc].await_data("vector")
                                P.op("vector", lambda e, o=acc[:, tt, oc * 512:(oc + 1) * 512],
                                     a=acc[:, tt, oc * 512:(oc + 1) * 512], b=bq[oc].t[:, :]:
                                     e.tensor_tensor(o, a, b, ALU.add), inc=bq[oc].rel("vector"))
                        d_done = (bq[3].mm, bq[3].mm.v)
                        P.waitk("gpsimd", d_done)
                        if si + 1 < len(subs):
                            load_gu("u", si + 1, sd_slot)

                def emit_half_ln(half):
                    if dbg and half == 0:
                        acc_done = P.mark("vector")
                        P.waitk("sync", acc_done)
                        tgt = P.dma("sync", dbg_acc, acc[:, 0, :], rldS)
                        P.wait("vector", rldS, tgt)
                    for tt in range(8):
                        P.opw("vector", lambda e: e.engine_nop())
                        P.opw("vector", lambda e, o=ln_m[:, :], s=acc[:, tt, :]: e.reduce_sum(o, s, axis=AX))
                        P.opw("vector", lambda e, o=ln_m[:, :], s=ln_m[:, :]:
                              e.tensor_single_scalar(o, s, 1.0 / D, ALU.mult))
                        xm = P.opw("vector", lambda e, o=l2_in[:, :], s=acc[:, tt, :], a=ln_m[:, :]:
                                   e.tensor_scalar(o, s, a, None, ALU.subtract))
                        P.waitk("scalar", xm)
                        sq2 = P.op("scalar", lambda e, o=acc[:, tt, :], s=l2_in[:, :], a2=ln_v[:, :]:
                                   e.activation(o, s, AF.Square, accum_out=a2), inc=True)
                        P.waitk("vector", sq2)
                        P.opw("vector", lambda e, o=ln_v[:, :], s=ln_v[:, :]:
                              e.tensor_single_scalar(o, s, 1.0 / D, ALU.mult))
                        P.opw("vector", lambda e, o=ln_v[:, :], s=ln_v[:, :]:
                              e.tensor_single_scalar(o, s, LN_EPS, ALU.add))
                        rv = P.opw("vector", lambda e, o=ln_r[:, :], s=ln_v[:, :]: e.reciprocal(o, s))
                        P.waitk("scalar", rv)
                        rs = P.op("scalar", lambda e, o=ln_r[:, :], s=ln_r[:, :]:
                                  e.activation(o, s, AF.Sqrt), inc=True)
                        P.waitk("vector", rs)
                        ob = P.op("vector", lambda e, o=acc[:, tt, :], s=l2_in[:, :], a=ln_r[:, :]:
                                  e.tensor_scalar(o, s, a, None, ALU.mult), inc=True)
                        P.waitk("sync", ob)
                        P.dma("sync", out_d[(half * 8 + tt) * 128:(half * 8 + tt + 1) * 128, :],
                              acc[:, tt, :], stS)

                for half in range(2):
                    emit_half_prep(half)
                    emit_half_experts(half)
                    emit_half_ln(half)
                    if half == 0:
                        P.barrier(bar, scalar_scratch=rowscr[0:1, 255:256])

        P.wait("sync", stS, stS.v)

        with nc.Block() as block:
            @block.sync
            def _(e):
                for fn in P.q["sync"]:
                    fn(e)

            @block.tensor
            def _(e):
                for fn in P.q["tensor"]:
                    fn(e)

            @block.vector
            def _(e):
                for fn in P.q["vector"]:
                    fn(e)

            @block.scalar
            def _(e):
                for fn in P.q["scalar"]:
                    fn(e)

            @block.gpsimd
            def _(e):
                for fn in P.q["gpsimd"]:
                    fn(e)

    return nc



def make_in_maps(inputs):
    f32 = np.float32
    asf = lambda a: np.ascontiguousarray(np.asarray(a), dtype=f32)
    x_full = asf(inputs["L0_attn_hidden_states"]).reshape(S_TOT, D)
    wq_t = np.ascontiguousarray(asf(inputs["L0_attn_w_q"]).T)
    wk_t = np.ascontiguousarray(asf(inputs["L0_attn_w_k"]).T)
    wv_t = np.ascontiguousarray(asf(inputs["L0_attn_w_v"]).T)
    wo_t = np.ascontiguousarray(asf(inputs["L0_attn_w_o"]).T)
    egw1_t = np.ascontiguousarray(asf(inputs["L0_attn_eg_w1"]).T)
    egb1_c = np.ascontiguousarray(asf(inputs["L0_attn_eg_b1"]).reshape(4, 128).T)
    egw2_c = np.ascontiguousarray(asf(inputs["L0_attn_eg_w2"]).reshape(4, 128).T)
    egb2_c = asf(inputs["L0_attn_eg_b2"]).reshape(1, 1)
    memgate = asf(inputs["L0_attn_memory_gate"]).reshape(1, NH)
    fmem = asf(inputs["L0_attn_fine_memories"])
    fn = asf(inputs["L0_attn_fine_memory_norms"])
    fnorm_c = np.zeros((128, NH, 5), f32)
    for m in range(NM):
        fnorm_c[:, :, m] = fn[m].T
    fnorm_c = np.ascontiguousarray(fnorm_c.reshape(128, NH * 5))
    gsg_t = np.ascontiguousarray(
        np.concatenate([asf(inputs["L0_ffn_gate_w"]).T, asf(inputs["L0_ffn_shared_gate_w"])], axis=1))
    wg = asf(inputs["L0_ffn_w_gate"])
    wu = asf(inputs["L0_ffn_w_up"])
    wd = asf(inputs["L0_ffn_w_down"])
    swg = asf(inputs["L0_ffn_sw_gate"])
    swu = asf(inputs["L0_ffn_sw_up"])
    swd = asf(inputs["L0_ffn_sw_down"])

    ident_bf = np.eye(128, dtype=BF16NP)
    ident_f32 = np.eye(128, dtype=f32)
    maskT = np.triu(np.ones((128, 128), f32))
    ones_row = np.ones((1, 128), f32)
    ones_col = np.ones((128, 1), f32)
    sel9 = np.zeros((9, 9 * 128), f32)
    for e in range(9):
        sel9[e, e * 128:(e + 1) * 128] = 1.0

    in_maps = []
    for cid in range(NCORES):
        pm = np.zeros((128, NCORES), f32)
        pm[:, :cid] = 1.0
        in_maps.append({
            "x": np.ascontiguousarray(x_full[cid * TPC:(cid + 1) * TPC, :]),
            "wq_t": wq_t, "wk_t": wk_t, "wv_t": wv_t, "wo_t": wo_t,
            "egw1_t": egw1_t, "egb1_c": egb1_c, "egw2_c": egw2_c, "egb2_c": egb2_c,
            "memgate": memgate, "fmem": fmem, "fnorm_c": fnorm_c, "gsg_t": gsg_t,
            "wg": wg, "wu": wu, "wd": wd, "swg": swg, "swu": swu, "swd": swd,
            "ident_bf": ident_bf, "ident_f32": ident_f32, "maskT": maskT,
            "ones_row": ones_row, "ones_col": ones_col, "sel9": sel9, "pmask": pm,
        })
    return in_maps


def kernel(**inputs):
    nc = build(dbg=False)
    in_maps = make_in_maps(inputs)
    res = run_bass_kernel_spmd(nc, in_maps, core_ids=list(range(NCORES)))
    out = np.concatenate([res.results[c]["out"] for c in range(NCORES)], axis=0)
    return out.reshape(1, S_TOT, D).astype(np.float32)
